# version 04
# baseline (speedup 1.0000x reference)
import math
from contextlib import ExitStack

import numpy as np
import ml_dtypes

import concourse.bass as bass
import concourse.mybir as mybir
from concourse.bass_utils import run_bass_kernel_spmd

F32 = mybir.dt.float32
BF16 = mybir.dt.bfloat16
AF = mybir.ActivationFunctionType
ALU = mybir.AluOpType
AX = mybir.AxisListType

NCORES = 8
SEQ = 8192
DM = 2048
DCH = DM // 128
NCH = 16
CHT = SEQ // NCH
TPC = 2 * CHT
CORE_CHUNKS = [(i, NCH - 1 - i) for i in range(NCORES)]

BF = ml_dtypes.bfloat16



def _hadamard(n):
    h = np.array([[1.0]], dtype=np.float64)
    while h.shape[0] < n:
        h = np.block([[h, h], [h, -h]])
    return h


def _rope_tables(n, d, theta):
    inv = 1.0 / theta ** (np.arange(0, d, 2, dtype=np.float64) / d)
    f = np.arange(n, dtype=np.float64)[:, None] * inv[None, :]
    return np.cos(f), np.sin(f)


def core_tokens(core):
    a, b = CORE_CHUNKS[core]
    return np.r_[a * CHT:(a + 1) * CHT, b * CHT:(b + 1) * CHT]


def shard_x(x):
    return [np.ascontiguousarray(x[core_tokens(i)].T) for i in range(NCORES)]


def unshard_x(shards):
    x = np.empty((SEQ, DM), np.float32)
    for i in range(NCORES):
        x[core_tokens(i)] = np.asarray(shards[i]).T
    return x




class SemC:
    __slots__ = ("name", "n", "h", "base")

    def __init__(self, name):
        self.name = name
        self.n = 0
        self.base = 0
        self.h = None

    @property
    def rel(self):
        return self.n - self.base


class _Eng:
    def __init__(self, prog, name):
        self._p = prog
        self._n = name

    def op(self, meth, *args, inc=None, **kw):
        if inc:
            for s, v in inc:
                s.n += v

        def closure(e, _m=meth, _a=args, _k=kw, _i=inc):
            r = getattr(e, _m)(*_a, **_k)
            if _i:
                for s, v in _i:
                    r.then_inc(s.h, v)
            return r

        self._p.ops.append((self._n, closure))

    def wait(self, sem, val=None):
        v = sem.n if val is None else (sem.base + val)
        self._p.ops.append((self._n, lambda e, _s=sem, _v=v: e.wait_ge(_s.h, _v)))

    def __getattr__(self, meth):
        def call(*args, inc=None, **kw):
            self.op(meth, *args, inc=inc, **kw)
        return call


class Prog:
    def __init__(self):
        self.ops = []
        self.sems = {}
        self.prefix = ""
        self.sync = _Eng(self, "sync")
        self.scalar = _Eng(self, "scalar")
        self.vector = _Eng(self, "vector")
        self.tensor = _Eng(self, "tensor")
        self.gpsimd = _Eng(self, "gpsimd")

    def sem(self, name):
        if name not in self.sems:
            self.sems[name] = SemC(name)
        return self.sems[name]

    def barrier(self):
        self.ops.append(("BARRIER", None))

    def replay(self, nc):
        segs = [[]]
        for item in self.ops:
            if item[0] == "BARRIER":
                if segs[-1]:
                    segs.append([])
            else:
                segs[-1].append(item)
        if segs and not segs[-1]:
            segs.pop()
        with ExitStack() as st:
            for s in self.sems.values():
                s.h = st.enter_context(nc.semaphore(s.name))
            for seg in segs:
                per = {}
                for eng, fn in seg:
                    per.setdefault(eng, []).append(fn)
                with nc.Block() as blk:
                    for eng, lst in per.items():
                        def make(lst_):
                            def body(e):
                                for f in lst_:
                                    f(e)
                            return body
                        getattr(blk, eng)(make(lst))


class Ctx:

    def __init__(self, name, bf16_banks=0):
        self.nc = bass.Bass("TRN2", target_bir_lowering=False)
        self.p = Prog()
        self.uid = 0
        self.name = name
        self._scratch = {}
        nf = 4 - (bf16_banks + 1) // 2
        self.pool = [self.nc.alloc_psum_tensor(f"pool{i}", [128, 1024], F32).ap()
                     for i in range(nf)]
        self.bpool = [self.nc.alloc_psum_tensor(f"bpool{i}", [128, 2048], BF16).ap()
                      for i in range((bf16_banks + 1) // 2)]

    def pv(self, bank, width=512):
        t = self.pool[bank // 2]
        off = (bank % 2) * 512
        return t[:, off:off + width]

    def prow(self, i):
        return self.pool[i][0:1, 0:1024]

    def pfull(self, i):
        return self.pool[i]

    def nid(self, base):
        self.uid += 1
        return f"{base}_{self.uid}"

    def enable_arena(self, nbytes):
        res = self.nc.bump_sbuf(nbytes)
        assert res is not None, "no SBUF left for arena"
        self.arena_base = res[0]
        self.arena_bytes = nbytes
        self.arena_mode = True
        self.aoff = 0
        self.amax_seen = 0

    def phase_reset(self, tag):
        self._scratch = {}
        self.aoff = 0
        for s in self.p.sems.values():
            s.base = s.n
        if hasattr(self, "xsq_buf"):
            del self.xsq_buf

    def sb(self, name, shape, dtype):
        if not getattr(self, "arena_mode", False):
            return self.nc.alloc_sbuf_tensor(self.nid(name), shape, dtype).ap()
        n = int(np.prod(shape[1:]))
        esz = 2 if dtype == BF16 else 4
        nb = (n * esz + 31) & ~31
        off = self.aoff
        assert off + n * esz <= self.arena_bytes, (
            f"arena overflow {name} {shape} {dtype} at {off}")
        self.aoff = off + nb
        self.amax_seen = max(self.amax_seen, self.aoff)
        t = self.nc.alloc_sbuf_tensor_at(self.nid("ar_" + name),
                                         [shape[0], n], dtype,
                                         offset=self.arena_base + off)
        return t.ap()

    def scratch(self, name, shape, dtype):
        key = (name, tuple(shape), dtype)
        if key not in self._scratch:
            self._scratch[key] = self.sb(name, shape, dtype)
        return self._scratch[key]

    def din(self, name, shape, dtype=F32):
        if getattr(self, "iomap", None) is not None:
            return self.iomap[name]
        return self.nc.dram_tensor(name, shape, dtype, kind="ExternalInput").ap()

    def dout(self, name, shape, dtype=F32):
        if getattr(self, "iomap", None) is not None:
            return self.iomap[name]
        return self.nc.dram_tensor(name, shape, dtype, kind="ExternalOutput").ap()

    def dint(self, name, shape, dtype=F32):
        return self.nc.dram_tensor(self.nid(name), shape, dtype, kind="Internal").ap()

    def alloc_consts(self):
        g = self.p.gpsimd
        self.ones_col_bf = self.sb("ones_col_bf", [128, 1], BF16)
        self.ones_row_bf = self.sb("ones_row_bf", [1, 128], BF16)
        self.ones_row_f = self.sb("ones_row_f", [1, 128], F32)
        g.memset(self.ones_col_bf, 1.0)
        g.memset(self.ones_row_bf, 1.0)
        g.memset(self.ones_row_f, 1.0)
        self._cbias = {}

    def cbias(self, value, parts=128):
        key = (float(value), parts)
        if key not in self._cbias:
            t = self.sb("cbias", [parts, 1], F32)
            self.p.gpsimd.memset(t, float(value))
            self._cbias[key] = t
        return self._cbias[key]



def emit_norm_fm(ctx, x_sb, h_out, T, rms_only=False, weight_col=None,
                 eps=1e-5):
    p = ctx.p
    nt = T // 512
    xbf = ctx.scratch("sn_xbf", [128, DCH * T], BF16) if not rms_only else None
    xsq = getattr(ctx, "xsq_buf", None)
    if xsq is None:
        xsq = ctx.scratch("sn_xsq2", [128, DCH * T], BF16)
    mean_r = ctx.scratch("sn_mean", [1, T], F32)
    rstd_r = ctx.scratch("sn_rstd", [1, T], F32)
    var_r = ctx.scratch("sn_var", [1, T], F32)
    tmp = ctx.scratch("sn_tmp", [128, T], F32)
    p.barrier()
    for c in range(DCH):
        sl = slice(c * T, (c + 1) * T)
        if not rms_only:
            p.vector.tensor_copy(xbf[:, sl], x_sb[:, sl])
        p.scalar.activation(xsq[:, sl], x_sb[:, sl], AF.Square)
    p.barrier()
    for c in range(DCH):
        for tt in range(nt):
            tsl = slice(c * T + tt * 512, c * T + tt * 512 + 512)
            osl = slice(tt * 512, tt * 512 + 512)
            if not rms_only:
                p.tensor.matmul(ctx.prow(0)[:, osl], ctx.ones_col_bf, xbf[:, tsl],
                                start=(c == 0), stop=(c == DCH - 1))
            p.tensor.matmul(ctx.prow(1)[:, osl], ctx.ones_col_bf, xsq[:, tsl],
                            start=(c == 0), stop=(c == DCH - 1))
    p.barrier()
    if rms_only:
        p.vector.tensor_scalar_mul(var_r, ctx.prow(1), 1.0 / DM)
    else:
        p.vector.tensor_scalar_mul(mean_r, ctx.prow(0), 1.0 / DM)
        p.vector.tensor_scalar_mul(var_r, ctx.prow(1), 1.0 / DM)
        p.vector.tensor_mul(rstd_r, mean_r, mean_r)
        p.vector.tensor_sub(var_r, var_r, rstd_r)
    p.barrier()
    p.scalar.activation(rstd_r, var_r, AF.Ln, bias=ctx.cbias(eps, 1), scale=1.0)
    p.scalar.activation(rstd_r, rstd_r, AF.Exp, scale=-0.5)
    p.barrier()
    for tt in range(nt):
        osl = slice(tt * 512, tt * 512 + 512)
        if not rms_only:
            p.tensor.matmul(ctx.pfull(0)[:, osl], ctx.ones_row_f, mean_r[:, osl],
                            start=True, stop=True)
        p.tensor.matmul(ctx.pfull(1)[:, osl], ctx.ones_row_f, rstd_r[:, osl],
                        start=True, stop=True)
    p.barrier()
    for c in range(DCH):
        sl = slice(c * T, (c + 1) * T)
        if rms_only:
            p.vector.tensor_mul(tmp, x_sb[:, sl], ctx.pfull(1))
        else:
            p.vector.tensor_sub(tmp, x_sb[:, sl], ctx.pfull(0))
            p.vector.tensor_mul(tmp, tmp, ctx.pfull(1))
        if weight_col is not None:
            p.vector.tensor_single_scalar(tmp, tmp, weight_col[:, c:c + 1],
                                          ALU.mult)
        p.vector.tensor_copy(h_out[:, sl], tmp)
    p.barrier()


def emit_load_x(ctx, xsh, x_sb, T):
    p = ctx.p
    sX = p.sem(ctx.nid("sx"))
    for c in range(DCH):
        p.sync.op("dma_start", out=x_sb[:, c * T:(c + 1) * T],
                  in_=xsh[c * 128:(c + 1) * 128, :], inc=[(sX, 16)])
    p.sync.wait(sX)
    p.barrier()


def emit_store_x(ctx, x_sb, xout, T, sO):
    p = ctx.p
    p.barrier()
    for c in range(DCH):
        p.sync.op("dma_start", out=xout[c * 128:(c + 1) * 128, :],
                  in_=x_sb[:, c * T:(c + 1) * T], inc=[(sO, 16)])
    p.sync.wait(sO)



def build_gprep(ctx=None):
    if ctx is None:
        ctx = Ctx("gprep")
    p = ctx.p
    W2T = ctx.din("W2T", [SEQ, DM], F32)
    H128 = ctx.din("H128", [128, 128], F32)
    G2 = ctx.dout("G2", [16, 64, 128, 128], BF16)

    h_sb = ctx.sb("h128", [128, 128], BF16)
    w_sb = [ctx.sb("w2c", [128, DM], BF16) for _ in range(2)]
    g_sb = [ctx.sb("gc", [128, DM], BF16) for _ in range(2)]

    sL = p.sem("ld")
    sW = [p.sem("w0"), p.sem("w1")]
    sM = p.sem("mm")
    sV = p.sem("vv")
    sO = p.sem("st")

    p.gpsimd.op("dma_start", out=h_sb, in_=H128, inc=[(sL, 16)])
    p.tensor.wait(sL)

    n_mm = 0
    n_vv = 0
    for ap_i in range(64):
        b = ap_i % 2
        if ap_i >= 2:
            p.gpsimd.wait(sM, (ap_i - 1) * 4)
        p.gpsimd.op("dma_start", out=w_sb[b],
                    in_=W2T[ap_i * 128:(ap_i + 1) * 128, :], inc=[(sW[b], 16)])
        p.tensor.wait(sW[b])
        for ot in range(4):
            p.tensor.op("matmul", ctx.pv(ot), h_sb,
                        w_sb[b][:, ot * 512:(ot + 1) * 512],
                        start=True, stop=True, inc=[(sM, 1)])
            n_mm += 1
        p.vector.wait(sM, n_mm)
        if ap_i >= 2:
            p.vector.wait(sO, 16 * (ap_i - 1))
        for ot in range(4):
            p.vector.op("tensor_copy", g_sb[b][:, ot * 512:(ot + 1) * 512],
                        ctx.pv(ot), inc=[(sV, 1)])
            n_vv += 1
        p.gpsimd.wait(sV, n_vv)
        out_ap = bass.AP(G2.tensor, ap_i * 128,
                         [[16384, 64], [8192, 2], [1048576, 16], [1, 128]])
        p.gpsimd.op("dma_start", out=out_ap, in_=g_sb[b], inc=[(sO, 16)])
    p.gpsimd.wait(sO)
    return ctx



def build_f0(ctx=None):
    if ctx is None:
        ctx = Ctx("f0", bf16_banks=2)
    p = ctx.p
    T = TPC
    nt = T // 512
    xsh = ctx.din("xsh", [DM, T], F32)
    W1p = ctx.din("W1p", [64, DCH, 128, 128], F32)
    W3p = ctx.din("W3p", [64, DCH, 128, 128], F32)
    G2 = ctx.din("G2", [16, 64, 128, 128], BF16)
    H64B = ctx.din("H64B", [128, 128], F32)
    xout = ctx.dout("xout", [DM, T], F32)

    ctx.alloc_consts()
    h64 = ctx.sb("h64", [128, 128], BF16)
    sA = ctx.sb("sA", [128, 64 * T], BF16)
    xsq = sA[:, 0:DCH * T]
    h_bf = ctx.sb("h_bf", [128, DCH * T], BF16)
    w1sb = [ctx.sb("w1sb", [128, DCH * 128], BF16) for _ in range(2)]
    w3sb = [ctx.sb("w3sb", [128, DCH * 128], BF16) for _ in range(2)]
    gsb = [ctx.sb("gsb", [128, 8 * 128], BF16) for _ in range(2)]
    s1 = ctx.sb("s1", [128, T], BF16)
    mid = s1
    xin = [ctx.sb("xin", [128, T], F32) for _ in range(2)]
    mean_r = ctx.sb("f0_mean", [1, T], F32)
    rstd_r = ctx.sb("f0_rstd", [1, T], F32)
    tmp = ctx.sb("f0_tmp", [128, T], F32)
    var_r = tmp[0:1, 0:T]
    xo_sb = tmp

    sX = p.sem("sxl")
    sW = [p.sem("sw0"), p.sem("sw1")]
    sG2w = [p.sem("sg0"), p.sem("sg1")]
    sM = p.sem("sm")
    sSc = p.sem("ssc")
    sV = p.sem("sv")
    sM2 = p.sem("sm2")
    sV2 = p.sem("sv2")
    sO = p.sem("so")
    sH = p.sem("sh")
    sLNv = p.sem("slnv")

    ctx.p.gpsimd.op("dma_start", out=h64, in_=H64B, inc=[(sH, 16)])

    for c in range(DCH):
        b = c % 2
        if c >= 2:
            p.sync.wait(sLNv, 2 * (c - 1))
        p.sync.op("dma_start", out=xin[b], in_=xsh[c * 128:(c + 1) * 128, :],
                  inc=[(sX, 16)])
        sl = slice(c * T, (c + 1) * T)
        p.vector.wait(sX, 16 * (c + 1))
        p.vector.op("tensor_copy", h_bf[:, sl], xin[b], inc=[(sLNv, 1)])
        p.scalar.wait(sX, 16 * (c + 1))
        p.scalar.op("activation", xsq[:, sl], xin[b], AF.Square, inc=[(sLNv, 1)])
    p.barrier()
    for c in range(DCH):
        for tt in range(nt):
            tsl = slice(c * T + tt * 512, c * T + tt * 512 + 512)
            osl = slice(tt * 512, tt * 512 + 512)
            p.tensor.matmul(ctx.prow(0)[:, osl], ctx.ones_col_bf, h_bf[:, tsl],
                            start=(c == 0), stop=(c == DCH - 1))
            p.tensor.matmul(ctx.prow(1)[:, osl], ctx.ones_col_bf, xsq[:, tsl],
                            start=(c == 0), stop=(c == DCH - 1))
    p.barrier()
    p.vector.tensor_scalar_mul(mean_r, ctx.prow(0), 1.0 / DM)
    p.vector.tensor_scalar_mul(var_r, ctx.prow(1), 1.0 / DM)
    p.vector.tensor_mul(rstd_r, mean_r, mean_r)
    p.vector.tensor_sub(var_r, var_r, rstd_r)
    p.barrier()
    p.scalar.activation(rstd_r, var_r, AF.Ln, bias=ctx.cbias(1e-5, 1), scale=1.0)
    p.scalar.activation(rstd_r, rstd_r, AF.Exp, scale=-0.5)
    p.barrier()
    for tt in range(nt):
        osl = slice(tt * 512, tt * 512 + 512)
        p.tensor.matmul(ctx.pfull(0)[:, osl], ctx.ones_row_f, mean_r[:, osl],
                        start=True, stop=True)
        p.tensor.matmul(ctx.pfull(1)[:, osl], ctx.ones_row_f, rstd_r[:, osl],
                        start=True, stop=True)
    p.barrier()
    for c in range(DCH):
        sl = slice(c * T, (c + 1) * T)
        p.vector.tensor_sub(tmp, h_bf[:, sl], ctx.pfull(0))
        p.vector.tensor_mul(tmp, tmp, ctx.pfull(1))
        p.vector.tensor_copy(h_bf[:, sl], tmp)
    p.barrier()
    p.tensor.wait(sH)

    GRP = 3 * nt
    nsc = 0
    nv = 0
    for jp in range(64):
        b = jp % 2
        if jp >= 2:
            p.gpsimd.wait(sM, (jp - 1) * GRP)
        p.gpsimd.op("dma_start", out=w1sb[b],
                    in_=W1p[jp].rearrange("c p f -> p c f"), inc=[(sW[b], 16)])
        p.gpsimd.op("dma_start", out=w3sb[b],
                    in_=W3p[jp].rearrange("c p f -> p c f"), inc=[(sW[b], 16)])
        p.tensor.wait(sW[b])
        if jp >= 1:
            p.tensor.wait(sSc, jp * nt)
            p.tensor.wait(sV, jp * 2 * nt)
        for tt in range(nt):
            for c in range(DCH):
                rsl = slice(c * T + tt * 512, c * T + tt * 512 + 512)
                p.tensor.op("matmul", ctx.pv(0 + tt), w1sb[b][:, c * 128:(c + 1) * 128],
                            h_bf[:, rsl], start=(c == 0), stop=(c == DCH - 1),
                            inc=[(sM, 1)] if c == DCH - 1 else None)
            for c in range(DCH):
                rsl = slice(c * T + tt * 512, c * T + tt * 512 + 512)
                p.tensor.op("matmul", ctx.pv(2 + tt), w3sb[b][:, c * 128:(c + 1) * 128],
                            h_bf[:, rsl], start=(c == 0), stop=(c == DCH - 1),
                            inc=[(sM, 1)] if c == DCH - 1 else None)
        p.scalar.wait(sM, jp * GRP + 2 * nt)
        if jp >= 1:
            p.scalar.wait(sV, (jp - 1) * 2 * nt + nt)
        for tt in range(nt):
            p.scalar.op("activation", s1[:, tt * 512:(tt + 1) * 512],
                        ctx.pv(0 + tt), AF.Silu, inc=[(sSc, 1)])
            nsc += 1
        p.vector.wait(sSc, nsc)
        for tt in range(nt):
            p.vector.op("tensor_mul", mid[:, tt * 512:(tt + 1) * 512],
                        s1[:, tt * 512:(tt + 1) * 512], ctx.pv(2 + tt),
                        inc=[(sV, 1)])
            nv += 1
        p.tensor.wait(sV, nv)
        for tt in range(nt):
            p.tensor.op("matmul", ctx.pv(4 + tt), h64,
                        mid[:, tt * 512:(tt + 1) * 512], start=True, stop=True,
                        inc=[(sM, 1)])
        p.vector.wait(sM, jp * GRP + 3 * nt)
        for tt in range(nt):
            p.vector.op("tensor_copy",
                        sA[:, jp * T + tt * 512: jp * T + tt * 512 + 512],
                        ctx.pv(4 + tt), inc=[(sV, 1)])
            nv += 1
    p.barrier()

    nv2 = 0
    sQ = p.sem("sq")
    for ot in range(DCH):
        if ot >= 2:
            p.gpsimd.wait(sV2, (ot - 1) * nt)
        p.gpsimd.op("dma_start", out=xin[ot % 2],
                    in_=xsh[ot * 128:(ot + 1) * 128, :], inc=[(sX, 16)])
        for q in range(8):
            qidx = ot * 8 + q
            b = qidx % 2
            if qidx >= 2:
                p.sync.wait(sQ, qidx - 1)
            p.sync.op("dma_start", out=gsb[b],
                      in_=G2[ot, q * 8:(q + 1) * 8].rearrange("jp p f -> p jp f"),
                      inc=[(sG2w[b], 16)])
            p.tensor.wait(sG2w[b])
            if q == 0 and ot >= 1:
                p.tensor.wait(sV2, ot * nt)
            for tt in range(nt):
                for jj in range(8):
                    jp = q * 8 + jj
                    last = (tt == nt - 1 and jj == 7)
                    p.tensor.op("matmul", ctx.pv(0 + tt),
                                gsb[b][:, jj * 128:(jj + 1) * 128],
                                sA[:, jp * T + tt * 512: jp * T + tt * 512 + 512],
                                start=(jp == 0), stop=(jp == 63),
                                inc=[(sQ, 1)] if last else None)
        p.vector.wait(sQ, (ot + 1) * 8)
        p.vector.wait(sX, 16 * (DCH + ot + 1))
        if ot >= 1:
            p.vector.wait(sO, 16 * ot)
        for tt in range(nt):
            p.vector.op("scalar_tensor_tensor",
                        xo_sb[:, tt * 512:(tt + 1) * 512], ctx.pv(0 + tt),
                        0.5 / math.sqrt(2.0),
                        xin[ot % 2][:, tt * 512:(tt + 1) * 512],
                        ALU.mult, ALU.add, inc=[(sV2, 1)])
            nv2 += 1
        p.sync.wait(sV2, nv2)
        p.sync.op("dma_start", out=xout[ot * 128:(ot + 1) * 128, :],
                  in_=xo_sb, inc=[(sO, 16)])
    p.sync.wait(sO)
    p.gpsimd.wait(sO)
    return ctx




def emit_stream_norm(ctx, xsh, h_bf, T, resid_xout=None, resid_scale=0.5,
                     eps=1e-5, h_f32_out=None):
    p = ctx.p
    nt = T // 512
    xin = ctx.scratch("sn_xin", [128, T], F32)
    xin2 = ctx.scratch("sn_xin2", [128, T], F32)
    xsq = getattr(ctx, "xsq_buf", None)
    if xsq is None:
        xsq = ctx.scratch("sn_xsq", [128, DCH * T], BF16)
    mean_r = ctx.scratch("sn_mean", [1, T], F32)
    rstd_r = ctx.scratch("sn_rstd", [1, T], F32)
    tmp = ctx.scratch("sn_tmp", [128, T], F32)
    var_r = ctx.scratch("sn_var", [1, T], F32)
    sX = ctx.p.sem("snx")
    sC = ctx.p.sem("snc")

    for c in range(DCH):
        b = c % 2
        buf = xin if b == 0 else xin2
        if c >= 2:
            p.sync.wait(sC, 2 * (c - 1))
        p.sync.op("dma_start", out=buf, in_=xsh[c * 128:(c + 1) * 128, :],
                  inc=[(sX, 16)])
        sl = slice(c * T, (c + 1) * T)
        p.vector.wait(sX, 16 * (c + 1))
        p.vector.op("tensor_copy", h_bf[:, sl], buf, inc=[(sC, 1)])
        p.scalar.wait(sX, 16 * (c + 1))
        p.scalar.op("activation", xsq[:, sl], buf, AF.Square, inc=[(sC, 1)])
    p.barrier()
    for c in range(DCH):
        for tt in range(nt):
            tsl = slice(c * T + tt * 512, c * T + tt * 512 + 512)
            osl = slice(tt * 512, tt * 512 + 512)
            p.tensor.matmul(ctx.prow(0)[:, osl], ctx.ones_col_bf, h_bf[:, tsl],
                            start=(c == 0), stop=(c == DCH - 1))
            p.tensor.matmul(ctx.prow(1)[:, osl], ctx.ones_col_bf, xsq[:, tsl],
                            start=(c == 0), stop=(c == DCH - 1))
    p.barrier()
    p.vector.tensor_scalar_mul(mean_r, ctx.prow(0), 1.0 / DM)
    p.vector.tensor_scalar_mul(var_r, ctx.prow(1), 1.0 / DM)
    p.vector.tensor_mul(rstd_r, mean_r, mean_r)
    p.vector.tensor_sub(var_r, var_r, rstd_r)
    p.barrier()
    p.scalar.activation(rstd_r, var_r, AF.Ln, bias=ctx.cbias(eps, 1), scale=1.0)
    p.scalar.activation(rstd_r, rstd_r, AF.Exp, scale=-0.5)
    p.barrier()
    for tt in range(nt):
        osl = slice(tt * 512, tt * 512 + 512)
        p.tensor.matmul(ctx.pfull(0)[:, osl], ctx.ones_row_f, mean_r[:, osl],
                        start=True, stop=True)
        p.tensor.matmul(ctx.pfull(1)[:, osl], ctx.ones_row_f, rstd_r[:, osl],
                        start=True, stop=True)
    p.barrier()
    if resid_xout is None:
        for c in range(DCH):
            sl = slice(c * T, (c + 1) * T)
            p.vector.tensor_sub(tmp, h_bf[:, sl], ctx.pfull(0))
            p.vector.tensor_mul(tmp, tmp, ctx.pfull(1))
            p.vector.tensor_copy(h_bf[:, sl], tmp)
    else:
        sX2 = ctx.p.sem("snx2")
        sV2 = ctx.p.sem("snv2")
        sO2 = ctx.p.sem("sno2")
        for c in range(DCH):
            b = c % 2
            buf = xin if b == 0 else xin2
            if c >= 2:
                p.sync.wait(sV2, 2 * (c - 1))
            p.sync.op("dma_start", out=buf, in_=xsh[c * 128:(c + 1) * 128, :],
                      inc=[(sX2, 16)])
            sl = slice(c * T, (c + 1) * T)
            p.vector.wait(sX2, 16 * (c + 1))
            if c >= 1 and h_f32_out is not None:
                p.vector.wait(sO2)
            p.vector.tensor_sub(tmp, buf, ctx.pfull(0))
            p.vector.tensor_mul(tmp, tmp, ctx.pfull(1))
            p.vector.tensor_copy(h_bf[:, sl], tmp)
            p.vector.op("scalar_tensor_tensor", buf, tmp, resid_scale, buf,
                        ALU.mult, ALU.add, inc=[(sV2, 2)])
            p.sync.wait(sV2, 2 * (c + 1))
            p.sync.op("dma_start", out=resid_xout[c * 128:(c + 1) * 128, :],
                      in_=buf, inc=[(sO2, 16)])
            if h_f32_out is not None:
                p.sync.op("dma_start", out=h_f32_out[c * 128:(c + 1) * 128, :],
                          in_=tmp, inc=[(sO2, 16)])
        p.sync.wait(sO2)
    p.barrier()


def emit_proj(ctx, h_bf, W, out_bf, dout, T, name):
    p = ctx.p
    nt = T // 512
    not_ = dout // 128
    wsl = [ctx.scratch(f"pj_w{i}", [128, DCH * 128], BF16) for i in range(2)]
    sW = [p.sem("pjw0"), p.sem("pjw1")]
    sM = p.sem("pjm")
    sV = p.sem("pjv")
    m0 = sM.rel
    v0 = sV.rel
    p.barrier()
    for ot in range(not_):
        b = ot % 2
        if ot >= 2:
            p.gpsimd.wait(sM, m0 + (ot - 1) * nt)
        p.gpsimd.op("dma_start", out=wsl[b],
                    in_=W[:, ot * 128:(ot + 1) * 128].rearrange(
                        "(c p) f -> p c f", p=128),
                    inc=[(sW[b], 16)])
        p.tensor.wait(sW[b])
        if ot >= 2:
            p.tensor.wait(sV, v0 + (ot - 1) * nt)
        for tt in range(nt):
            pb = ctx.pv((ot % 2) * 2 + tt)
            for c in range(DCH):
                rsl = slice(c * T + tt * 512, c * T + tt * 512 + 512)
                p.tensor.op("matmul", pb, wsl[b][:, c * 128:(c + 1) * 128],
                            h_bf[:, rsl], start=(c == 0), stop=(c == DCH - 1),
                            inc=[(sM, 1)] if c == DCH - 1 else None)
        p.vector.wait(sM, m0 + (ot + 1) * nt)
        for tt in range(nt):
            pb = ctx.pv((ot % 2) * 2 + tt)
            p.vector.op("tensor_copy",
                        out_bf[:, ot * T + tt * 512: ot * T + tt * 512 + 512],
                        pb, inc=[(sV, 1)])
    p.barrier()


def emit_rope_head(ctx, src, dst, T, cdup_sb, sdup_sb, pswap_sb, bank=4):
    p = ctx.p
    nt = T // 512
    rt1 = ctx.scratch("rp_t1", [128, T], BF16)
    rt2 = ctx.scratch("rp_t2", [128, T], BF16)
    p.barrier()
    for tt in range(nt):
        p.tensor.matmul(ctx.pv(bank + tt), pswap_sb,
                        src[:, tt * 512:(tt + 1) * 512], start=True, stop=True)
    p.vector.tensor_mul(rt1, src, cdup_sb)
    p.barrier()
    for tt in range(nt):
        sl = slice(tt * 512, (tt + 1) * 512)
        p.vector.tensor_mul(rt2[:, sl], ctx.pv(bank + tt), sdup_sb[:, sl])
    p.barrier()
    p.vector.tensor_add(dst, rt1, rt2)
    p.barrier()


def emit_bound_softmax_fm(ctx, q_bf, nheads, T, scratch_bf):
    p = ctx.p
    nt = T // 512
    row1 = ctx.scratch("bs_row1", [1, 512], F32)
    row2 = ctx.scratch("bs_row2", [1, 512], F32)
    p.barrier()
    for h in range(nheads):
        for tt in range(nt):
            sl = slice(h * T + tt * 512, h * T + tt * 512 + 512)
            p.barrier()
            p.scalar.activation(scratch_bf, q_bf[:, sl], AF.Square)
            p.barrier()
            p.tensor.matmul(ctx.prow(0)[:, 0:512], ctx.ones_col_bf, scratch_bf,
                            start=True, stop=True)
            p.barrier()
            p.scalar.activation(row1, ctx.prow(0)[:, 0:512], AF.Ln,
                                bias=ctx.cbias(1e-12, 1), scale=1.0)
            p.scalar.activation(row1, row1, AF.Exp, scale=0.5)
            p.barrier()
            p.vector.tensor_scalar_mul(row1, row1, -1.0)
            p.barrier()
            p.tensor.matmul(ctx.pfull(1)[:, 0:512], ctx.ones_row_f, row1,
                            start=True, stop=True)
            p.barrier()
            p.vector.tensor_add(scratch_bf, q_bf[:, sl], ctx.pfull(1)[:, 0:512])
            p.barrier()
            p.scalar.activation(scratch_bf, scratch_bf, AF.Exp)
            p.barrier()
            p.tensor.matmul(ctx.prow(0)[:, 0:512], ctx.ones_col_bf, scratch_bf,
                            start=True, stop=True)
            p.barrier()
            p.vector.reciprocal(row2, ctx.prow(0)[:, 0:512])
            p.barrier()
            p.tensor.matmul(ctx.pfull(1)[:, 0:512], ctx.ones_row_f, row2,
                            start=True, stop=True)
            p.barrier()
            p.vector.tensor_mul(q_bf[:, sl], scratch_bf, ctx.pfull(1)[:, 0:512])
    p.barrier()



def build_a0a(ctx=None):
    if ctx is None:
        ctx = Ctx("a0a", bf16_banks=2)
    p = ctx.p
    T = TPC
    nt = T // 512
    NH, NG = 16, 4
    xsh = ctx.din("xsh", [DM, T], F32)
    Wq = ctx.din("Wq", [DM, DM], F32)
    Wk = ctx.din("Wk", [DM, 512], F32)
    Wv = ctx.din("Wv", [DM, 512], F32)
    cosd = ctx.din("cosd", [128, T], BF16)
    sind = ctx.din("sind", [128, T], BF16)
    PSW = ctx.din("PSW", [128, 128], BF16)
    CUM = ctx.din("CUM", [128, 128], BF16)
    SUF = ctx.din("SUF", [128, 128], BF16)
    IDT = ctx.din("IDT", [128, 128], BF16)

    sq_out = ctx.dout("sq_out", [DM, T], BF16)
    sk_out = ctx.dout("sk_out", [512, T], BF16)
    vtok_out = ctx.dout("vtok_out", [T, 512], BF16)
    ogla_out = ctx.dout("ogla_out", [8, NH, 128, 128], BF16)
    acache_out = ctx.dout("acache_out", [8, NH, 128, 128], BF16)
    boff_out = ctx.dout("boff_out", [128, 8 * NG], F32)
    D_out = ctx.dout("D_out", [2, 128, NG], F32)
    U_out = ctx.dout("U_out", [2, NG, 128, 128], F32)
    kmax_out = ctx.dout("kmax_out", [1, NG], F32)

    ctx.alloc_consts()
    h_bf = ctx.sb("h_bf", [128, DCH * T], BF16)
    q_bf = ctx.sb("q_bf", [128, NH * T], BF16)
    ctx.xsq_buf = q_bf[:, 0:DCH * T]
    k_bf = ctx.sb("k_bf", [128, NG * T], BF16)
    v_bf = ctx.sb("v_bf", [128, NG * T], BF16)
    sq_t = [ctx.sb("sq_t", [128, T], BF16) for _ in range(2)]
    sk_sb = ctx.sb("sk_sb", [128, NG * T], BF16)
    cos_sb = ctx.sb("cos_sb", [128, T], BF16)
    sin_sb = ctx.sb("sin_sb", [128, T], BF16)
    psw_sb = ctx.sb("psw_sb", [128, 128], BF16)
    cum_sb = ctx.sb("cum_sb", [128, 128], BF16)
    suf_sb = ctx.sb("suf_sb", [128, 128], BF16)
    idt_sb = ctx.sb("idt_sb", [128, 128], BF16)
    ktok = ctx.sb("ktok", [128, 8 * 512], BF16)
    vtok = ctx.sb("vtok", [128, 8 * 512], BF16)
    kstok = ctx.sb("kstok", [128, 8 * 512], BF16)
    gl_bf = ctx.sb("gl_bf", [128, 8 * 512], BF16)
    bsx = ctx.sb("bsx", [128, 512], BF16)
    expB = ctx.sb("expB", [128, 512], BF16)
    expnB = ctx.sb("expnB", [128, 512], BF16)
    exps = ctx.sb("exps", [128, 512], BF16)
    C_tok = ctx.sb("C_tok", [128, 512], BF16)
    Ktil = ctx.sb("Ktil", [128, 512], BF16)
    C_dm = ctx.sb("C_dm", [128, 512], BF16)
    A_sc = ctx.sb("A_sc", [128, NH * 128], BF16)
    o_sc = ctx.sb("o_sc", [128, NH * 128], BF16)
    S_sb = ctx.sb("S_sb", [128, 512], F32)
    S_bf = ctx.sb("S_bf", [128, 512], BF16)
    boff_sb = ctx.sb("boff_sb", [128, NG], F32)
    boffe_sb = ctx.sb("boffe_sb", [128, 8 * NG], F32)
    km_sb = ctx.sb("km_sb", [1, NG], F32)
    kn_row = ctx.scratch("sn_mean", [1, T], F32)

    sL = p.sem("a0l")
    sO = p.sem("a0o")

    p.sync.op("dma_start", out=cos_sb, in_=cosd, inc=[(sL, 16)])
    p.sync.op("dma_start", out=sin_sb, in_=sind, inc=[(sL, 16)])
    p.sync.op("dma_start", out=psw_sb, in_=PSW, inc=[(sL, 16)])
    p.sync.op("dma_start", out=cum_sb, in_=CUM, inc=[(sL, 16)])
    p.sync.op("dma_start", out=suf_sb, in_=SUF, inc=[(sL, 16)])
    p.sync.op("dma_start", out=idt_sb, in_=IDT, inc=[(sL, 16)])
    p.sync.wait(sL)

    emit_stream_norm(ctx, xsh, h_bf, T)
    emit_proj(ctx, h_bf, Wq, q_bf, DM, T, "q")
    emit_proj(ctx, h_bf, Wk, k_bf, 512, T, "k")
    emit_proj(ctx, h_bf, Wv, v_bf, 512, T, "v")

    for h in range(NH):
        b = h % 2
        if h >= 2:
            p.sync.wait(sO)
        emit_rope_head(ctx, q_bf[:, h * T:(h + 1) * T], sq_t[b], T,
                       cos_sb, sin_sb, psw_sb)
        p.sync.op("dma_start", out=sq_out[h * 128:(h + 1) * 128, :],
                  in_=sq_t[b], inc=[(sO, 16)])
    p.barrier()
    for g in range(NG):
        emit_rope_head(ctx, k_bf[:, g * T:(g + 1) * T],
                       sk_sb[:, g * T:(g + 1) * T], T, cos_sb, sin_sb, psw_sb)
        p.sync.op("dma_start", out=sk_out[g * 128:(g + 1) * 128, :],
                  in_=sk_sb[:, g * T:(g + 1) * T], inc=[(sO, 16)])

    p.barrier()
    for g in range(NG):
        p.barrier()
        p.scalar.activation(bsx, k_bf[:, g * T:g * T + 512], AF.Square)
        p.scalar.activation(gl_bf[:, 0:512], k_bf[:, g * T + 512:g * T + 1024],
                            AF.Square)
        p.barrier()
        p.tensor.matmul(ctx.prow(0)[:, 0:512], ctx.ones_col_bf, bsx,
                        start=True, stop=True)
        p.tensor.matmul(ctx.prow(0)[:, 512:1024], ctx.ones_col_bf,
                        gl_bf[:, 0:512], start=True, stop=True)
        p.barrier()
        p.vector.tensor_copy(kn_row, ctx.prow(0))
        p.barrier()
        p.vector.reduce_max(km_sb[:, g:g + 1], kn_row, axis=AX.X)
    p.barrier()
    p.sync.op("dma_start", out=kmax_out, in_=km_sb, inc=[(sO, 16)])

    p.barrier()
    ntr = 0
    sTT = p.sem("a0tt")
    sTV = p.sem("a0tv")
    for tt in range(8):
        for g in range(NG):
            half = ntr % 2
            bview = ctx.bpool[0][:, half * 1024: half * 1024 + 128]
            if ntr >= 2:
                p.tensor.wait(sTV, ntr - 1)
            p.tensor.op("transpose", bview,
                        k_bf[:, g * T + tt * 128: g * T + (tt + 1) * 128],
                        idt_sb, inc=[(sTT, 1)])
            ntr += 1
            p.vector.wait(sTT, ntr)
            p.vector.op("tensor_copy",
                        ktok[:, tt * 512 + g * 128: tt * 512 + (g + 1) * 128],
                        bview, inc=[(sTV, 1)])
            half = ntr % 2
            bview = ctx.bpool[0][:, half * 1024: half * 1024 + 128]
            if ntr >= 2:
                p.tensor.wait(sTV, ntr - 1)
            p.tensor.op("transpose", bview,
                        v_bf[:, g * T + tt * 128: g * T + (tt + 1) * 128],
                        idt_sb, inc=[(sTT, 1)])
            ntr += 1
            p.vector.wait(sTT, ntr)
            p.vector.op("tensor_copy",
                        vtok[:, tt * 512 + g * 128: tt * 512 + (g + 1) * 128],
                        bview, inc=[(sTV, 1)])
    p.barrier()
    for tt in range(8):
        p.sync.op("dma_start",
                  out=vtok_out[tt * 128:(tt + 1) * 128, :],
                  in_=vtok[:, tt * 512:(tt + 1) * 512], inc=[(sO, 16)])

    p.barrier()
    sn_tmp = ctx.scratch("sn_tmp", [128, T], F32)
    t1f = sn_tmp[:, 0:512]
    t2f = sn_tmp[:, 512:1024]
    mcol = ctx.sb("mcol", [128, 1], F32)
    zcol = ctx.sb("zcol", [128, 1], F32)
    for tt in range(8):
        sl = slice(tt * 512, (tt + 1) * 512)
        p.barrier()
        p.scalar.activation(t1f, ktok[:, sl], AF.Abs)
        p.barrier()
        p.scalar.activation(t1f, t1f, AF.Exp, scale=-1.0)
        p.barrier()
        p.scalar.activation(t1f, t1f, AF.Ln, bias=1.0, scale=1.0)
        p.barrier()
        p.scalar.activation(t2f, ktok[:, sl], AF.Relu, scale=-1.0)
        p.barrier()
        p.vector.tensor_add(t1f, t1f, t2f)
        p.vector.tensor_scalar_mul(gl_bf[:, sl], t1f, -1.0 / 16.0)
        for g in range(NG):
            gsl = slice(tt * 512 + g * 128, tt * 512 + (g + 1) * 128)
            p.barrier()
            p.vector.reduce_max(mcol, ktok[:, gsl], axis=AX.X, negate=True)
            p.barrier()
            p.scalar.activation(t1f[:, 0:128], ktok[:, gsl], AF.Exp, bias=mcol,
                                accum_out=zcol)
            p.barrier()
            p.vector.reciprocal(zcol, zcol)
            p.barrier()
            p.vector.tensor_single_scalar(kstok[:, gsl], t1f[:, 0:128], zcol,
                                          ALU.mult)
    p.barrier()

    emit_bound_softmax_fm(ctx, q_bf, NH, T, bsx)

    gz = p.sem("a0gz")
    p.gpsimd.op("memset", S_sb, 0.0, inc=[(gz, 1)])
    p.gpsimd.op("memset", S_bf, 0.0, inc=[(gz, 1)])
    p.gpsimd.op("memset", boff_sb, 0.0, inc=[(gz, 1)])
    p.barrier()
    for sc in range(8):
        chunk_start = (sc % 4 == 0)
        sl = slice(sc * 512, (sc + 1) * 512)
        if chunk_start and sc > 0:
            p.gpsimd.memset(S_sb, 0.0)
            p.gpsimd.memset(S_bf, 0.0)
            p.gpsimd.memset(boff_sb, 0.0)
            p.barrier()
        p.scalar.activation(boffe_sb[:, sc * NG:(sc + 1) * NG], boff_sb, AF.Exp)
        p.barrier()
        for g in range(NG):
            p.tensor.matmul(ctx.pfull(0)[:, g * 128:(g + 1) * 128],
                            gl_bf[:, sc * 512 + g * 128: sc * 512 + (g + 1) * 128],
                            cum_sb, start=True, stop=True)
        p.tensor.matmul(ctx.pfull(1)[:, 0:512], cum_sb, gl_bf[:, sl],
                        start=True, stop=True)
        p.tensor.matmul(ctx.pfull(2)[:, 0:512], suf_sb, gl_bf[:, sl],
                        start=True, stop=True)
        p.barrier()
        p.scalar.activation(expB, ctx.pfull(0)[:, 0:512], AF.Exp)
        p.scalar.activation(expnB, ctx.pfull(1)[:, 0:512], AF.Exp, scale=-1.0)
        p.scalar.activation(exps, ctx.pfull(2)[:, 0:512], AF.Exp)
        p.barrier()
        for g in range(NG):
            p.vector.tensor_add(boff_sb[:, g:g + 1], boff_sb[:, g:g + 1],
                                ctx.pfull(0)[:, g * 128 + 127: g * 128 + 128])
        p.vector.tensor_mul(C_tok, kstok[:, sl], expnB)
        p.vector.tensor_mul(Ktil, kstok[:, sl], exps)
        p.barrier()
        for g in range(NG):
            half = g % 2
            bview = ctx.bpool[0][:, half * 1024: half * 1024 + 128]
            p.tensor.op("transpose", bview, C_tok[:, g * 128:(g + 1) * 128],
                        idt_sb)
            p.barrier()
            p.vector.tensor_copy(C_dm[:, g * 128:(g + 1) * 128], bview)
            p.barrier()
        for h in range(NH):
            g = h // 4
            p.vector.tensor_mul(A_sc[:, h * 128:(h + 1) * 128],
                                q_bf[:, h * T + sc * 128: h * T + sc * 128 + 128],
                                expB[:, g * 128:(g + 1) * 128])
        p.barrier()
        p.sync.op("dma_start",
                  out=acache_out[sc].rearrange("h d t -> d h t"),
                  in_=A_sc, inc=[(sO, 16)])
        for h in range(NH):
            g = h // 4
            pb = ctx.pv(4 + (h % 2))
            p.tensor.matmul(pb[:, 0:128], C_dm[:, g * 128:(g + 1) * 128],
                            A_sc[:, h * 128:(h + 1) * 128], start=True, stop=True)
            p.barrier()
            p.vector.tensor_mul(bsx[:, 0:128], pb[:, 0:128], cum_sb)
            p.barrier()
            ob = ctx.pfull(1)[:, (h % 4) * 128:(h % 4) * 128 + 128]
            p.tensor.matmul(ob, vtok[:, sc * 512 + g * 128: sc * 512 + (g + 1) * 128],
                            bsx[:, 0:128], start=True, stop=chunk_start)
            if not chunk_start:
                p.tensor.matmul(ob, S_bf[:, g * 128:(g + 1) * 128],
                                A_sc[:, h * 128:(h + 1) * 128], start=False,
                                stop=True)
            p.barrier()
            p.vector.tensor_scalar_mul(o_sc[:, h * 128:(h + 1) * 128], ob, 0.5)
            p.barrier()
        p.sync.op("dma_start",
                  out=ogla_out[sc].rearrange("h d t -> d h t"),
                  in_=o_sc, inc=[(sO, 16)])
        for g in range(NG):
            p.tensor.matmul(ctx.pfull(2)[:, g * 128:(g + 1) * 128],
                            Ktil[:, g * 128:(g + 1) * 128],
                            vtok[:, sc * 512 + g * 128: sc * 512 + (g + 1) * 128],
                            start=True, stop=True)
        p.barrier()
        for g in range(NG):
            p.vector.scalar_tensor_tensor(
                S_sb[:, g * 128:(g + 1) * 128], S_sb[:, g * 128:(g + 1) * 128],
                expB[:, g * 128 + 127: g * 128 + 128],
                ctx.pfull(2)[:, g * 128:(g + 1) * 128], ALU.mult, ALU.add)
        p.barrier()
        p.vector.tensor_copy(S_bf, S_sb)
        p.barrier()
        if sc % 4 == 3:
            ci = sc // 4
            p.scalar.activation(t1f[:, 0:NG], boff_sb, AF.Exp)
            p.barrier()
            p.sync.op("dma_start", out=U_out[ci].rearrange("g d j -> d g j"),
                      in_=S_sb, inc=[(sO, 16)])
            p.sync.op("dma_start", out=D_out[ci],
                      in_=t1f[:, 0:NG], inc=[(sO, 16)])
            p.barrier()
    p.barrier()
    p.sync.op("dma_start", out=boff_out, in_=boffe_sb, inc=[(sO, 16)])
    p.sync.wait(sO)
    p.gpsimd.wait(sO)
    return ctx




SC128 = 1.0 / math.sqrt(128.0)


def emit_proj_resid(ctx, src_bf, W, xsh, xout, T, scale, sO, nj=DCH,
                    h2_in=None, h2_out=None, bias_col=None):
    p = ctx.p
    nt = T // 512
    nev = 2 if h2_in is not None else 1
    wsl = [ctx.scratch(f"pr_w{i}", [128, nj * 128], BF16) for i in range(2)]
    xin = ctx.scratch("pr_xin", [128, T], F32)
    xin2 = ctx.scratch("pr_xin2", [128, T], F32)
    xo = ctx.scratch("pr_xo", [128, T], F32)
    if h2_in is not None:
        hin = ctx.scratch("pr_hin", [128, T], F32)
        ho = ctx.scratch("pr_ho", [128, T], F32)
    sW = [p.sem("prw0"), p.sem("prw1")]
    sM = p.sem("prm")
    sV = p.sem("prv")
    sX = p.sem("prx")
    p.barrier()
    for ot in range(DCH):
        b = ot % 2
        buf = xin if b == 0 else xin2
        if ot >= 2:
            p.gpsimd.wait(sM, (ot - 1) * nt)
            p.gpsimd.wait(sV, (ot - 1) * nt * nev)
        p.gpsimd.op("dma_start", out=wsl[b],
                    in_=W[:, ot * 128:(ot + 1) * 128].rearrange(
                        "(c p) f -> p c f", p=128), inc=[(sW[b], 16)])
        p.gpsimd.op("dma_start", out=buf, in_=xsh[ot * 128:(ot + 1) * 128, :],
                    inc=[(sX, 16)])
        if h2_in is not None:
            if ot >= 1:
                p.gpsimd.wait(sV, ot * nt * nev)
            p.gpsimd.op("dma_start", out=hin,
                        in_=h2_in[ot * 128:(ot + 1) * 128, :], inc=[(sX, 16)])
        p.tensor.wait(sW[b])
        if ot >= 1:
            p.tensor.wait(sV, ot * nt * nev)
        for tt in range(nt):
            pb = ctx.pv((ot % 2) * 2 + tt)
            for c in range(nj):
                rsl = slice(c * T + tt * 512, c * T + tt * 512 + 512)
                p.tensor.op("matmul", pb, wsl[b][:, c * 128:(c + 1) * 128],
                            src_bf[:, rsl], start=(c == 0), stop=(c == nj - 1),
                            inc=[(sM, 1)] if c == nj - 1 else None)
        p.vector.wait(sM, (ot + 1) * nt)
        p.vector.wait(sX)
        if ot >= 1:
            p.vector.wait(sO, 16 * ot * nev)
        for tt in range(nt):
            pb = ctx.pv((ot % 2) * 2 + tt)
            ts = slice(tt * 512, (tt + 1) * 512)
            if bias_col is not None:
                p.vector.op("scalar_tensor_tensor", xo[:, ts], pb,
                            bias_col[:, ot:ot + 1], buf[:, ts],
                            ALU.add, ALU.add, inc=[(sV, 1)])
            else:
                p.vector.op("scalar_tensor_tensor", xo[:, ts], pb, scale,
                            buf[:, ts], ALU.mult, ALU.add, inc=[(sV, 1)])
            if h2_in is not None:
                p.vector.op("scalar_tensor_tensor", ho[:, ts], pb, 1.0,
                            hin[:, ts], ALU.mult, ALU.add, inc=[(sV, 1)])
        p.sync.wait(sV, (ot + 1) * nt * nev)
        p.sync.op("dma_start", out=xout[ot * 128:(ot + 1) * 128, :],
                  in_=xo, inc=[(sO, 16)])
        if h2_in is not None:
            p.sync.op("dma_start", out=h2_out[ot * 128:(ot + 1) * 128, :],
                      in_=ho, inc=[(sO, 16)])
    p.barrier()


def build_flash(mode="a0", ctx=None):
    if ctx is None:
        ctx = Ctx("flash_" + mode, bf16_banks=2)
    p = ctx.p
    T = TPC
    NH, NG = 16, 4
    xsh = ctx.din("xsh", [DM, T], F32)
    sq_in = ctx.din("sq_in", [DM, T], BF16)
    skall = ctx.din("skall", [NCORES, 512, T], BF16)
    vall = ctx.din("vall", [NCORES, T, 512], BF16)
    skown = ctx.din("skown", [512, T], BF16)
    vown = ctx.din("vown", [T, 512], BF16)
    if mode == "a0":
        ogla = ctx.din("ogla", [8, NH, 128, 128], BF16)
        acache = ctx.din("acache", [8, NH, 128, 128], BF16)
        boffe = ctx.din("boffe", [128, 32], F32)
        Dg = ctx.din("Dg", [NCORES, 2, 128, NG], F32)
        Ug = ctx.din("Ug", [NCORES, 2, NG, 128, 128], F32)
        selpf = ctx.din("selpf", [128, 32], F32)
    else:
        hidden_in = ctx.din("hidden_in", [DM, T], F32)
        h2_out = ctx.dout("h2_out", [DM, T], F32)
    kmx = ctx.din("kmx", [1, 32], F32)
    zsel = ctx.din("zsel", [1, 32], F32)
    DMASK = ctx.din("DMASK", [4, 128, 512], BF16)
    Wo = ctx.din("Wo", [DM, DM], F32)
    xout = ctx.dout("xout", [DM, T], F32)

    ctx.alloc_consts()
    sq_sb = ctx.sb("sq_sb", [128, NH * T], BF16)
    OG = ctx.sb("OG", [128, NH * T], BF16)
    sko_sb = ctx.sb("sko", [128, NG * T], BF16)
    vo_sb = ctx.sb("vo", [128, 8 * 512], BF16)
    kvr = [ctx.sb("kvr", [128, 4096], BF16) for _ in range(2)]
    Spf = ctx.sb("Spf", [128, 2 * 512], F32)
    Spf_bf = ctx.sb("Spf_bf", [128, 2 * 512], BF16)
    dld = [ctx.sb("dld", [128, NG], F32) for _ in range(2)]
    uld2 = [ctx.sb("uld", [128, NG * 128], F32) for _ in range(2)]
    og_t = ctx.sb("og_t", [128, NH * 128], BF16)
    ac_t = ctx.sb("ac_t", [128, NH * 128], BF16)
    bo_sb = ctx.sb("bo_sb", [128, 32], F32)
    spf_sb = ctx.sb("spf_sb", [128, 32], F32)
    a2_t = ctx.sb("a2_t", [128, 128], BF16)
    km_sb = ctx.sb("km_sb", [1, 32], F32)
    kr_sb = ctx.sb("kr_sb", [1, NG], F32)
    zs_sb = ctx.sb("zs_sb", [1, 32], F32)
    dmask_sb = ctx.sb("dmask_sb", [128, 4 * 512], BF16)
    bsx = ctx.sb("bsx", [128, 512], BF16)
    row1 = ctx.sb("row1", [1, 512], F32)
    negm = ctx.sb("negm", [1, 3 * 512], BF16)
    besb = ctx.sb("besb", [1, 6 * 512], BF16)
    Pbuf = ctx.sb("Pbuf", [128, 3 * 512], BF16)
    linv = ctx.sb("linv", [1, 512], F32)
    linv_bc = ctx.sb("linv_bc", [128, 512], F32)
    t1f = ctx.sb("t1f", [128, 512], F32)

    sL = p.sem("bL")
    sO = p.sem("bO")
    sUD = p.sem("bUD")

    for h2 in range(NH):
        p.sync.op("dma_start", out=sq_sb[:, h2 * T:(h2 + 1) * T],
                  in_=sq_in[h2 * 128:(h2 + 1) * 128, :], inc=[(sL, 16)])
    for g in range(NG):
        p.sync.op("dma_start", out=sko_sb[:, g * T:(g + 1) * T],
                  in_=skown[g * 128:(g + 1) * 128, :], inc=[(sL, 16)])
    p.sync.op("dma_start", out=vo_sb,
              in_=vown.rearrange("(s p) j -> p s j", p=128), inc=[(sL, 16)])
    if mode == "a0":
        p.sync.op("dma_start", out=bo_sb, in_=boffe, inc=[(sL, 16)])
        p.sync.op("dma_start", out=spf_sb, in_=selpf, inc=[(sL, 16)])
    p.sync.op("dma_start", out=km_sb, in_=kmx,
              allow_slow_non_contiguous=True, inc=[(sL, 16)])
    p.sync.op("dma_start", out=zs_sb, in_=zsel, inc=[(sL, 16)])
    p.sync.op("dma_start", out=dmask_sb,
              in_=DMASK.rearrange("s p t -> p s t"), inc=[(sL, 16)])
    p.sync.wait(sL)
    p.barrier()

    gz = p.sem("bgz")
    p.gpsimd.op("memset", Spf, 0.0, inc=[(gz, 1)])
    if mode != "a0":
        p.gpsimd.op("memset", OG, 0.0, inc=[(gz, 1)])
    p.barrier()
    for slot in (range(2) if mode == "a0" else []):
        for j in range(15):
            b = j % 2
            own = j if j <= 7 else 15 - j
            osl = 0 if j <= 7 else 1
            scol = spf_sb[:, slot * 16 + j: slot * 16 + j + 1]
            p.barrier()
            p.sync.op("dma_start", out=dld[b], in_=Dg[own, osl],
                      inc=[(sUD, 16)])
            p.sync.op("dma_start", out=uld2[b],
                      in_=Ug[own, osl].rearrange("g d j2 -> d g j2"),
                      inc=[(sUD, 16)])
            p.vector.wait(sUD)
            p.barrier()
            p.vector.tensor_scalar_add(dld[b], dld[b], -1.0)
            p.vector.tensor_single_scalar(dld[b], dld[b], scol, ALU.mult)
            p.vector.tensor_scalar_add(dld[b], dld[b], 1.0)
            p.vector.tensor_single_scalar(uld2[b], uld2[b], scol, ALU.mult)
            p.barrier()
            for g in range(NG):
                p.vector.scalar_tensor_tensor(
                    Spf[:, slot * 512 + g * 128: slot * 512 + (g + 1) * 128],
                    Spf[:, slot * 512 + g * 128: slot * 512 + (g + 1) * 128],
                    dld[b][:, g:g + 1],
                    uld2[b][:, g * 128:(g + 1) * 128], ALU.mult, ALU.add)
    p.barrier()
    p.vector.tensor_copy(Spf_bf, Spf)
    p.barrier()

    sC = p.sem("bC")
    for slot in (range(2) if mode == "a0" else []):
        for lsc in range(4):
            sc = slot * 4 + lsc
            p.barrier()
            p.sync.op("dma_start", out=og_t,
                      in_=ogla[sc].rearrange("h d t -> d h t"), inc=[(sC, 16)])
            p.sync.op("dma_start", out=ac_t,
                      in_=acache[sc].rearrange("h d t -> d h t"), inc=[(sC, 16)])
            p.vector.wait(sC)
            p.tensor.wait(sC)
            p.barrier()
            for h in range(NH):
                g = h // 4
                p.vector.tensor_single_scalar(a2_t, ac_t[:, h * 128:(h + 1) * 128],
                                              bo_sb[:, sc * NG + g: sc * NG + g + 1],
                                              ALU.mult)
                p.barrier()
                p.tensor.matmul(ctx.pv(0, 128),
                                Spf_bf[:, slot * 512 + g * 128: slot * 512 + (g + 1) * 128],
                                a2_t, start=True, stop=True)
                p.barrier()
                p.vector.scalar_tensor_tensor(
                    OG[:, h * T + sc * 128: h * T + (sc + 1) * 128],
                    ctx.pv(0, 128), 0.5, og_t[:, h * 128:(h + 1) * 128],
                    ALU.mult, ALU.add)
                p.barrier()
    p.barrier()

    p.vector.reduce_max(kr_sb, km_sb.rearrange("o (g c) -> o g c", c=8), axis=AX.X)
    p.barrier()
    p.scalar.activation(kr_sb, kr_sb, AF.Ln, bias=ctx.cbias(1e-12, 1), scale=1.0)
    p.scalar.activation(kr_sb, kr_sb, AF.Exp, scale=0.5)
    p.barrier()

    HGRPS = [[2 * i, 2 * i + 1] for i in range(8)]
    nQK = p.sem("bQK")
    nEX = p.sem("bEX")
    nMK = p.sem("bMK")
    nAV = p.sem("bAV")
    nBE = p.sem("bBE")
    nKV = p.sem("bKV")
    nEV = p.sem("bEV")

    for slot in range(2):
        nkc = 8 if slot == 0 else 16
        qsl = slice(slot * 512, slot * 512 + 512)
        for gi, grp in enumerate(HGRPS):
            p.barrier()
            for hi, h in enumerate(grp):
                g = h // 4
                p.barrier()
                p.scalar.activation(bsx, sq_sb[:, h * T + slot * 512:
                                               h * T + slot * 512 + 512], AF.Square)
                p.barrier()
                p.tensor.matmul(ctx.pv(4)[0:1, :], ctx.ones_col_bf, bsx,
                                start=True, stop=True)
                p.barrier()
                p.scalar.activation(row1, ctx.pv(4)[0:1, :], AF.Ln,
                                    bias=ctx.cbias(1e-12, 1), scale=1.0)
                p.scalar.activation(row1, row1, AF.Exp, scale=0.5)
                p.barrier()
                p.vector.tensor_single_scalar(row1, row1, kr_sb[:, g:g + 1],
                                              ALU.mult)
                p.vector.tensor_scalar_mul(negm[:, hi * 512:(hi + 1) * 512],
                                           row1, -1.0)
            p.barrier()

            tiles = []
            for hi, h in enumerate(grp):
                for sub in range(4):
                    tiles.append(("diag", hi, h, sub))
            for kc in range(nkc):
                for hi, h in enumerate(grp):
                    for sub in range(4):
                        tiles.append(("main", hi, h, sub, kc))
            ntile = len(tiles)
            h_first = {}
            h_last = {}
            for idx, t_ in enumerate(tiles):
                hh = t_[2]
                if hh not in h_first:
                    h_first[hh] = idx
                h_last[hh] = idx

            qk0 = nQK.rel
            ex0 = nEX.rel
            av0 = nAV.rel
            be0 = nBE.rel
            kv0 = nKV.rel
            mk0 = nMK.rel
            nh = len(grp)

            for kc in range(nkc):
                b = kc % 2
                own = kc if kc <= 7 else 15 - kc
                osl = 0 if kc <= 7 else 1
                if kc >= 2:
                    p.gpsimd.wait(nAV, av0 + 4 * nh * kc)
                p.gpsimd.op("dma_start", out=kvr[b][:, 0:2048],
                            in_=skall[own][:, osl * 512:(osl + 1) * 512].rearrange(
                                "(g p) t -> p g t", p=128), inc=[(nKV, 16)])
                p.gpsimd.op("dma_start", out=kvr[b][:, 2048:4096],
                            in_=vall[own][osl * 512:(osl + 1) * 512, :].rearrange(
                                "(s p) j -> p s j", p=128), inc=[(nKV, 16)])

            def _besb_rows(kc):
                ring = kc % 2
                for hi2, h2 in enumerate(grp):
                    p.vector.tensor_single_scalar(
                        besb[:, (ring * 3 + hi2) * 512:(ring * 3 + hi2 + 1) * 512],
                        negm[:, hi2 * 512:(hi2 + 1) * 512],
                        zs_sb[:, slot * 16 + kc: slot * 16 + kc + 1], ALU.add)
                p.vector.op("engine_nop", inc=[(nBE, 1)])

            for hi, h in enumerate(grp):
                p.vector.tensor_copy(besb[:, (0 * 3 + hi) * 512:(0 * 3 + hi + 1) * 512],
                                     negm[:, hi * 512:(hi + 1) * 512])
            p.vector.op("engine_nop", inc=[(nBE, 1)])
            for idx in range(4 * nh):
                p.vector.wait(nEX, ex0 + idx + 1)
                sub = idx % 4
                p.vector.op("tensor_mul", Pbuf[:, (idx % 3) * 512:(idx % 3 + 1) * 512],
                            Pbuf[:, (idx % 3) * 512:(idx % 3 + 1) * 512],
                            dmask_sb[:, sub * 512:(sub + 1) * 512], inc=[(nMK, 1)])
            for kc in range(nkc):
                if kc == 0:
                    p.vector.wait(nQK, qk0 + 4 * nh)
                if kc >= 2:
                    p.vector.wait(nQK, qk0 + 4 * nh * kc)
                _besb_rows(kc)

            for idx, t_ in enumerate(tiles):
                p.scalar.wait(nQK, qk0 + idx + 1)
                if idx >= 3:
                    p.scalar.wait(nAV, av0 + idx - 2)
                p.scalar.op("activation", Pbuf[:, (idx % 3) * 512:(idx % 3 + 1) * 512],
                            ctx.pv(4 + idx % 2), AF.Exp, scale=SC128,
                            inc=[(nEX, 1)])

            ndiag = 4 * nh
            for idx, t_ in enumerate(tiles):
                kind, hi, h, sub = t_[0], t_[1], t_[2], t_[3]
                g = h // 4
                if kind == "diag":
                    lhs = sko_sb[:, g * T + slot * 512 + sub * 128:
                                 g * T + slot * 512 + (sub + 1) * 128]
                    brow = besb[:, (0 * 3 + hi) * 512:(0 * 3 + hi + 1) * 512]
                    p.tensor.wait(nBE, be0 + 1)
                else:
                    kc = t_[4]
                    lhs = kvr[kc % 2][:, g * 512 + sub * 128:
                                      g * 512 + (sub + 1) * 128]
                    brow = besb[:, ((kc % 2) * 3 + hi) * 512:
                                ((kc % 2) * 3 + hi + 1) * 512]
                    p.tensor.wait(nBE, be0 + 2 + kc)
                    p.tensor.wait(nKV, kv0 + 32 * (kc + 1))
                if idx >= 2:
                    p.tensor.wait(nEX, ex0 + idx - 1)
                p.tensor.matmul(ctx.pv(4 + idx % 2), lhs, sq_sb[:, h * T + qsl.start:
                                h * T + qsl.start + 512], start=True, stop=False)
                p.tensor.op("matmul", ctx.pv(4 + idx % 2), ctx.ones_row_bf, brow,
                            start=False, stop=True, inc=[(nQK, 1)])
                for cidx in ([idx - 1] if idx >= 1 else []) + ([idx] if idx == ntile - 1 else []):
                    ct = tiles[cidx]
                    chi, ch = ct[1], ct[2]
                    cg = ch // 4
                    if cidx < ndiag:
                        p.tensor.wait(nMK, mk0 + cidx + 1)
                    else:
                        p.tensor.wait(nEX, ex0 + cidx + 1)
                    pb = Pbuf[:, (cidx % 3) * 512:(cidx % 3 + 1) * 512]
                    st = (cidx == h_first[ch])
                    sp = (cidx == h_last[ch])
                    p.tensor.matmul(ctx.pv(2 + chi)[0:1, :], ctx.ones_col_bf, pb,
                                    start=st, stop=sp)
                    if ct[0] == "diag":
                        vlhs = vo_sb[:, (slot * 4 + ct[3]) * 512 + cg * 128:
                                     (slot * 4 + ct[3]) * 512 + (cg + 1) * 128]
                    else:
                        vlhs = kvr[ct[4] % 2][:, 2048 + ct[3] * 512 + cg * 128:
                                              2048 + ct[3] * 512 + (cg + 1) * 128]
                    p.tensor.op("matmul", ctx.pv(chi), vlhs, pb,
                                start=st, stop=sp, inc=[(nAV, 1)])

            p.barrier()
            for hi, h in enumerate(grp):
                p.barrier()
                p.vector.reciprocal(linv, ctx.pv(2 + hi)[0:1, :])
                p.barrier()
                p.tensor.matmul(ctx.pv(4), ctx.ones_row_f, linv, start=True,
                                stop=True)
                p.barrier()
                p.vector.tensor_copy(linv_bc, ctx.pv(4))
                p.barrier()
                p.vector.tensor_mul(t1f, ctx.pv(hi), linv_bc)
                p.barrier()
                osl = slice(h * T + slot * 512, h * T + slot * 512 + 512)
                p.vector.scalar_tensor_tensor(OG[:, osl], t1f,
                                              0.5 if mode == "a0" else 1.0,
                                              OG[:, osl], ALU.mult, ALU.add)
            p.barrier()

    if mode == "a0":
        emit_proj_resid(ctx, OG, Wo, xsh, xout, T, 1.0, sO)
    else:
        emit_proj_resid(ctx, OG, Wo, xsh, xout, T, 0.5, sO,
                        h2_in=hidden_in, h2_out=h2_out)
    p.sync.wait(sO)
    p.gpsimd.wait(sO)
    return ctx


def build_a0b():
    return build_flash("a0")




def emit_headnorm(ctx, buf, nheads, T, w_col, eps=1e-6):
    p = ctx.p
    nt = T // 512
    bsx = ctx.scratch("hn_bsx", [128, 512], BF16)
    row1 = ctx.scratch("hn_row", [1, 512], F32)
    p.barrier()
    for h in range(nheads):
        for tt in range(nt):
            sl = slice(h * T + tt * 512, h * T + tt * 512 + 512)
            p.barrier()
            p.scalar.activation(bsx, buf[:, sl], AF.Square)
            p.barrier()
            p.tensor.matmul(ctx.prow(0)[:, 0:512], ctx.ones_col_bf, bsx,
                            start=True, stop=True)
            p.barrier()
            p.scalar.activation(row1, ctx.prow(0)[:, 0:512], AF.Ln,
                                bias=ctx.cbias(eps, 1), scale=1.0 / 128.0)
            p.scalar.activation(row1, row1, AF.Exp, scale=-0.5)
            p.barrier()
            p.tensor.matmul(ctx.pfull(1)[:, 0:512], ctx.ones_row_f, row1,
                            start=True, stop=True)
            p.barrier()
            p.vector.tensor_mul(buf[:, sl], buf[:, sl], ctx.pfull(1)[:, 0:512])
            p.vector.tensor_single_scalar(buf[:, sl], buf[:, sl], w_col,
                                          ALU.mult)
    p.barrier()


def build_f1a(ctx=None):
    if ctx is None:
        ctx = Ctx("f1a", bf16_banks=2)
    p = ctx.p
    T = TPC
    NH, NG = 16, 4
    xsh = ctx.din("xsh", [DM, T], F32)
    anw = ctx.din("anw", [128, DCH], F32)
    qnw = ctx.din("qnw", [128, 1], F32)
    knw = ctx.din("knw", [128, 1], F32)
    qwT = ctx.din("qwT", [DM, DM], F32)
    kwT = ctx.din("kwT", [DM, 512], F32)
    vwT = ctx.din("vwT", [DM, 512], F32)
    cosd = ctx.din("cosd", [128, T], BF16)
    sind = ctx.din("sind", [128, T], BF16)
    PSW = ctx.din("PSW", [128, 128], BF16)
    IDT = ctx.din("IDT", [128, 128], BF16)
    xout = ctx.dout("xout", [DM, T], F32)
    hidden_out = ctx.dout("hidden_out", [DM, T], F32)
    sq_out = ctx.dout("sq_out", [DM, T], BF16)
    sk_out = ctx.dout("sk_out", [512, T], BF16)
    vtok_out = ctx.dout("vtok_out", [T, 512], BF16)
    kmax_out = ctx.dout("kmax_out", [1, NG], F32)

    ctx.alloc_consts()
    h_bf = ctx.sb("h_bf", [128, DCH * T], BF16)
    h1_bf = ctx.sb("h1_bf", [128, DCH * T], BF16)
    q_bf = ctx.sb("q_bf", [128, NH * T], BF16)
    ctx.xsq_buf = q_bf[:, 0:DCH * T]
    k_bf = ctx.sb("k_bf", [128, NG * T], BF16)
    v_bf = ctx.sb("v_bf", [128, NG * T], BF16)
    sq_t = [ctx.sb("sq_t", [128, T], BF16) for _ in range(2)]
    sk_sb = ctx.sb("sk_sb", [128, NG * T], BF16)
    vtok = ctx.sb("vtok", [128, 8 * 512], BF16)
    cos_sb = ctx.sb("cos_sb", [128, T], BF16)
    sin_sb = ctx.sb("sin_sb", [128, T], BF16)
    psw_sb = ctx.sb("psw_sb", [128, 128], BF16)
    idt_sb = ctx.sb("idt_sb", [128, 128], BF16)
    anw_sb = ctx.sb("anw_sb", [128, DCH], F32)
    qnw_sb = ctx.sb("qnw_sb", [128, 1], F32)
    knw_sb = ctx.sb("knw_sb", [128, 1], F32)
    km_sb = ctx.sb("km_sb", [1, NG], F32)
    bsx = ctx.scratch("hn_bsx", [128, 512], BF16)
    kn_row = ctx.scratch("sn_mean", [1, T], F32)

    sL = p.sem("fL")
    sO = p.sem("fO")

    p.sync.op("dma_start", out=cos_sb, in_=cosd, inc=[(sL, 16)])
    p.sync.op("dma_start", out=sin_sb, in_=sind, inc=[(sL, 16)])
    p.sync.op("dma_start", out=psw_sb, in_=PSW, inc=[(sL, 16)])
    p.sync.op("dma_start", out=idt_sb, in_=IDT, inc=[(sL, 16)])
    p.sync.op("dma_start", out=anw_sb, in_=anw, inc=[(sL, 16)])
    p.sync.op("dma_start", out=qnw_sb, in_=qnw, inc=[(sL, 16)])
    p.sync.op("dma_start", out=knw_sb, in_=knw, inc=[(sL, 16)])
    p.sync.wait(sL)
    p.barrier()

    emit_stream_norm(ctx, xsh, h_bf, T, resid_xout=xout, resid_scale=0.5,
                     h_f32_out=hidden_out)
    emit_norm_fm(ctx, h_bf, h1_bf, T, rms_only=True, weight_col=anw_sb,
                 eps=1e-6)
    emit_proj(ctx, h1_bf, qwT, q_bf, DM, T, "q")
    emit_proj(ctx, h1_bf, kwT, k_bf, 512, T, "k")
    emit_proj(ctx, h1_bf, vwT, v_bf, 512, T, "v")
    emit_headnorm(ctx, q_bf, NH, T, qnw_sb)
    emit_headnorm(ctx, k_bf, NG, T, knw_sb)

    p.barrier()
    for g in range(NG):
        for tt in range(2):
            p.barrier()
            p.scalar.activation(bsx, k_bf[:, g * T + tt * 512:
                                          g * T + tt * 512 + 512], AF.Square)
            p.barrier()
            p.tensor.matmul(ctx.prow(0)[:, tt * 512:(tt + 1) * 512],
                            ctx.ones_col_bf, bsx, start=True, stop=True)
        p.barrier()
        p.vector.tensor_copy(kn_row, ctx.prow(0))
        p.barrier()
        p.vector.reduce_max(km_sb[:, g:g + 1], kn_row, axis=AX.X)
    p.barrier()
    p.sync.op("dma_start", out=kmax_out, in_=km_sb, inc=[(sO, 16)])

    for h in range(NH):
        b = h % 2
        if h >= 2:
            p.sync.wait(sO)
        emit_rope_head(ctx, q_bf[:, h * T:(h + 1) * T], sq_t[b], T,
                       cos_sb, sin_sb, psw_sb)
        p.sync.op("dma_start", out=sq_out[h * 128:(h + 1) * 128, :],
                  in_=sq_t[b], inc=[(sO, 16)])
    p.barrier()
    for g in range(NG):
        emit_rope_head(ctx, k_bf[:, g * T:(g + 1) * T],
                       sk_sb[:, g * T:(g + 1) * T], T, cos_sb, sin_sb, psw_sb)
        p.sync.op("dma_start", out=sk_out[g * 128:(g + 1) * 128, :],
                  in_=sk_sb[:, g * T:(g + 1) * T], inc=[(sO, 16)])

    p.barrier()
    ntr = 0
    sTT = p.sem("ftt")
    sTV = p.sem("ftv")
    for tt in range(8):
        for g in range(NG):
            half = ntr % 2
            bview = ctx.bpool[0][:, half * 1024: half * 1024 + 128]
            if ntr >= 2:
                p.tensor.wait(sTV, ntr - 1)
            p.tensor.op("transpose", bview,
                        v_bf[:, g * T + tt * 128: g * T + (tt + 1) * 128],
                        idt_sb, inc=[(sTT, 1)])
            ntr += 1
            p.vector.wait(sTT, ntr)
            p.vector.op("tensor_copy",
                        vtok[:, tt * 512 + g * 128: tt * 512 + (g + 1) * 128],
                        bview, inc=[(sTV, 1)])
    p.barrier()
    for tt in range(8):
        p.sync.op("dma_start", out=vtok_out[tt * 128:(tt + 1) * 128, :],
                  in_=vtok[:, tt * 512:(tt + 1) * 512], inc=[(sO, 16)])
    p.sync.wait(sO)
    p.gpsimd.wait(sO)
    return ctx


def build_moe(debug=False, ctx=None):
    if ctx is None:
        ctx = Ctx("moe", bf16_banks=2)
    p = ctx.p
    T = TPC
    E, NI = 8, 11
    xsh = ctx.din("xsh", [DM, T], F32)
    h2sh = ctx.din("h2sh", [DM, T], F32)
    fnw = ctx.din("fnw", [128, DCH], F32)
    gTf = ctx.din("gTf", [DM, E], F32)
    wgT = ctx.din("wgT", [E, DM, 1408], F32)
    wuT = ctx.din("wuT", [E, DM, 1408], F32)
    wdT = ctx.din("wdT", [E, 1408, DM], F32)
    IDTF = ctx.din("IDTF", [128, 128], F32)
    SEL = ctx.din("SEL", [E, E * 128], F32)
    xout = ctx.dout("xout", [DM, T], F32)
    if debug:
        dbg_lg = ctx.dout("dbg_lg", [128, 64], F32)
        dbg_c8 = ctx.dout("dbg_c8", [E, T], F32)
        dbg_acc = ctx.dout("dbg_acc", [DM, T], F32)
        dbg_h2n = ctx.dout("dbg_h2n", [DM, T], F32)

    ctx.alloc_consts()
    h2n_bf = ctx.sb("h2n_bf", [128, DCH * T], BF16)
    moe_acc = ctx.sb("moe_acc", [128, DCH * T], F32)
    mid = ctx.sb("mid", [128, NI * T], BF16)
    hin = [ctx.sb("hin", [128, T], F32) for _ in range(2)]
    h2n_f = ctx.sb("h2n_f", [128, T], F32)
    rstd_bc = ctx.sb("rstd_bc", [128, T], F32)
    gT_sb = ctx.sb("gT_sb", [128, DCH * E], F32)
    fnw_sb = ctx.sb("fnw_sb", [128, DCH], F32)
    idtf_sb = ctx.sb("idtf_sb", [128, 128], F32)
    idt_sb = ctx.sb("idt_sb", [128, 128], BF16)
    comb_bf = ctx.sb("comb_bf", [128, E], BF16)
    sel_sb = ctx.sb("sel_sb", [E, E * 128], F32)
    lgall = ctx.sb("lgall", [128, 8 * E], F32)
    c8 = ctx.sb("c8", [E, T], F32)
    cbc = ctx.sb("cbc", [128, T], BF16)
    wga = [ctx.sb("wga", [128, DCH * 128], BF16) for _ in range(2)]
    wua = [ctx.sb("wua", [128, DCH * 128], BF16) for _ in range(2)]
    wda = [ctx.sb("wda", [128, NI * 128], BF16) for _ in range(2)]
    s1 = ctx.sb("s1", [128, 512], BF16)
    eus = ctx.sb("eus", [128, 512], BF16)
    xsq = ctx.sb("msq", [128, T], BF16)
    rstd_r = ctx.scratch("sn_rstd", [1, T], F32)
    t8 = ctx.sb("t8", [128, E], F32)
    m1c = ctx.sb("m1c", [128, 1], F32)
    m2c = ctx.sb("m2c", [128, 1], F32)
    zc = ctx.sb("zc", [128, 1], F32)
    comb = ctx.sb("comb", [128, E], F32)

    sL = p.sem("mL")
    sX = p.sem("mX")
    sO = p.sem("mO")
    sW = [p.sem("mw0"), p.sem("mw1")]
    sWd = [p.sem("md0"), p.sem("md1")]

    p.sync.op("dma_start", out=gT_sb,
              in_=gTf.rearrange("(c p) e -> p c e", p=128), inc=[(sL, 16)])
    p.sync.op("dma_start", out=fnw_sb, in_=fnw, inc=[(sL, 16)])
    p.sync.op("dma_start", out=idtf_sb, in_=IDTF, inc=[(sL, 16)])
    p.gpsimd.op("dma_start", out=idt_sb, in_=IDTF, inc=[(sL, 16)])
    p.sync.op("dma_start", out=sel_sb, in_=SEL, inc=[(sL, 16)])
    p.sync.wait(sL)
    p.barrier()

    for c in range(DCH):
        b = c % 2
        p.sync.op("dma_start", out=hin[b], in_=h2sh[c * 128:(c + 1) * 128, :],
                  inc=[(sX, 16)])
        p.scalar.wait(sX, 16 * (c + 1))
        p.scalar.activation(xsq, hin[b], AF.Square)
        p.barrier()
        for tt in range(2):
            p.tensor.matmul(ctx.prow(0)[:, tt * 512:(tt + 1) * 512],
                            ctx.ones_col_bf, xsq[:, tt * 512:(tt + 1) * 512],
                            start=(c == 0), stop=(c == DCH - 1))
        p.barrier()
    p.vector.tensor_scalar_mul(rstd_r, ctx.prow(0), 1.0 / DM)
    p.barrier()
    p.scalar.activation(rstd_r, rstd_r, AF.Ln, bias=ctx.cbias(1e-6, 1), scale=1.0)
    p.scalar.activation(rstd_r, rstd_r, AF.Exp, scale=-0.5)
    p.barrier()
    for tt in range(2):
        p.tensor.matmul(ctx.pfull(0)[:, tt * 512:(tt + 1) * 512],
                        ctx.ones_row_f, rstd_r[:, tt * 512:(tt + 1) * 512],
                        start=True, stop=True)
    p.barrier()
    p.vector.tensor_copy(rstd_bc, ctx.pfull(0))
    p.barrier()

    for c in range(DCH):
        b = c % 2
        p.sync.op("dma_start", out=hin[b], in_=h2sh[c * 128:(c + 1) * 128, :],
                  inc=[(sX, 16)])
        p.vector.wait(sX, 16 * (DCH + c + 1))
        p.barrier()
        p.vector.tensor_mul(h2n_f, hin[b], rstd_bc)
        p.vector.tensor_single_scalar(h2n_f, h2n_f, fnw_sb[:, c:c + 1], ALU.mult)
        p.vector.tensor_copy(h2n_bf[:, c * T:(c + 1) * T], h2n_f)
        p.barrier()
        for ts in range(8):
            p.tensor.op("matmul", ctx.pv(0)[:, ts * E:(ts + 1) * E],
                        h2n_f[:, ts * 128:(ts + 1) * 128],
                        gT_sb[:, c * E:(c + 1) * E],
                        start=(c == 0 and ts == 0), stop=(c == DCH - 1),
                        skip_group_check=True)
        p.barrier()
    p.vector.tensor_copy(lgall, ctx.pv(0)[:, 0:64])
    p.barrier()

    nm1 = ctx.sb("nm1", [128, 1], F32)
    for ts in range(8):
        lg = lgall[:, ts * E:(ts + 1) * E]
        p.barrier()
        p.vector.reduce_max(nm1, lg, axis=AX.X, negate=True)
        p.vector.reduce_max(m1c, lg, axis=AX.X)
        p.barrier()
        p.scalar.activation(t8, lg, AF.Exp, bias=nm1)
        p.barrier()
        p.vector.reduce_sum(zc, t8, axis=AX.X)
        p.barrier()
        p.vector.reciprocal(zc, zc)
        p.barrier()
        p.vector.tensor_single_scalar(t8, t8, zc, ALU.mult)
        p.barrier()
        p.vector.tensor_single_scalar(comb, lg, m1c, ALU.is_ge)
        p.barrier()
        p.vector.scalar_tensor_tensor(comb, comb, -1e30, lg, ALU.mult, ALU.add)
        p.barrier()
        p.vector.reduce_max(m2c, comb, axis=AX.X)
        p.barrier()
        p.vector.tensor_single_scalar(comb, lg, m2c, ALU.is_ge)
        p.barrier()
        p.vector.tensor_mul(t8, t8, comb)
        p.barrier()
        p.vector.reduce_sum(zc, t8, axis=AX.X)
        p.barrier()
        p.vector.tensor_scalar_add(zc, zc, 1e-9)
        p.vector.reciprocal(zc, zc)
        p.barrier()
        p.vector.tensor_single_scalar(comb, t8, zc, ALU.mult)
        p.barrier()
        p.vector.tensor_copy(comb_bf, comb)
        p.barrier()
        p.tensor.op("transpose", ctx.bpool[0][0:8, 0:128], comb_bf, idt_sb)
        p.barrier()
        p.vector.tensor_copy(c8[:, ts * 128:(ts + 1) * 128],
                             ctx.bpool[0][0:8, 0:128])
        p.barrier()

    gz = p.sem("mgz")
    p.gpsimd.op("memset", moe_acc, 0.0, inc=[(gz, 1)])
    p.barrier()
    for e in range(E):
        p.barrier()
        for tt in range(2):
            p.tensor.matmul(ctx.pv(4), sel_sb[:, e * 128:(e + 1) * 128],
                            c8[:, tt * 512:(tt + 1) * 512], start=True, stop=True)
            p.barrier()
            p.vector.tensor_copy(cbc[:, tt * 512:(tt + 1) * 512], ctx.pv(4))
            p.barrier()
        for it in range(NI):
            b = it % 2
            p.gpsimd.op("dma_start", out=wga[b],
                        in_=wgT[e][:, it * 128:(it + 1) * 128].rearrange(
                            "(c p) f -> p c f", p=128), inc=[(sW[b], 16)])
            p.gpsimd.op("dma_start", out=wua[b],
                        in_=wuT[e][:, it * 128:(it + 1) * 128].rearrange(
                            "(c p) f -> p c f", p=128), inc=[(sW[b], 16)])
            p.tensor.wait(sW[b])
            p.barrier()
            for tt in range(2):
                for c in range(DCH):
                    rsl = slice(c * T + tt * 512, c * T + tt * 512 + 512)
                    p.tensor.matmul(ctx.pv(0 + tt), wga[b][:, c * 128:(c + 1) * 128],
                                    h2n_bf[:, rsl], start=(c == 0),
                                    stop=(c == DCH - 1))
                for c in range(DCH):
                    rsl = slice(c * T + tt * 512, c * T + tt * 512 + 512)
                    p.tensor.matmul(ctx.pv(2 + tt), wua[b][:, c * 128:(c + 1) * 128],
                                    h2n_bf[:, rsl], start=(c == 0),
                                    stop=(c == DCH - 1))
            p.barrier()
            for tt in range(2):
                p.scalar.activation(s1, ctx.pv(0 + tt), AF.Silu)
                p.barrier()
                p.vector.tensor_mul(eus, ctx.pv(2 + tt),
                                    cbc[:, tt * 512:(tt + 1) * 512])
                p.barrier()
                p.vector.tensor_mul(mid[:, it * T + tt * 512: it * T + tt * 512 + 512],
                                    s1, eus)
                p.barrier()
        for ot in range(DCH):
            b = ot % 2
            p.gpsimd.op("dma_start", out=wda[b],
                        in_=wdT[e][:, ot * 128:(ot + 1) * 128].rearrange(
                            "(i p) f -> p i f", p=128), inc=[(sWd[b], 16)])
            p.tensor.wait(sWd[b])
            p.barrier()
            for tt in range(2):
                for it in range(NI):
                    rsl = slice(it * T + tt * 512, it * T + tt * 512 + 512)
                    p.tensor.matmul(ctx.pv(4 + tt), wda[b][:, it * 128:(it + 1) * 128],
                                    mid[:, rsl], start=(it == 0),
                                    stop=(it == NI - 1))
            p.barrier()
            for tt in range(2):
                asl = slice(ot * T + tt * 512, ot * T + tt * 512 + 512)
                p.vector.scalar_tensor_tensor(moe_acc[:, asl], ctx.pv(4 + tt),
                                              1.0, moe_acc[:, asl],
                                              ALU.mult, ALU.add)
            p.barrier()
    p.barrier()
    if debug:
        p.sync.op("dma_start", out=dbg_lg, in_=lgall, inc=[(sO, 16)])
        p.sync.op("dma_start", out=dbg_c8, in_=c8, inc=[(sO, 16)])
        for c in range(DCH):
            p.sync.op("dma_start", out=dbg_acc[c * 128:(c + 1) * 128, :],
                      in_=moe_acc[:, c * T:(c + 1) * T], inc=[(sO, 16)])
        p.gpsimd.op("dma_start", out=dbg_h2n.rearrange("(c p) t -> p c t", p=128),
                    in_=h2n_bf, inc=[(sO, 16)])
        p.sync.wait(sO)
        p.barrier()
    for c in range(DCH):
        b = c % 2
        p.sync.op("dma_start", out=hin[b], in_=xsh[c * 128:(c + 1) * 128, :],
                  inc=[(sX, 16)])
        p.vector.wait(sX)
        if c >= 1:
            p.vector.wait(sO)
        p.vector.scalar_tensor_tensor(h2n_f, moe_acc[:, c * T:(c + 1) * T], 0.5,
                                      hin[b], ALU.mult, ALU.add)
        p.barrier()
        p.sync.op("dma_start", out=xout[c * 128:(c + 1) * 128, :], in_=h2n_f,
                  inc=[(sO, 16)])
    p.sync.wait(sO)
    p.gpsimd.wait(sO)
    return ctx




def build_a1a(ctx=None):
    if ctx is None:
        ctx = Ctx("a1a", bf16_banks=2)
    p = ctx.p
    T = TPC
    xsh = ctx.din("xsh", [DM, T], F32)
    Ws = ctx.din("Ws", [DM, 256], F32)
    Wa = ctx.din("Wa", [DM, 256], F32)
    IDT = ctx.din("IDT", [128, 128], BF16)
    sdm_out = ctx.dout("sdm_out", [256, T], BF16)
    adm_out = ctx.dout("adm_out", [256, T], BF16)
    htok_out = ctx.dout("htok_out", [T, DM], BF16)
    amax_out = ctx.dout("amax_out", [1, 1], F32)

    ctx.alloc_consts()
    h_bf = ctx.sb("h_bf", [128, DCH * T], BF16)
    s_bf = ctx.sb("s_bf", [128, 2 * T], BF16)
    a_bf = ctx.sb("a_bf", [128, 2 * T], BF16)
    htk = ctx.sb("htk", [128, DM], BF16)
    idt_sb = ctx.sb("idt_sb", [128, 128], BF16)
    am_sb = ctx.sb("am_sb", [1, 1], F32)
    an_row = ctx.scratch("sn_mean", [1, T], F32)
    bsx = ctx.sb("bsx", [128, 512], BF16)

    sL = p.sem("aL")
    sO = p.sem("aO")

    p.gpsimd.op("dma_start", out=idt_sb, in_=IDT, inc=[(sL, 16)])
    p.gpsimd.wait(sL)
    emit_stream_norm(ctx, xsh, h_bf, T)
    emit_proj(ctx, h_bf, Ws, s_bf, 256, T, "s")
    emit_proj(ctx, h_bf, Wa, a_bf, 256, T, "a")
    for r in range(2):
        p.sync.op("dma_start", out=sdm_out[r * 128:(r + 1) * 128, :],
                  in_=s_bf[:, r * T:(r + 1) * T], inc=[(sO, 16)])
        p.sync.op("dma_start", out=adm_out[r * 128:(r + 1) * 128, :],
                  in_=a_bf[:, r * T:(r + 1) * T], inc=[(sO, 16)])

    p.barrier()
    for tt in range(2):
        for r in range(2):
            p.barrier()
            p.scalar.activation(bsx, a_bf[:, r * T + tt * 512:
                                          r * T + tt * 512 + 512], AF.Square)
            p.barrier()
            p.tensor.matmul(ctx.prow(0)[:, tt * 512:(tt + 1) * 512],
                            ctx.ones_col_bf, bsx, start=(r == 0), stop=(r == 1))
    p.barrier()
    p.vector.tensor_copy(an_row, ctx.prow(0))
    p.barrier()
    p.vector.reduce_max(am_sb, an_row, axis=AX.X)
    p.barrier()
    p.sync.op("dma_start", out=amax_out, in_=am_sb, inc=[(sO, 16)])

    ntr = 0
    sTT = p.sem("att")
    sTV = p.sem("atv")
    p.barrier()
    for tt in range(8):
        if tt >= 1:
            p.tensor.wait(sO)
        for c in range(DCH):
            half = ntr % 2
            bview = ctx.bpool[0][:, half * 1024: half * 1024 + 128]
            if ntr >= 2:
                p.tensor.wait(sTV, ntr - 1)
            p.tensor.op("transpose", bview,
                        h_bf[:, c * T + tt * 128: c * T + (tt + 1) * 128],
                        idt_sb, inc=[(sTT, 1)])
            ntr += 1
            p.vector.wait(sTT, ntr)
            if tt >= 1 and c == 0:
                p.vector.wait(sO)
            p.vector.op("tensor_copy", htk[:, c * 128:(c + 1) * 128],
                        bview, inc=[(sTV, 1)])
        p.sync.wait(sTV, ntr)
        p.sync.op("dma_start", out=htok_out[tt * 128:(tt + 1) * 128, :],
                  in_=htk, inc=[(sO, 16)])
    p.sync.wait(sO)
    p.gpsimd.wait(sO)
    return ctx


def build_a1b(ctx=None):
    if ctx is None:
        ctx = Ctx("a1b", bf16_banks=2)
    p = ctx.p
    T = TPC
    SC16 = 1.0 / 16.0
    xsh = ctx.din("xsh", [DM, T], F32)
    sdm = ctx.din("sdm", [256, T], BF16)
    aall = ctx.din("aall", [NCORES, 256, T], BF16)
    hall = ctx.din("hall", [NCORES, T, DM], BF16)
    aown = ctx.din("aown", [256, T], BF16)
    hown = ctx.din("hown", [T, DM], BF16)
    amx = ctx.din("amx", [1, 8], F32)
    zsel = ctx.din("zsel", [1, 32], F32)
    DMASK = ctx.din("DMASK", [4, 128, 512], BF16)
    WoT = ctx.din("WoT", [DM, DM], F32)
    bo = ctx.din("bo", [128, DCH], F32)
    xout = ctx.dout("xout", [DM, T], F32)

    ctx.alloc_consts()
    s_sb = ctx.sb("s_sb", [128, 2 * T], BF16)
    OC = ctx.sb("OC", [128, DCH * T], BF16)
    ao_sb = ctx.sb("ao_sb", [128, 2 * T], BF16)
    ho_sb = ctx.sb("ho_sb", [128, 8 * DM], BF16)
    kvr = [ctx.sb("kvr", [128, 2 * 512 + 4 * DM], BF16) for _ in range(2)]
    zs_sb = ctx.sb("zs_sb", [1, 32], F32)
    am_sb = ctx.sb("am_sb", [1, 8], F32)
    ar_sb = ctx.sb("ar_sb", [1, 1], F32)
    dmask_sb = ctx.sb("dmask_sb", [128, 4 * 512], BF16)
    bsx = ctx.sb("bsx", [128, 512], BF16)
    row1 = ctx.sb("row1", [1, 512], F32)
    negm = ctx.sb("negm", [1, 512], BF16)
    besb = ctx.sb("besb", [1, 2 * 512], BF16)
    Pbuf = ctx.sb("Pbuf", [128, 3 * 512], BF16)
    linv = ctx.sb("linv", [1, 512], F32)
    linv_bc = ctx.sb("linv_bc", [128, 512], F32)
    t1f = ctx.sb("t1f", [128, 512], F32)

    sL = p.sem("cL")
    sO = p.sem("cO")

    for r in range(2):
        p.sync.op("dma_start", out=s_sb[:, r * T:(r + 1) * T],
                  in_=sdm[r * 128:(r + 1) * 128, :], inc=[(sL, 16)])
        p.sync.op("dma_start", out=ao_sb[:, r * T:(r + 1) * T],
                  in_=aown[r * 128:(r + 1) * 128, :], inc=[(sL, 16)])
    p.sync.op("dma_start", out=ho_sb,
              in_=hown.rearrange("(s p) j -> p s j", p=128), inc=[(sL, 16)])
    p.sync.op("dma_start", out=zs_sb, in_=zsel, inc=[(sL, 16)])
    p.sync.op("dma_start", out=am_sb, in_=amx, inc=[(sL, 16)])
    p.sync.op("dma_start", out=dmask_sb,
              in_=DMASK.rearrange("s p t -> p s t"), inc=[(sL, 16)])
    p.sync.wait(sL)
    p.barrier()
    p.vector.reduce_max(ar_sb, am_sb, axis=AX.X)
    p.barrier()
    p.scalar.activation(ar_sb, ar_sb, AF.Ln, bias=ctx.cbias(1e-12, 1), scale=1.0)
    p.scalar.activation(ar_sb, ar_sb, AF.Exp, scale=0.5)
    p.barrier()

    JG = [[0, 1, 2], [3, 4, 5], [6, 7, 8], [9, 10, 11], [12, 13, 14], [15]]
    nQK = p.sem("cQK")
    nEX = p.sem("cEX")
    nMK = p.sem("cMK")
    nAV = p.sem("cAV")
    nBE = p.sem("cBE")
    nKV = p.sem("cKV")

    for slot in range(2):
        nkc = 8 if slot == 0 else 16
        p.barrier()
        for r in range(2):
            p.barrier()
            p.scalar.activation(bsx, s_sb[:, r * T + slot * 512:
                                          r * T + slot * 512 + 512], AF.Square)
            p.barrier()
            p.tensor.matmul(ctx.pv(4)[0:1, :], ctx.ones_col_bf, bsx,
                            start=(r == 0), stop=(r == 1))
        p.barrier()
        p.scalar.activation(row1, ctx.pv(4)[0:1, :], AF.Ln,
                            bias=ctx.cbias(1e-12, 1), scale=1.0)
        p.scalar.activation(row1, row1, AF.Exp, scale=0.5)
        p.barrier()
        p.vector.tensor_single_scalar(row1, row1, ar_sb[:, 0:1], ALU.mult)
        p.vector.tensor_scalar_mul(negm, row1, -1.0)
        p.barrier()

        for gi, grp in enumerate(JG):
            tiles = []
            for sub in range(4):
                tiles.append(("diag", sub, None))
            for kc in range(nkc):
                for sub in range(4):
                    tiles.append(("main", sub, kc))
            ntile = len(tiles)
            qk0 = nQK.rel
            ex0 = nEX.rel
            av0 = nAV.rel
            be0 = nBE.rel
            kv0 = nKV.rel
            mk0 = nMK.rel
            first = 0
            last = ntile - 1

            for kc in range(nkc):
                b = kc % 2
                own = kc if kc <= 7 else 15 - kc
                osl = 0 if kc <= 7 else 1
                if kc >= 2:
                    p.gpsimd.wait(nAV, av0 + len(grp) * 4 * (kc - 1) + len(grp) * 4)
                p.gpsimd.op("dma_start", out=kvr[b][:, 0:1024],
                            in_=aall[own][:, osl * 512:(osl + 1) * 512].rearrange(
                                "(r p) t -> p r t", p=128), inc=[(nKV, 16)])
                p.gpsimd.op("dma_start", out=kvr[b][:, 1024:1024 + 4 * DM],
                            in_=hall[own][osl * 512:(osl + 1) * 512, :].rearrange(
                                "(s p) j -> p s j", p=128), inc=[(nKV, 16)])

            p.vector.tensor_copy(besb[:, 0:512], negm)
            p.vector.op("engine_nop", inc=[(nBE, 1)])
            for idx in range(4):
                p.vector.wait(nEX, ex0 + idx + 1)
                p.vector.op("tensor_mul", Pbuf[:, (idx % 3) * 512:(idx % 3 + 1) * 512],
                            Pbuf[:, (idx % 3) * 512:(idx % 3 + 1) * 512],
                            dmask_sb[:, idx * 512:(idx + 1) * 512], inc=[(nMK, 1)])
            for kc in range(nkc):
                ring = kc % 2
                if kc == 0:
                    p.vector.wait(nQK, qk0 + 4)
                if kc >= 2:
                    p.vector.wait(nQK, qk0 + 4 * (kc + 1) - 4)
                p.vector.tensor_single_scalar(
                    besb[:, ring * 512:(ring + 1) * 512], negm,
                    zs_sb[:, slot * 16 + kc: slot * 16 + kc + 1], ALU.add)
                p.vector.op("engine_nop", inc=[(nBE, 1)])

            for idx in range(ntile):
                p.scalar.wait(nQK, qk0 + idx + 1)
                if idx >= 3:
                    p.scalar.wait(nAV, av0 + (idx - 2) * len(grp))
                p.scalar.op("activation", Pbuf[:, (idx % 3) * 512:(idx % 3 + 1) * 512],
                            ctx.pv(4 + idx % 2), AF.Exp, scale=SC16,
                            inc=[(nEX, 1)])

            for idx, t_ in enumerate(tiles):
                kind, sub, kc = t_
                if kind == "diag":
                    qlhs = [ao_sb[:, r * T + slot * 512 + sub * 128:
                                  r * T + slot * 512 + (sub + 1) * 128]
                            for r in range(2)]
                    brow = besb[:, 0:512]
                    p.tensor.wait(nBE, be0 + 1)
                else:
                    qlhs = [kvr[kc % 2][:, r * 512 + sub * 128:
                                        r * 512 + (sub + 1) * 128]
                            for r in range(2)]
                    brow = besb[:, (kc % 2) * 512:(kc % 2 + 1) * 512]
                    p.tensor.wait(nBE, be0 + 2 + kc)
                    p.tensor.wait(nKV, kv0 + 32 * (kc + 1))
                if idx >= 2:
                    p.tensor.wait(nEX, ex0 + idx - 1)
                for r in range(2):
                    p.tensor.matmul(ctx.pv(4 + idx % 2), qlhs[r],
                                    s_sb[:, r * T + slot * 512:
                                         r * T + slot * 512 + 512],
                                    start=(r == 0), stop=False)
                p.tensor.op("matmul", ctx.pv(4 + idx % 2), ctx.ones_row_bf, brow,
                            start=False, stop=True, inc=[(nQK, 1)])
                for cidx in ([idx - 1] if idx >= 1 else []) + ([idx] if idx == ntile - 1 else []):
                    ct = tiles[cidx]
                    if cidx < 4:
                        p.tensor.wait(nMK, mk0 + cidx + 1)
                    else:
                        p.tensor.wait(nEX, ex0 + cidx + 1)
                    pb = Pbuf[:, (cidx % 3) * 512:(cidx % 3 + 1) * 512]
                    st = (cidx == first)
                    sp = (cidx == last)
                    if gi == 0:
                        p.tensor.matmul(ctx.pv(3)[0:1, :], ctx.ones_col_bf, pb,
                                        start=st, stop=sp)
                    for ji, jc in enumerate(grp):
                        if ct[0] == "diag":
                            vlhs = ho_sb[:, (slot * 4 + ct[1]) * DM + jc * 128:
                                         (slot * 4 + ct[1]) * DM + (jc + 1) * 128]
                        else:
                            vlhs = kvr[ct[2] % 2][:, 1024 + ct[1] * DM + jc * 128:
                                                  1024 + ct[1] * DM + (jc + 1) * 128]
                        mm_inc = [(nAV, 1)]
                        p.tensor.op("matmul", ctx.pv(ji), vlhs, pb,
                                    start=st, stop=sp, inc=mm_inc)

            p.barrier()
            if gi == 0:
                p.vector.reciprocal(linv, ctx.pv(3)[0:1, :])
                p.barrier()
                p.tensor.matmul(ctx.pv(4), ctx.ones_row_f, linv, start=True,
                                stop=True)
                p.barrier()
                p.vector.tensor_copy(linv_bc, ctx.pv(4))
                p.barrier()
            for ji, jc in enumerate(grp):
                p.barrier()
                p.vector.tensor_mul(t1f, ctx.pv(ji), linv_bc)
                p.barrier()
                p.vector.tensor_copy(OC[:, jc * T + slot * 512:
                                        jc * T + slot * 512 + 512], t1f)
            p.barrier()

    bo_sb = ctx.sb("bo_sb", [128, DCH], F32)
    sB = p.sem("cB")
    p.gpsimd.op("dma_start", out=bo_sb, in_=bo, inc=[(sB, 16)])
    p.vector.wait(sB)
    p.barrier()
    emit_proj_resid(ctx, OC, WoT, xsh, xout, T, 1.0, sO, bias_col=bo_sb)
    p.sync.wait(sO)
    p.gpsimd.wait(sO)
    return ctx



def host_prep(I):
    f32 = lambda a: np.ascontiguousarray(np.asarray(a, np.float32))
    bfa = lambda a: np.ascontiguousarray(np.asarray(a).astype(BF))
    H = {}
    W1T = f32(I["L0_ffn_W1"]).T
    W3T = f32(I["L0_ffn_W3"]).T

    def pack_w(WT):
        w = np.ascontiguousarray(WT).reshape(DCH, 128, 64, 64, 2)
        w = np.transpose(w, (3, 0, 1, 4, 2))
        return np.ascontiguousarray(w.reshape(64, DCH, 128, 128).astype(np.float32))

    H["W1p"] = pack_w(W1T)
    H["W3p"] = pack_w(W3T)
    H["W2T"] = np.ascontiguousarray(f32(I["L0_ffn_W2"]).T)
    H["H128"] = np.ascontiguousarray(_hadamard(128).astype(np.float32))
    H64 = _hadamard(64) / 64.0
    H64B = np.zeros((128, 128), np.float64)
    H64B[:64, :64] = H64
    H64B[64:, 64:] = H64
    H["H64B"] = np.ascontiguousarray(H64B.astype(np.float32))

    PSW = np.zeros((128, 128), np.float32)
    for m in range(128):
        PSW[(m + 64) % 128, m] = 1.0
    H["PSW"] = bfa(PSW)
    s_le_t = np.tril(np.ones((128, 128), np.float32)).T
    H["CUM"] = bfa(s_le_t)
    H["SUF"] = bfa(1.0 - s_le_t)
    H["IDT"] = np.eye(128, dtype=BF)
    H["IDTF"] = np.eye(128, dtype=np.float32)
    msk = []
    for sub in range(4):
        m_ = np.zeros((128, 512), np.float32)
        for ii in range(128):
            m_[ii, sub * 128 + ii:] = 1.0
        msk.append(m_)
    H["DMASK"] = bfa(np.stack(msk))
    H["zsel"] = []
    H["selpf"] = []
    for i in range(NCORES):
        c0, c1 = CORE_CHUNKS[i]
        z = np.zeros((1, 32), np.float32)
        sel = np.zeros((128, 32), np.float32)
        for slot, cch in ((0, c0), (1, c1)):
            for kc in range(16):
                if kc >= cch:
                    z[0, slot * 16 + kc] = -1e9
            for j in range(15):
                if j < cch:
                    sel[:, slot * 16 + j] = 1.0
        H["zsel"].append(z)
        H["selpf"].append(np.ascontiguousarray(sel))

    for nm, theta in (("r0", 10000.0), ("r1", 1000000.0)):
        cos, sin = _rope_tables(SEQ, 128, theta)
        cdup = bfa(np.concatenate([cos.T, cos.T], axis=0))
        sdup = bfa(np.concatenate([-sin.T, sin.T], axis=0))
        H[f"cos_{nm}"] = [np.ascontiguousarray(cdup[:, core_tokens(i)])
                          for i in range(NCORES)]
        H[f"sin_{nm}"] = [np.ascontiguousarray(sdup[:, core_tokens(i)])
                          for i in range(NCORES)]

    H["Wq"] = f32(I["L0_attn_Wq"])
    H["Wk"] = f32(I["L0_attn_Wk"])
    H["Wv"] = f32(I["L0_attn_Wv"])
    H["Wo0"] = f32(I["L0_attn_Wo"])
    H["anw"] = np.ascontiguousarray(f32(I["L1_ffn_attn_norm_w"]).reshape(DCH, 128).T)
    H["qnw"] = f32(I["L1_ffn_q_norm_w"]).reshape(128, 1)
    H["knw"] = f32(I["L1_ffn_k_norm_w"]).reshape(128, 1)
    H["qwT"] = np.ascontiguousarray(f32(I["L1_ffn_q_w"]).T)
    H["kwT"] = np.ascontiguousarray(f32(I["L1_ffn_k_w"]).T)
    H["vwT"] = np.ascontiguousarray(f32(I["L1_ffn_v_w"]).T)
    H["owT"] = np.ascontiguousarray(f32(I["L1_ffn_o_w"]).T)
    H["fnw"] = np.ascontiguousarray(f32(I["L1_ffn_ffn_norm_w"]).reshape(DCH, 128).T)
    H["gTf"] = np.ascontiguousarray(f32(I["L1_ffn_gate_w"]).T)
    H["wgT"] = np.ascontiguousarray(np.transpose(f32(I["L1_ffn_wg"]), (0, 2, 1)))
    H["wuT"] = np.ascontiguousarray(np.transpose(f32(I["L1_ffn_wu"]), (0, 2, 1)))
    H["wdT"] = np.ascontiguousarray(np.transpose(f32(I["L1_ffn_wd"]), (0, 2, 1)))
    SEL = np.zeros((8, 8 * 128), np.float32)
    for e in range(8):
        SEL[e, e * 128:(e + 1) * 128] = 1.0
    H["SEL"] = SEL
    H["Ws1"] = f32(I["L1_attn_Ws"])
    H["Wa1"] = f32(I["L1_attn_Wa"])
    H["WoT1"] = np.ascontiguousarray(f32(I["L1_attn_Wo"]).T)
    H["bo1"] = np.ascontiguousarray(f32(I["L1_attn_bo"]).reshape(DCH, 128).T)
    return H




def _emit_gather(ctx, src2d, dst2d, name):
    p = ctx.p
    cc = p.sem("cc_" + name)
    p.barrier()
    p.gpsimd.op("collective_compute", "AllGather", ALU.bypass,
                replica_groups=[list(range(NCORES))],
                ins=[src2d], outs=[dst2d], inc=[(cc, 1)])
    p.gpsimd.wait(cc)
    p.barrier()


def build_fused():
    ctx = Ctx("fused", bf16_banks=2)
    nc, p = ctx.nc, ctx.p
    T = TPC

    EXT = {}

    def ein(name, shape, dtype=F32):
        EXT[name] = nc.dram_tensor(name, shape, dtype, kind="ExternalInput").ap()
        return EXT[name]

    xsh0 = ein("xsh", [DM, T])
    xout_e = nc.dram_tensor("xout", [DM, T], F32, kind="ExternalOutput").ap()
    for nm, shp, dt in [
        ("W2T", [SEQ, DM], F32), ("H128", [128, 128], F32),
        ("W1p", [64, DCH, 128, 128], F32), ("W3p", [64, DCH, 128, 128], F32),
        ("H64B", [128, 128], F32),
        ("Wq", [DM, DM], F32), ("Wk", [DM, 512], F32), ("Wv", [DM, 512], F32),
        ("Wo0", [DM, DM], F32),
        ("cos_r0", [128, T], BF16), ("sin_r0", [128, T], BF16),
        ("cos_r1", [128, T], BF16), ("sin_r1", [128, T], BF16),
        ("CUM", [128, 128], BF16), ("SUF", [128, 128], BF16),
        ("IDT", [128, 128], BF16), ("PSW", [128, 128], BF16),
        ("IDTF", [128, 128], F32),
        ("DMASK", [4, 128, 512], BF16), ("zsel", [1, 32], F32),
        ("selpf", [128, 32], F32),
        ("anw", [128, DCH], F32), ("qnw", [128, 1], F32), ("knw", [128, 1], F32),
        ("qwT", [DM, DM], F32), ("kwT", [DM, 512], F32), ("vwT", [DM, 512], F32),
        ("owT", [DM, DM], F32), ("fnw", [128, DCH], F32), ("gTf", [DM, 8], F32),
        ("wgT", [8, DM, 1408], F32), ("wuT", [8, DM, 1408], F32),
        ("wdT", [8, 1408, DM], F32), ("SEL", [8, 8 * 128], F32),
        ("Ws1", [DM, 256], F32), ("Wa1", [DM, 256], F32),
        ("WoT1", [DM, DM], F32), ("bo1", [128, DCH], F32),
    ]:
        ein(nm, shp, dt)

    def dint(name, shape, dtype):
        return nc.dram_tensor(name, shape, dtype, kind="Internal").ap()

    def dint_sh(name, shape, dtype):
        return nc.dram_tensor(name, shape, dtype, kind="Internal",
                              addr_space="Shared").ap()

    G2b = dint("G2b", [16, 64, 128, 128], BF16)
    xA = dint("xA", [DM, T], F32)
    xB = dint("xB", [DM, T], F32)
    hidb = dint("hidb", [DM, T], F32)
    h2b = dint("h2b", [DM, T], F32)
    sqL = dint("sqL", [DM, T], BF16)
    skL = dint("skL", [512, T], BF16)
    vtL = dint("vtL", [T, 512], BF16)
    kmL = dint("kmL", [1, 4], F32)
    oglaL = dint("oglaL", [8, 16, 128, 128], BF16)
    acL = dint("acL", [8, 16, 128, 128], BF16)
    boffL = dint("boffL", [128, 32], F32)
    DL = dint("DL", [2, 128, 4], F32)
    UL = dint("UL", [2, 4, 128, 128], F32)
    skg = dint_sh("skg", [NCORES * 512, T], BF16)
    vg = dint_sh("vg", [NCORES * T, 512], BF16)
    kmg = dint_sh("kmg", [NCORES, 4], F32)
    Dgf = dint_sh("Dgf", [NCORES * 2 * 128, 4], F32)
    Ugf = dint_sh("Ugf", [NCORES * 2 * 4 * 128, 128], F32)
    aL = dint("aL", [256, T], BF16)
    htL = dint("htL", [T, DM], BF16)
    amL = dint("amL", [1, 1], F32)
    ag = dint_sh("ag", [NCORES * 256, T], BF16)
    hg = dint_sh("hg", [NCORES * T, DM], BF16)
    amg = dint_sh("amg", [NCORES, 1], F32)

    ctx.alloc_consts()
    for v in (1e-5, 1e-6, 1e-12):
        ctx.cbias(v, 1)
    ctx.enable_arena(206 * 1024)

    kmx_view = bass.AP(kmg.tensor, 0, [[1, 4], [4, 8]])
    skg_v = skg.rearrange("(n r) t -> n r t", n=NCORES)
    vg_v = vg.rearrange("(n r) t -> n r t", n=NCORES)
    Dg_v = Dgf.rearrange("(n a b) c -> n a b c", n=NCORES, a=2)
    Ug_v = Ugf.rearrange("(n a g b) c -> n a g b c", n=NCORES, a=2, g=4)
    ag_v = ag.rearrange("(n r) t -> n r t", n=NCORES)
    hg_v = hg.rearrange("(n r) t -> n r t", n=NCORES)
    amx_view = amg.rearrange("c o -> (c o)")

    def phase(tag, builder, iomap, *args):
        ctx.phase_reset(tag)
        ctx.iomap = iomap
        builder(*args, ctx=ctx)
        ctx.iomap = None
        p.barrier()

    shared0 = {k: EXT[k] for k in ("CUM", "SUF", "IDT", "PSW", "DMASK",
                                   "zsel", "selpf", "IDTF", "SEL")}

    phase("g", build_gprep, {"W2T": EXT["W2T"], "H128": EXT["H128"], "G2": G2b})

    def f0(tag, xin, xo):
        phase(tag, build_f0, {"xsh": xin, "W1p": EXT["W1p"], "W3p": EXT["W3p"],
                              "G2": G2b, "H64B": EXT["H64B"], "xout": xo})

    def a0(tag, xin, xo):
        phase(tag + "a", build_a0a,
              {"xsh": xin, "Wq": EXT["Wq"], "Wk": EXT["Wk"], "Wv": EXT["Wv"],
               "cosd": EXT["cos_r0"], "sind": EXT["sin_r0"], **shared0,
               "sq_out": sqL, "sk_out": skL, "vtok_out": vtL,
               "ogla_out": oglaL, "acache_out": acL, "boff_out": boffL,
               "D_out": DL, "U_out": UL, "kmax_out": kmL})
        _emit_gather(ctx, skL, skg, tag + "sk")
        _emit_gather(ctx, vtL, vg, tag + "v")
        _emit_gather(ctx, kmL, kmg, tag + "km")
        _emit_gather(ctx, DL.rearrange("a b c -> (a b) c"), Dgf, tag + "D")
        _emit_gather(ctx, UL.rearrange("a g b c -> (a g b) c"), Ugf, tag + "U")
        phase(tag + "b", lambda ctx=None: build_flash("a0", ctx=ctx),
              {"xsh": xin, "sq_in": sqL, "skall": skg_v, "vall": vg_v,
               "skown": skL, "vown": vtL, "ogla": oglaL, "acache": acL,
               "boffe": boffL, "Dg": Dg_v, "Ug": Ug_v, "kmx": kmx_view,
               **shared0, "Wo": EXT["Wo0"], "xout": xo})

    def f1(tag, xin, xmid1, xmid2, xo):
        phase(tag + "a", build_f1a,
              {"xsh": xin, "anw": EXT["anw"], "qnw": EXT["qnw"],
               "knw": EXT["knw"], "qwT": EXT["qwT"], "kwT": EXT["kwT"],
               "vwT": EXT["vwT"], "cosd": EXT["cos_r1"], "sind": EXT["sin_r1"],
               **shared0, "xout": xmid1, "hidden_out": hidb, "sq_out": sqL,
               "sk_out": skL, "vtok_out": vtL, "kmax_out": kmL})
        _emit_gather(ctx, skL, skg, tag + "sk")
        _emit_gather(ctx, vtL, vg, tag + "v")
        _emit_gather(ctx, kmL, kmg, tag + "km")
        phase(tag + "b", lambda ctx=None: build_flash("f1", ctx=ctx),
              {"xsh": xmid1, "sq_in": sqL, "skall": skg_v, "vall": vg_v,
               "skown": skL, "vown": vtL, "hidden_in": hidb, "h2_out": h2b,
               "kmx": kmx_view, **shared0, "Wo": EXT["owT"], "xout": xmid2})
        phase(tag + "c", lambda ctx=None: build_moe(False, ctx=ctx),
              {"xsh": xmid2, "h2sh": h2b, "fnw": EXT["fnw"], "gTf": EXT["gTf"],
               "wgT": EXT["wgT"], "wuT": EXT["wuT"], "wdT": EXT["wdT"],
               **shared0, "xout": xo})

    def a1(tag, xin, xo):
        phase(tag + "a", build_a1a,
              {"xsh": xin, "Ws": EXT["Ws1"], "Wa": EXT["Wa1"], **shared0,
               "sdm_out": sqL[0:256, :] if False else aL,
               "adm_out": aL, "htok_out": htL, "amax_out": amL})
        pass

    sdmL = dint("sdmL", [256, T], BF16)

    def a1(tag, xin, xo):
        phase(tag + "a", build_a1a,
              {"xsh": xin, "Ws": EXT["Ws1"], "Wa": EXT["Wa1"], **shared0,
               "sdm_out": sdmL, "adm_out": aL, "htok_out": htL,
               "amax_out": amL})
        _emit_gather(ctx, aL, ag, tag + "a")
        _emit_gather(ctx, htL, hg, tag + "h")
        _emit_gather(ctx, amL, amg, tag + "am")
        phase(tag + "b", build_a1b,
              {"xsh": xin, "sdm": sdmL, "aall": ag_v, "hall": hg_v,
               "aown": aL, "hown": htL, "amx": amx_view, **shared0,
               "WoT": EXT["WoT1"], "bo": EXT["bo1"], "xout": xo})

    f0("p1", xsh0, xA)
    a0("p2", xA, xB)
    f0("p3", xB, xA)
    f1("p4", xA, xB, xA, xB)
    a1("p5", xB, xA)
    f1("p6", xA, xB, xA, xout_e)
    return ctx



_CTX_CACHE = {}


def _get(name, builder, *args):
    if name not in _CTX_CACHE:
        ctx = builder(*args)
        ctx.p.replay(ctx.nc)
        _CTX_CACHE[name] = ctx
    return _CTX_CACHE[name]


def _run(ctx, in_maps):
    return run_bass_kernel_spmd(ctx.nc, in_maps, core_ids=list(range(NCORES)))


def _run_built(name, builder, in_maps):
    ctx = _get(name, builder)
    return run_bass_kernel_spmd(ctx.nc, in_maps, core_ids=list(range(NCORES)))


def kernel(**I):
    return kernel_fused(**I)


def kernel_fused(**I):
    x = np.ascontiguousarray(np.asarray(I["L0_attn_hidden_states"],
                                        np.float32)[0])
    H = host_prep(I)
    shards = shard_x(x)
    ctx = _get("fused", build_fused)
    in_maps = []
    for i in range(NCORES):
        in_maps.append({
            "xsh": shards[i], "W2T": H["W2T"], "H128": H["H128"],
            "W1p": H["W1p"], "W3p": H["W3p"], "H64B": H["H64B"],
            "Wq": H["Wq"], "Wk": H["Wk"], "Wv": H["Wv"], "Wo0": H["Wo0"],
            "cos_r0": H["cos_r0"][i], "sin_r0": H["sin_r0"][i],
            "cos_r1": H["cos_r1"][i], "sin_r1": H["sin_r1"][i],
            "CUM": H["CUM"], "SUF": H["SUF"], "IDT": H["IDT"],
            "PSW": H["PSW"], "IDTF": H["IDTF"], "DMASK": H["DMASK"],
            "zsel": H["zsel"][i], "selpf": H["selpf"][i],
            "anw": H["anw"], "qnw": H["qnw"], "knw": H["knw"],
            "qwT": H["qwT"], "kwT": H["kwT"], "vwT": H["vwT"],
            "owT": H["owT"], "fnw": H["fnw"], "gTf": H["gTf"],
            "wgT": H["wgT"], "wuT": H["wuT"], "wdT": H["wdT"],
            "SEL": H["SEL"], "Ws1": H["Ws1"], "Wa1": H["Wa1"],
            "WoT1": H["WoT1"], "bo1": H["bo1"],
        })
    r = run_bass_kernel_spmd(ctx.nc, in_maps, core_ids=list(range(NCORES)))
    xo = unshard_x([np.asarray(r.results[i]["xout"]) for i in range(NCORES)])
    return np.ascontiguousarray(xo).reshape(1, SEQ, DM)


def kernel_multilaunch(**I):
    x = np.ascontiguousarray(np.asarray(I["L0_attn_hidden_states"],
                                        np.float32)[0])
    H = host_prep(I)
    asbf = np.asarray

    r = _run_built("gprep", build_gprep,
                   [{"W2T": H["W2T"], "H128": H["H128"]}] * NCORES)
    G2 = asbf(r.results[0]["G2"])

    def run_f0(shards):
        in_maps = [{"xsh": shards[i], "W1p": H["W1p"], "W3p": H["W3p"],
                    "G2": G2, "H64B": H["H64B"]} for i in range(NCORES)]
        r = _run_built("f0", build_f0, in_maps)
        return [asbf(r.results[i]["xout"]) for i in range(NCORES)]

    def assemble_kv(ra):
        skall = np.stack([asbf(ra.results[i]["sk_out"]) for i in range(NCORES)])
        vall = np.stack([asbf(ra.results[i]["vtok_out"]) for i in range(NCORES)])
        km = np.stack([asbf(ra.results[i]["kmax_out"])[0] for i in range(NCORES)])
        km_in = np.zeros((1, 32), np.float32)
        for g in range(4):
            km_in[0, g * 8:(g + 1) * 8] = km[:, g]
        return skall, vall, km_in

    def run_a0(shards):
        in_maps = [{"xsh": shards[i], "Wq": H["Wq"], "Wk": H["Wk"],
                    "Wv": H["Wv"], "cosd": H["cos_r0"][i], "sind": H["sin_r0"][i],
                    "CUM": H["CUM"], "SUF": H["SUF"], "IDT": H["IDT"],
                    "PSW": H["PSW"]} for i in range(NCORES)]
        ra = _run_built("a0a", build_a0a, in_maps)
        skall, vall, km_in = assemble_kv(ra)
        Dg = np.stack([asbf(ra.results[i]["D_out"]) for i in range(NCORES)])
        Ug = np.stack([asbf(ra.results[i]["U_out"]) for i in range(NCORES)])
        in_maps_b = []
        for i in range(NCORES):
            rr = ra.results[i]
            in_maps_b.append({
                "xsh": shards[i], "sq_in": asbf(rr["sq_out"]),
                "skall": skall, "vall": vall,
                "skown": asbf(rr["sk_out"]), "vown": asbf(rr["vtok_out"]),
                "ogla": asbf(rr["ogla_out"]), "acache": asbf(rr["acache_out"]),
                "boffe": asbf(rr["boff_out"]),
                "Dg": Dg, "Ug": Ug, "selpf": H["selpf"][i],
                "kmx": km_in, "zsel": H["zsel"][i], "DMASK": H["DMASK"],
                "Wo": H["Wo0"]})
        rb = _run_built("a0b", build_a0b, in_maps_b)
        return [asbf(rb.results[i]["xout"]) for i in range(NCORES)]

    def run_f1(shards):
        in_maps = [{"xsh": shards[i], "anw": H["anw"], "qnw": H["qnw"],
                    "knw": H["knw"], "qwT": H["qwT"], "kwT": H["kwT"],
                    "vwT": H["vwT"], "cosd": H["cos_r1"][i],
                    "sind": H["sin_r1"][i], "PSW": H["PSW"], "IDT": H["IDT"]}
                   for i in range(NCORES)]
        ra = _run_built("f1a", build_f1a, in_maps)
        skall, vall, km_in = assemble_kv(ra)
        in_maps_b = []
        for i in range(NCORES):
            rr = ra.results[i]
            in_maps_b.append({
                "xsh": asbf(rr["xout"]), "sq_in": asbf(rr["sq_out"]),
                "skall": skall, "vall": vall,
                "skown": asbf(rr["sk_out"]), "vown": asbf(rr["vtok_out"]),
                "hidden_in": asbf(rr["hidden_out"]),
                "kmx": km_in, "zsel": H["zsel"][i], "DMASK": H["DMASK"],
                "Wo": H["owT"]})
        rb = _run_built("f1flash", lambda: build_flash("f1"), in_maps_b)
        in_maps_c = [{"xsh": asbf(rb.results[i]["xout"]),
                      "h2sh": asbf(rb.results[i]["h2_out"]),
                      "fnw": H["fnw"], "gTf": H["gTf"], "wgT": H["wgT"],
                      "wuT": H["wuT"], "wdT": H["wdT"], "IDTF": H["IDTF"],
                      "SEL": H["SEL"]} for i in range(NCORES)]
        rc = _run_built("moe", build_moe, in_maps_c)
        return [asbf(rc.results[i]["xout"]) for i in range(NCORES)]

    def run_a1(shards):
        in_maps = [{"xsh": shards[i], "Ws": H["Ws1"], "Wa": H["Wa1"],
                    "IDT": H["IDT"]} for i in range(NCORES)]
        ra = _run_built("a1a", build_a1a, in_maps)
        aall = np.stack([asbf(ra.results[i]["adm_out"]) for i in range(NCORES)])
        hall = np.stack([asbf(ra.results[i]["htok_out"]) for i in range(NCORES)])
        amx = np.zeros((1, 8), np.float32)
        for i in range(NCORES):
            amx[0, i] = asbf(ra.results[i]["amax_out"])[0, 0]
        in_maps_b = []
        for i in range(NCORES):
            rr = ra.results[i]
            in_maps_b.append({
                "xsh": shards[i], "sdm": asbf(rr["sdm_out"]),
                "aall": aall, "hall": hall,
                "aown": asbf(rr["adm_out"]), "hown": asbf(rr["htok_out"]),
                "amx": amx, "zsel": H["zsel"][i], "DMASK": H["DMASK"],
                "WoT": H["WoT1"], "bo": H["bo1"]})
        rb = _run_built("a1b", build_a1b, in_maps_b)
        return [asbf(rb.results[i]["xout"]) for i in range(NCORES)]

    sh = shard_x(x)
    sh = run_f0(sh)
    sh = run_a0(sh)
    sh = run_f0(sh)
    sh = run_f1(sh)
    sh = run_a1(sh)
    sh = run_f1(sh)
    x = unshard_x(sh)
    return np.ascontiguousarray(x).reshape(1, SEQ, DM)


# version 06
# speedup vs baseline: 1.1126x; 1.1126x over previous
import math
from contextlib import ExitStack

import numpy as np
import ml_dtypes

import concourse.bass as bass
import concourse.mybir as mybir
from concourse.bass_utils import run_bass_kernel_spmd

F32 = mybir.dt.float32
BF16 = mybir.dt.bfloat16
AF = mybir.ActivationFunctionType
ALU = mybir.AluOpType
AX = mybir.AxisListType

NCORES = 8
SEQ = 8192
DM = 2048
DCH = DM // 128
NCH = 16
CHT = SEQ // NCH
TPC = 2 * CHT
CORE_CHUNKS = [(i, NCH - 1 - i) for i in range(NCORES)]

BF = ml_dtypes.bfloat16



def _hadamard(n):
    h = np.array([[1.0]], dtype=np.float64)
    while h.shape[0] < n:
        h = np.block([[h, h], [h, -h]])
    return h


def _rope_tables(n, d, theta):
    inv = 1.0 / theta ** (np.arange(0, d, 2, dtype=np.float64) / d)
    f = np.arange(n, dtype=np.float64)[:, None] * inv[None, :]
    return np.cos(f), np.sin(f)


def core_tokens(core):
    a, b = CORE_CHUNKS[core]
    return np.r_[a * CHT:(a + 1) * CHT, b * CHT:(b + 1) * CHT]


def shard_x(x):
    return [np.ascontiguousarray(x[core_tokens(i)].T) for i in range(NCORES)]


def unshard_x(shards):
    x = np.empty((SEQ, DM), np.float32)
    for i in range(NCORES):
        x[core_tokens(i)] = np.asarray(shards[i]).T
    return x




class SemC:
    __slots__ = ("name", "n", "h", "base")

    def __init__(self, name):
        self.name = name
        self.n = 0
        self.base = 0
        self.h = None

    @property
    def rel(self):
        return self.n - self.base


class _Eng:
    def __init__(self, prog, name):
        self._p = prog
        self._n = name

    def op(self, meth, *args, inc=None, **kw):
        if inc:
            for s, v in inc:
                s.n += v

        def closure(e, _m=meth, _a=args, _k=kw, _i=inc):
            r = getattr(e, _m)(*_a, **_k)
            if _i:
                for s, v in _i:
                    r.then_inc(s.h, v)
            return r

        self._p.ops.append((self._n, closure))

    def wait(self, sem, val=None):
        v = sem.n if val is None else (sem.base + val)
        self._p.ops.append((self._n, lambda e, _s=sem, _v=v: e.wait_ge(_s.h, _v)))

    def __getattr__(self, meth):
        def call(*args, inc=None, **kw):
            self.op(meth, *args, inc=inc, **kw)
        return call


class Prog:
    def __init__(self):
        self.ops = []
        self.sems = {}
        self.prefix = ""
        self.sync = _Eng(self, "sync")
        self.scalar = _Eng(self, "scalar")
        self.vector = _Eng(self, "vector")
        self.tensor = _Eng(self, "tensor")
        self.gpsimd = _Eng(self, "gpsimd")

    def sem(self, name):
        if name not in self.sems:
            self.sems[name] = SemC(name)
        return self.sems[name]

    def barrier(self):
        self.ops.append(("BARRIER", None))

    def replay(self, nc):
        segs = [[]]
        for item in self.ops:
            if item[0] == "BARRIER":
                if segs[-1]:
                    segs.append([])
            else:
                segs[-1].append(item)
        if segs and not segs[-1]:
            segs.pop()
        with ExitStack() as st:
            for s in self.sems.values():
                s.h = st.enter_context(nc.semaphore(s.name))
            for seg in segs:
                per = {}
                for eng, fn in seg:
                    per.setdefault(eng, []).append(fn)
                with nc.Block() as blk:
                    for eng, lst in per.items():
                        def make(lst_):
                            def body(e):
                                for f in lst_:
                                    f(e)
                            return body
                        getattr(blk, eng)(make(lst))


class Ctx:

    def __init__(self, name, bf16_banks=0):
        self.nc = bass.Bass("TRN2", target_bir_lowering=False)
        self.p = Prog()
        self.uid = 0
        self.name = name
        self._scratch = {}
        nf = 4 - (bf16_banks + 1) // 2
        self.pool = [self.nc.alloc_psum_tensor(f"pool{i}", [128, 1024], F32).ap()
                     for i in range(nf)]
        self.bpool = [self.nc.alloc_psum_tensor(f"bpool{i}", [128, 2048], BF16).ap()
                      for i in range((bf16_banks + 1) // 2)]

    def pv(self, bank, width=512):
        t = self.pool[bank // 2]
        off = (bank % 2) * 512
        return t[:, off:off + width]

    def prow(self, i):
        return self.pool[i][0:1, 0:1024]

    def pfull(self, i):
        return self.pool[i]

    def nid(self, base):
        self.uid += 1
        return f"{base}_{self.uid}"

    def enable_arena(self, nbytes):
        res = self.nc.bump_sbuf(nbytes)
        assert res is not None, "no SBUF left for arena"
        self.arena_base = res[0]
        self.arena_bytes = nbytes
        self.arena_mode = True
        self.aoff = 0
        self.amax_seen = 0

    def phase_reset(self, tag):
        self._scratch = {}
        self.aoff = 0
        for s in self.p.sems.values():
            s.base = s.n
        if hasattr(self, "xsq_buf"):
            del self.xsq_buf

    def sb(self, name, shape, dtype):
        if not getattr(self, "arena_mode", False):
            return self.nc.alloc_sbuf_tensor(self.nid(name), shape, dtype).ap()
        n = int(np.prod(shape[1:]))
        esz = 2 if dtype == BF16 else 4
        nb = (n * esz + 31) & ~31
        off = self.aoff
        assert off + n * esz <= self.arena_bytes, (
            f"arena overflow {name} {shape} {dtype} at {off}")
        self.aoff = off + nb
        self.amax_seen = max(self.amax_seen, self.aoff)
        t = self.nc.alloc_sbuf_tensor_at(self.nid("ar_" + name),
                                         [shape[0], n], dtype,
                                         offset=self.arena_base + off)
        return t.ap()

    def scratch(self, name, shape, dtype):
        key = (name, tuple(shape), dtype)
        if key not in self._scratch:
            self._scratch[key] = self.sb(name, shape, dtype)
        return self._scratch[key]

    def din(self, name, shape, dtype=F32):
        if getattr(self, "iomap", None) is not None:
            return self.iomap[name]
        return self.nc.dram_tensor(name, shape, dtype, kind="ExternalInput").ap()

    def dout(self, name, shape, dtype=F32):
        if getattr(self, "iomap", None) is not None:
            return self.iomap[name]
        return self.nc.dram_tensor(name, shape, dtype, kind="ExternalOutput").ap()

    def dint(self, name, shape, dtype=F32):
        return self.nc.dram_tensor(self.nid(name), shape, dtype, kind="Internal").ap()

    def alloc_consts(self):
        g = self.p.gpsimd
        self.ones_col_bf = self.sb("ones_col_bf", [128, 1], BF16)
        self.ones_row_bf = self.sb("ones_row_bf", [1, 128], BF16)
        self.ones_row_f = self.sb("ones_row_f", [1, 128], F32)
        g.memset(self.ones_col_bf, 1.0)
        g.memset(self.ones_row_bf, 1.0)
        g.memset(self.ones_row_f, 1.0)
        self._cbias = {}

    def cbias(self, value, parts=128):
        key = (float(value), parts)
        if key not in self._cbias:
            t = self.sb("cbias", [parts, 1], F32)
            self.p.gpsimd.memset(t, float(value))
            self._cbias[key] = t
        return self._cbias[key]



def emit_norm_fm(ctx, x_sb, h_out, T, rms_only=False, weight_col=None,
                 eps=1e-5):
    p = ctx.p
    nt = T // 512
    xbf = ctx.scratch("sn_xbf", [128, DCH * T], BF16) if not rms_only else None
    xsq = getattr(ctx, "xsq_buf", None)
    if xsq is None:
        xsq = ctx.scratch("sn_xsq2", [128, DCH * T], BF16)
    mean_r = ctx.scratch("sn_mean", [1, T], F32)
    rstd_r = ctx.scratch("sn_rstd", [1, T], F32)
    var_r = ctx.scratch("sn_var", [1, T], F32)
    tmp = ctx.scratch("sn_tmp", [128, T], F32)
    p.barrier()
    for c in range(DCH):
        sl = slice(c * T, (c + 1) * T)
        if not rms_only:
            p.vector.tensor_copy(xbf[:, sl], x_sb[:, sl])
        p.scalar.activation(xsq[:, sl], x_sb[:, sl], AF.Square)
    p.barrier()
    for c in range(DCH):
        for tt in range(nt):
            tsl = slice(c * T + tt * 512, c * T + tt * 512 + 512)
            osl = slice(tt * 512, tt * 512 + 512)
            if not rms_only:
                p.tensor.matmul(ctx.prow(0)[:, osl], ctx.ones_col_bf, xbf[:, tsl],
                                start=(c == 0), stop=(c == DCH - 1))
            p.tensor.matmul(ctx.prow(1)[:, osl], ctx.ones_col_bf, xsq[:, tsl],
                            start=(c == 0), stop=(c == DCH - 1))
    p.barrier()
    if rms_only:
        p.vector.tensor_scalar_mul(var_r, ctx.prow(1), 1.0 / DM)
    else:
        p.vector.tensor_scalar_mul(mean_r, ctx.prow(0), 1.0 / DM)
        p.vector.tensor_scalar_mul(var_r, ctx.prow(1), 1.0 / DM)
        p.vector.tensor_mul(rstd_r, mean_r, mean_r)
        p.vector.tensor_sub(var_r, var_r, rstd_r)
    p.barrier()
    p.scalar.activation(rstd_r, var_r, AF.Ln, bias=ctx.cbias(eps, 1), scale=1.0)
    p.scalar.activation(rstd_r, rstd_r, AF.Exp, scale=-0.5)
    p.barrier()
    for tt in range(nt):
        osl = slice(tt * 512, tt * 512 + 512)
        if not rms_only:
            p.tensor.matmul(ctx.pfull(0)[:, osl], ctx.ones_row_f, mean_r[:, osl],
                            start=True, stop=True)
        p.tensor.matmul(ctx.pfull(1)[:, osl], ctx.ones_row_f, rstd_r[:, osl],
                        start=True, stop=True)
    p.barrier()
    for c in range(DCH):
        sl = slice(c * T, (c + 1) * T)
        if rms_only:
            p.vector.tensor_mul(tmp, x_sb[:, sl], ctx.pfull(1))
        else:
            p.vector.tensor_sub(tmp, x_sb[:, sl], ctx.pfull(0))
            p.vector.tensor_mul(tmp, tmp, ctx.pfull(1))
        if weight_col is not None:
            p.vector.tensor_single_scalar(tmp, tmp, weight_col[:, c:c + 1],
                                          ALU.mult)
        p.vector.tensor_copy(h_out[:, sl], tmp)
    p.barrier()


def emit_load_x(ctx, xsh, x_sb, T):
    p = ctx.p
    sX = p.sem(ctx.nid("sx"))
    for c in range(DCH):
        p.sync.op("dma_start", out=x_sb[:, c * T:(c + 1) * T],
                  in_=xsh[c * 128:(c + 1) * 128, :], inc=[(sX, 16)])
    p.sync.wait(sX)
    p.barrier()


def emit_store_x(ctx, x_sb, xout, T, sO):
    p = ctx.p
    p.barrier()
    for c in range(DCH):
        p.sync.op("dma_start", out=xout[c * 128:(c + 1) * 128, :],
                  in_=x_sb[:, c * T:(c + 1) * T], inc=[(sO, 16)])
    p.sync.wait(sO)



def build_gprep(ctx=None):
    if ctx is None:
        ctx = Ctx("gprep")
    p = ctx.p
    W2T = ctx.din("W2T", [SEQ, DM], F32)
    H128 = ctx.din("H128", [128, 128], F32)
    G2 = ctx.dout("G2", [16, 64, 128, 128], BF16)

    h_sb = ctx.sb("h128", [128, 128], BF16)
    w_sb = [ctx.sb("w2c", [128, DM], BF16) for _ in range(2)]
    g_sb = [ctx.sb("gc", [128, DM], BF16) for _ in range(2)]

    sL = p.sem("ld")
    sW = [p.sem("w0"), p.sem("w1")]
    sM = p.sem("mm")
    sV = p.sem("vv")
    sO = p.sem("st")

    p.gpsimd.op("dma_start", out=h_sb, in_=H128, inc=[(sL, 16)])
    p.tensor.wait(sL)

    n_mm = 0
    n_vv = 0
    for ap_i in range(64):
        b = ap_i % 2
        if ap_i >= 2:
            p.gpsimd.wait(sM, (ap_i - 1) * 4)
        p.gpsimd.op("dma_start", out=w_sb[b],
                    in_=W2T[ap_i * 128:(ap_i + 1) * 128, :], inc=[(sW[b], 16)])
        p.tensor.wait(sW[b])
        for ot in range(4):
            p.tensor.op("matmul", ctx.pv(ot), h_sb,
                        w_sb[b][:, ot * 512:(ot + 1) * 512],
                        start=True, stop=True, inc=[(sM, 1)])
            n_mm += 1
        p.vector.wait(sM, n_mm)
        if ap_i >= 2:
            p.vector.wait(sO, 16 * (ap_i - 1))
        for ot in range(4):
            p.vector.op("tensor_copy", g_sb[b][:, ot * 512:(ot + 1) * 512],
                        ctx.pv(ot), inc=[(sV, 1)])
            n_vv += 1
        p.gpsimd.wait(sV, n_vv)
        out_ap = bass.AP(G2.tensor, ap_i * 128,
                         [[16384, 64], [8192, 2], [1048576, 16], [1, 128]])
        p.gpsimd.op("dma_start", out=out_ap, in_=g_sb[b], inc=[(sO, 16)])
    p.gpsimd.wait(sO)
    return ctx



def build_f0(ctx=None):
    if ctx is None:
        ctx = Ctx("f0", bf16_banks=2)
    p = ctx.p
    T = TPC
    nt = T // 512
    xsh = ctx.din("xsh", [DM, T], F32)
    W1p = ctx.din("W1p", [64, DCH, 128, 128], F32)
    W3p = ctx.din("W3p", [64, DCH, 128, 128], F32)
    G2 = ctx.din("G2", [16, 64, 128, 128], BF16)
    H64B = ctx.din("H64B", [128, 128], F32)
    xout = ctx.dout("xout", [DM, T], F32)

    ctx.alloc_consts()
    h64 = ctx.sb("h64", [128, 128], BF16)
    sA = ctx.sb("sA", [128, 64 * T], BF16)
    xsq = sA[:, 0:DCH * T]
    h_bf = ctx.sb("h_bf", [128, DCH * T], BF16)
    w1sb = [ctx.sb("w1sb", [128, DCH * 128], BF16) for _ in range(2)]
    w3sb = [ctx.sb("w3sb", [128, DCH * 128], BF16) for _ in range(2)]
    gsb = [ctx.sb("gsb", [128, 8 * 128], BF16) for _ in range(2)]
    s1 = ctx.sb("s1", [128, T], BF16)
    mid = s1
    xin = [ctx.sb("xin", [128, T], F32) for _ in range(2)]
    mean_r = ctx.sb("f0_mean", [1, T], F32)
    rstd_r = ctx.sb("f0_rstd", [1, T], F32)
    tmp = ctx.sb("f0_tmp", [128, T], F32)
    var_r = tmp[0:1, 0:T]
    xo_sb = tmp

    sX = p.sem("sxl")
    sW = [p.sem("sw0"), p.sem("sw1")]
    sG2w = [p.sem("sg0"), p.sem("sg1")]
    sM = p.sem("sm")
    sSc = p.sem("ssc")
    sV = p.sem("sv")
    sM2 = p.sem("sm2")
    sV2 = p.sem("sv2")
    sO = p.sem("so")
    sH = p.sem("sh")
    sLNv = p.sem("slnv")

    ctx.p.gpsimd.op("dma_start", out=h64, in_=H64B, inc=[(sH, 16)])

    for c in range(DCH):
        b = c % 2
        if c >= 2:
            p.sync.wait(sLNv, 2 * (c - 1))
        p.sync.op("dma_start", out=xin[b], in_=xsh[c * 128:(c + 1) * 128, :],
                  inc=[(sX, 16)])
        sl = slice(c * T, (c + 1) * T)
        p.vector.wait(sX, 16 * (c + 1))
        p.vector.op("tensor_copy", h_bf[:, sl], xin[b], inc=[(sLNv, 1)])
        p.scalar.wait(sX, 16 * (c + 1))
        p.scalar.op("activation", xsq[:, sl], xin[b], AF.Square, inc=[(sLNv, 1)])
    p.barrier()
    for c in range(DCH):
        for tt in range(nt):
            tsl = slice(c * T + tt * 512, c * T + tt * 512 + 512)
            osl = slice(tt * 512, tt * 512 + 512)
            p.tensor.matmul(ctx.prow(0)[:, osl], ctx.ones_col_bf, h_bf[:, tsl],
                            start=(c == 0), stop=(c == DCH - 1))
            p.tensor.matmul(ctx.prow(1)[:, osl], ctx.ones_col_bf, xsq[:, tsl],
                            start=(c == 0), stop=(c == DCH - 1))
    p.barrier()
    p.vector.tensor_scalar_mul(mean_r, ctx.prow(0), 1.0 / DM)
    p.vector.tensor_scalar_mul(var_r, ctx.prow(1), 1.0 / DM)
    p.vector.tensor_mul(rstd_r, mean_r, mean_r)
    p.vector.tensor_sub(var_r, var_r, rstd_r)
    p.barrier()
    p.scalar.activation(rstd_r, var_r, AF.Ln, bias=ctx.cbias(1e-5, 1), scale=1.0)
    p.scalar.activation(rstd_r, rstd_r, AF.Exp, scale=-0.5)
    p.barrier()
    for tt in range(nt):
        osl = slice(tt * 512, tt * 512 + 512)
        p.tensor.matmul(ctx.pfull(0)[:, osl], ctx.ones_row_f, mean_r[:, osl],
                        start=True, stop=True)
        p.tensor.matmul(ctx.pfull(1)[:, osl], ctx.ones_row_f, rstd_r[:, osl],
                        start=True, stop=True)
    p.barrier()
    for c in range(DCH):
        sl = slice(c * T, (c + 1) * T)
        p.vector.tensor_sub(tmp, h_bf[:, sl], ctx.pfull(0))
        p.vector.tensor_mul(tmp, tmp, ctx.pfull(1))
        p.vector.tensor_copy(h_bf[:, sl], tmp)
    p.barrier()
    p.tensor.wait(sH)

    GRP = 3 * nt
    nsc = 0
    nv = 0
    for jp in range(64):
        b = jp % 2
        if jp >= 2:
            p.gpsimd.wait(sM, (jp - 1) * GRP)
        p.gpsimd.op("dma_start", out=w1sb[b],
                    in_=W1p[jp].rearrange("c p f -> p c f"), inc=[(sW[b], 16)])
        p.gpsimd.op("dma_start", out=w3sb[b],
                    in_=W3p[jp].rearrange("c p f -> p c f"), inc=[(sW[b], 16)])
        p.tensor.wait(sW[b])
        if jp >= 1:
            p.tensor.wait(sSc, jp * nt)
            p.tensor.wait(sV, jp * 2 * nt)
        for tt in range(nt):
            for c in range(DCH):
                rsl = slice(c * T + tt * 512, c * T + tt * 512 + 512)
                p.tensor.op("matmul", ctx.pv(0 + tt), w1sb[b][:, c * 128:(c + 1) * 128],
                            h_bf[:, rsl], start=(c == 0), stop=(c == DCH - 1),
                            inc=[(sM, 1)] if c == DCH - 1 else None)
            for c in range(DCH):
                rsl = slice(c * T + tt * 512, c * T + tt * 512 + 512)
                p.tensor.op("matmul", ctx.pv(2 + tt), w3sb[b][:, c * 128:(c + 1) * 128],
                            h_bf[:, rsl], start=(c == 0), stop=(c == DCH - 1),
                            inc=[(sM, 1)] if c == DCH - 1 else None)
        p.scalar.wait(sM, jp * GRP + 2 * nt)
        if jp >= 1:
            p.scalar.wait(sV, (jp - 1) * 2 * nt + nt)
        for tt in range(nt):
            p.scalar.op("activation", s1[:, tt * 512:(tt + 1) * 512],
                        ctx.pv(0 + tt), AF.Silu, inc=[(sSc, 1)])
            nsc += 1
        p.vector.wait(sSc, nsc)
        for tt in range(nt):
            p.vector.op("tensor_mul", mid[:, tt * 512:(tt + 1) * 512],
                        s1[:, tt * 512:(tt + 1) * 512], ctx.pv(2 + tt),
                        inc=[(sV, 1)])
            nv += 1
        p.tensor.wait(sV, nv)
        for tt in range(nt):
            p.tensor.op("matmul", ctx.pv(4 + tt), h64,
                        mid[:, tt * 512:(tt + 1) * 512], start=True, stop=True,
                        inc=[(sM, 1)])
        p.vector.wait(sM, jp * GRP + 3 * nt)
        for tt in range(nt):
            p.vector.op("tensor_copy",
                        sA[:, jp * T + tt * 512: jp * T + tt * 512 + 512],
                        ctx.pv(4 + tt), inc=[(sV, 1)])
            nv += 1
    p.barrier()

    nv2 = 0
    sQ = p.sem("sq")
    for ot in range(DCH):
        if ot >= 2:
            p.gpsimd.wait(sV2, (ot - 1) * nt)
        p.gpsimd.op("dma_start", out=xin[ot % 2],
                    in_=xsh[ot * 128:(ot + 1) * 128, :], inc=[(sX, 16)])
        for q in range(8):
            qidx = ot * 8 + q
            b = qidx % 2
            if qidx >= 2:
                p.sync.wait(sQ, qidx - 1)
            p.sync.op("dma_start", out=gsb[b],
                      in_=G2[ot, q * 8:(q + 1) * 8].rearrange("jp p f -> p jp f"),
                      inc=[(sG2w[b], 16)])
            p.tensor.wait(sG2w[b])
            if q == 0 and ot >= 1:
                p.tensor.wait(sV2, ot * nt)
            for tt in range(nt):
                for jj in range(8):
                    jp = q * 8 + jj
                    last = (tt == nt - 1 and jj == 7)
                    p.tensor.op("matmul", ctx.pv(0 + tt),
                                gsb[b][:, jj * 128:(jj + 1) * 128],
                                sA[:, jp * T + tt * 512: jp * T + tt * 512 + 512],
                                start=(jp == 0), stop=(jp == 63),
                                inc=[(sQ, 1)] if last else None)
        p.vector.wait(sQ, (ot + 1) * 8)
        p.vector.wait(sX, 16 * (DCH + ot + 1))
        if ot >= 1:
            p.vector.wait(sO, 16 * ot)
        for tt in range(nt):
            p.vector.op("scalar_tensor_tensor",
                        xo_sb[:, tt * 512:(tt + 1) * 512], ctx.pv(0 + tt),
                        0.5 / math.sqrt(2.0),
                        xin[ot % 2][:, tt * 512:(tt + 1) * 512],
                        ALU.mult, ALU.add, inc=[(sV2, 1)])
            nv2 += 1
        p.sync.wait(sV2, nv2)
        p.sync.op("dma_start", out=xout[ot * 128:(ot + 1) * 128, :],
                  in_=xo_sb, inc=[(sO, 16)])
    p.sync.wait(sO)
    p.gpsimd.wait(sO)
    return ctx




def emit_stream_norm(ctx, xsh, h_bf, T, resid_xout=None, resid_scale=0.5,
                     eps=1e-5, h_f32_out=None):
    p = ctx.p
    nt = T // 512
    xin = ctx.scratch("sn_xin", [128, T], F32)
    xin2 = ctx.scratch("sn_xin2", [128, T], F32)
    xsq = getattr(ctx, "xsq_buf", None)
    if xsq is None:
        xsq = ctx.scratch("sn_xsq", [128, DCH * T], BF16)
    mean_r = ctx.scratch("sn_mean", [1, T], F32)
    rstd_r = ctx.scratch("sn_rstd", [1, T], F32)
    tmp = ctx.scratch("sn_tmp", [128, T], F32)
    var_r = ctx.scratch("sn_var", [1, T], F32)
    sX = ctx.p.sem("snx")
    sC = ctx.p.sem("snc")

    for c in range(DCH):
        b = c % 2
        buf = xin if b == 0 else xin2
        if c >= 2:
            p.sync.wait(sC, 2 * (c - 1))
        p.sync.op("dma_start", out=buf, in_=xsh[c * 128:(c + 1) * 128, :],
                  inc=[(sX, 16)])
        sl = slice(c * T, (c + 1) * T)
        p.vector.wait(sX, 16 * (c + 1))
        p.vector.op("tensor_copy", h_bf[:, sl], buf, inc=[(sC, 1)])
        p.scalar.wait(sX, 16 * (c + 1))
        p.scalar.op("activation", xsq[:, sl], buf, AF.Square, inc=[(sC, 1)])
    p.barrier()
    for c in range(DCH):
        for tt in range(nt):
            tsl = slice(c * T + tt * 512, c * T + tt * 512 + 512)
            osl = slice(tt * 512, tt * 512 + 512)
            p.tensor.matmul(ctx.prow(0)[:, osl], ctx.ones_col_bf, h_bf[:, tsl],
                            start=(c == 0), stop=(c == DCH - 1))
            p.tensor.matmul(ctx.prow(1)[:, osl], ctx.ones_col_bf, xsq[:, tsl],
                            start=(c == 0), stop=(c == DCH - 1))
    p.barrier()
    p.vector.tensor_scalar_mul(mean_r, ctx.prow(0), 1.0 / DM)
    p.vector.tensor_scalar_mul(var_r, ctx.prow(1), 1.0 / DM)
    p.vector.tensor_mul(rstd_r, mean_r, mean_r)
    p.vector.tensor_sub(var_r, var_r, rstd_r)
    p.barrier()
    p.scalar.activation(rstd_r, var_r, AF.Ln, bias=ctx.cbias(eps, 1), scale=1.0)
    p.scalar.activation(rstd_r, rstd_r, AF.Exp, scale=-0.5)
    p.barrier()
    for tt in range(nt):
        osl = slice(tt * 512, tt * 512 + 512)
        p.tensor.matmul(ctx.pfull(0)[:, osl], ctx.ones_row_f, mean_r[:, osl],
                        start=True, stop=True)
        p.tensor.matmul(ctx.pfull(1)[:, osl], ctx.ones_row_f, rstd_r[:, osl],
                        start=True, stop=True)
    p.barrier()
    if resid_xout is None:
        for c in range(DCH):
            sl = slice(c * T, (c + 1) * T)
            p.vector.tensor_sub(tmp, h_bf[:, sl], ctx.pfull(0))
            p.vector.tensor_mul(tmp, tmp, ctx.pfull(1))
            p.vector.tensor_copy(h_bf[:, sl], tmp)
    else:
        sX2 = ctx.p.sem("snx2")
        sV2 = ctx.p.sem("snv2")
        sO2 = ctx.p.sem("sno2")
        for c in range(DCH):
            b = c % 2
            buf = xin if b == 0 else xin2
            if c >= 2:
                p.sync.wait(sV2, 2 * (c - 1))
            p.sync.op("dma_start", out=buf, in_=xsh[c * 128:(c + 1) * 128, :],
                      inc=[(sX2, 16)])
            sl = slice(c * T, (c + 1) * T)
            p.vector.wait(sX2, 16 * (c + 1))
            if c >= 1 and h_f32_out is not None:
                p.vector.wait(sO2)
            p.vector.tensor_sub(tmp, buf, ctx.pfull(0))
            p.vector.tensor_mul(tmp, tmp, ctx.pfull(1))
            p.vector.tensor_copy(h_bf[:, sl], tmp)
            p.vector.op("scalar_tensor_tensor", buf, tmp, resid_scale, buf,
                        ALU.mult, ALU.add, inc=[(sV2, 2)])
            p.sync.wait(sV2, 2 * (c + 1))
            p.sync.op("dma_start", out=resid_xout[c * 128:(c + 1) * 128, :],
                      in_=buf, inc=[(sO2, 16)])
            if h_f32_out is not None:
                p.sync.op("dma_start", out=h_f32_out[c * 128:(c + 1) * 128, :],
                          in_=tmp, inc=[(sO2, 16)])
        p.sync.wait(sO2)
    p.barrier()


def emit_proj(ctx, h_bf, W, out_bf, dout, T, name):
    p = ctx.p
    nt = T // 512
    not_ = dout // 128
    wsl = [ctx.scratch(f"pj_w{i}", [128, DCH * 128], BF16) for i in range(2)]
    sW = [p.sem("pjw0"), p.sem("pjw1")]
    sM = p.sem("pjm")
    sV = p.sem("pjv")
    m0 = sM.rel
    v0 = sV.rel
    p.barrier()
    for ot in range(not_):
        b = ot % 2
        if ot >= 2:
            p.gpsimd.wait(sM, m0 + (ot - 1) * nt)
        p.gpsimd.op("dma_start", out=wsl[b],
                    in_=W[:, ot * 128:(ot + 1) * 128].rearrange(
                        "(c p) f -> p c f", p=128),
                    inc=[(sW[b], 16)])
        p.tensor.wait(sW[b])
        if ot >= 2:
            p.tensor.wait(sV, v0 + (ot - 1) * nt)
        for tt in range(nt):
            pb = ctx.pv((ot % 2) * 2 + tt)
            for c in range(DCH):
                rsl = slice(c * T + tt * 512, c * T + tt * 512 + 512)
                p.tensor.op("matmul", pb, wsl[b][:, c * 128:(c + 1) * 128],
                            h_bf[:, rsl], start=(c == 0), stop=(c == DCH - 1),
                            inc=[(sM, 1)] if c == DCH - 1 else None)
        p.vector.wait(sM, m0 + (ot + 1) * nt)
        for tt in range(nt):
            pb = ctx.pv((ot % 2) * 2 + tt)
            p.vector.op("tensor_copy",
                        out_bf[:, ot * T + tt * 512: ot * T + tt * 512 + 512],
                        pb, inc=[(sV, 1)])
    p.barrier()


def emit_rope_head(ctx, src, dst, T, cdup_sb, sdup_sb, pswap_sb, bank=4):
    p = ctx.p
    nt = T // 512
    rt1 = ctx.scratch("rp_t1", [128, T], BF16)
    rt2 = ctx.scratch("rp_t2", [128, T], BF16)
    p.barrier()
    for tt in range(nt):
        p.tensor.matmul(ctx.pv(bank + tt), pswap_sb,
                        src[:, tt * 512:(tt + 1) * 512], start=True, stop=True)
    p.vector.tensor_mul(rt1, src, cdup_sb)
    p.barrier()
    for tt in range(nt):
        sl = slice(tt * 512, (tt + 1) * 512)
        p.vector.tensor_mul(rt2[:, sl], ctx.pv(bank + tt), sdup_sb[:, sl])
    p.barrier()
    p.vector.tensor_add(dst, rt1, rt2)
    p.barrier()


def emit_bound_softmax_fm(ctx, q_bf, nheads, T, scratch_bf):
    p = ctx.p
    nt = T // 512
    row1 = ctx.scratch("bs_row1", [1, 512], F32)
    row2 = ctx.scratch("bs_row2", [1, 512], F32)
    p.barrier()
    for h in range(nheads):
        for tt in range(nt):
            sl = slice(h * T + tt * 512, h * T + tt * 512 + 512)
            p.barrier()
            p.scalar.activation(scratch_bf, q_bf[:, sl], AF.Square)
            p.barrier()
            p.tensor.matmul(ctx.prow(0)[:, 0:512], ctx.ones_col_bf, scratch_bf,
                            start=True, stop=True)
            p.barrier()
            p.scalar.activation(row1, ctx.prow(0)[:, 0:512], AF.Ln,
                                bias=ctx.cbias(1e-12, 1), scale=1.0)
            p.scalar.activation(row1, row1, AF.Exp, scale=0.5)
            p.barrier()
            p.vector.tensor_scalar_mul(row1, row1, -1.0)
            p.barrier()
            p.tensor.matmul(ctx.pfull(1)[:, 0:512], ctx.ones_row_f, row1,
                            start=True, stop=True)
            p.barrier()
            p.vector.tensor_add(scratch_bf, q_bf[:, sl], ctx.pfull(1)[:, 0:512])
            p.barrier()
            p.scalar.activation(scratch_bf, scratch_bf, AF.Exp)
            p.barrier()
            p.tensor.matmul(ctx.prow(0)[:, 0:512], ctx.ones_col_bf, scratch_bf,
                            start=True, stop=True)
            p.barrier()
            p.vector.reciprocal(row2, ctx.prow(0)[:, 0:512])
            p.barrier()
            p.tensor.matmul(ctx.pfull(1)[:, 0:512], ctx.ones_row_f, row2,
                            start=True, stop=True)
            p.barrier()
            p.vector.tensor_mul(q_bf[:, sl], scratch_bf, ctx.pfull(1)[:, 0:512])
    p.barrier()



def build_a0a(ctx=None):
    if ctx is None:
        ctx = Ctx("a0a", bf16_banks=2)
    p = ctx.p
    T = TPC
    nt = T // 512
    NH, NG = 16, 4
    xsh = ctx.din("xsh", [DM, T], F32)
    Wq = ctx.din("Wq", [DM, DM], F32)
    Wk = ctx.din("Wk", [DM, 512], F32)
    Wv = ctx.din("Wv", [DM, 512], F32)
    cosd = ctx.din("cosd", [128, T], BF16)
    sind = ctx.din("sind", [128, T], BF16)
    PSW = ctx.din("PSW", [128, 128], BF16)
    CUM = ctx.din("CUM", [128, 128], BF16)
    SUF = ctx.din("SUF", [128, 128], BF16)
    IDT = ctx.din("IDT", [128, 128], BF16)

    sq_out = ctx.dout("sq_out", [DM, T], BF16)
    sk_out = ctx.dout("sk_out", [512, T], BF16)
    vtok_out = ctx.dout("vtok_out", [T, 512], BF16)
    ogla_out = ctx.dout("ogla_out", [8, NH, 128, 128], BF16)
    acache_out = ctx.dout("acache_out", [8, NH, 128, 128], BF16)
    boff_out = ctx.dout("boff_out", [128, 8 * NG], F32)
    D_out = ctx.dout("D_out", [2, 128, NG], F32)
    U_out = ctx.dout("U_out", [2, NG, 128, 128], F32)
    kmax_out = ctx.dout("kmax_out", [1, NG], F32)

    ctx.alloc_consts()
    h_bf = ctx.sb("h_bf", [128, DCH * T], BF16)
    q_bf = ctx.sb("q_bf", [128, NH * T], BF16)
    ctx.xsq_buf = q_bf[:, 0:DCH * T]
    k_bf = ctx.sb("k_bf", [128, NG * T], BF16)
    v_bf = ctx.sb("v_bf", [128, NG * T], BF16)
    sq_t = [ctx.sb("sq_t", [128, T], BF16) for _ in range(2)]
    sk_sb = ctx.sb("sk_sb", [128, NG * T], BF16)
    cos_sb = ctx.sb("cos_sb", [128, T], BF16)
    sin_sb = ctx.sb("sin_sb", [128, T], BF16)
    psw_sb = ctx.sb("psw_sb", [128, 128], BF16)
    cum_sb = ctx.sb("cum_sb", [128, 128], BF16)
    suf_sb = ctx.sb("suf_sb", [128, 128], BF16)
    idt_sb = ctx.sb("idt_sb", [128, 128], BF16)
    ktok = ctx.sb("ktok", [128, 8 * 512], BF16)
    vtok = ctx.sb("vtok", [128, 8 * 512], BF16)
    kstok = ctx.sb("kstok", [128, 8 * 512], BF16)
    gl_bf = ctx.sb("gl_bf", [128, 8 * 512], BF16)
    bsx = ctx.sb("bsx", [128, 512], BF16)
    expB = ctx.sb("expB", [128, 512], BF16)
    expnB = ctx.sb("expnB", [128, 512], BF16)
    exps = ctx.sb("exps", [128, 512], BF16)
    C_tok = ctx.sb("C_tok", [128, 512], BF16)
    Ktil = ctx.sb("Ktil", [128, 512], BF16)
    C_dm = ctx.sb("C_dm", [128, 512], BF16)
    A_sc = ctx.sb("A_sc", [128, NH * 128], BF16)
    o_sc = ctx.sb("o_sc", [128, NH * 128], BF16)
    S_sb = ctx.sb("S_sb", [128, 512], F32)
    S_bf = ctx.sb("S_bf", [128, 512], BF16)
    boff_sb = ctx.sb("boff_sb", [128, NG], F32)
    boffe_sb = ctx.sb("boffe_sb", [128, 8 * NG], F32)
    km_sb = ctx.sb("km_sb", [1, NG], F32)
    kn_row = ctx.scratch("sn_mean", [1, T], F32)

    sL = p.sem("a0l")
    sO = p.sem("a0o")

    p.sync.op("dma_start", out=cos_sb, in_=cosd, inc=[(sL, 16)])
    p.sync.op("dma_start", out=sin_sb, in_=sind, inc=[(sL, 16)])
    p.sync.op("dma_start", out=psw_sb, in_=PSW, inc=[(sL, 16)])
    p.sync.op("dma_start", out=cum_sb, in_=CUM, inc=[(sL, 16)])
    p.sync.op("dma_start", out=suf_sb, in_=SUF, inc=[(sL, 16)])
    p.sync.op("dma_start", out=idt_sb, in_=IDT, inc=[(sL, 16)])
    p.sync.wait(sL)

    emit_stream_norm(ctx, xsh, h_bf, T)
    emit_proj(ctx, h_bf, Wq, q_bf, DM, T, "q")
    emit_proj(ctx, h_bf, Wk, k_bf, 512, T, "k")
    emit_proj(ctx, h_bf, Wv, v_bf, 512, T, "v")

    for h in range(NH):
        b = h % 2
        if h >= 2:
            p.sync.wait(sO)
        emit_rope_head(ctx, q_bf[:, h * T:(h + 1) * T], sq_t[b], T,
                       cos_sb, sin_sb, psw_sb)
        p.sync.op("dma_start", out=sq_out[h * 128:(h + 1) * 128, :],
                  in_=sq_t[b], inc=[(sO, 16)])
    p.barrier()
    for g in range(NG):
        emit_rope_head(ctx, k_bf[:, g * T:(g + 1) * T],
                       sk_sb[:, g * T:(g + 1) * T], T, cos_sb, sin_sb, psw_sb)
        p.sync.op("dma_start", out=sk_out[g * 128:(g + 1) * 128, :],
                  in_=sk_sb[:, g * T:(g + 1) * T], inc=[(sO, 16)])

    p.barrier()
    for g in range(NG):
        p.barrier()
        p.scalar.activation(bsx, k_bf[:, g * T:g * T + 512], AF.Square)
        p.scalar.activation(gl_bf[:, 0:512], k_bf[:, g * T + 512:g * T + 1024],
                            AF.Square)
        p.barrier()
        p.tensor.matmul(ctx.prow(0)[:, 0:512], ctx.ones_col_bf, bsx,
                        start=True, stop=True)
        p.tensor.matmul(ctx.prow(0)[:, 512:1024], ctx.ones_col_bf,
                        gl_bf[:, 0:512], start=True, stop=True)
        p.barrier()
        p.vector.tensor_copy(kn_row, ctx.prow(0))
        p.barrier()
        p.vector.reduce_max(km_sb[:, g:g + 1], kn_row, axis=AX.X)
    p.barrier()
    p.sync.op("dma_start", out=kmax_out, in_=km_sb, inc=[(sO, 16)])

    p.barrier()
    ntr = 0
    sTT = p.sem("a0tt")
    sTV = p.sem("a0tv")
    for tt in range(8):
        for g in range(NG):
            half = ntr % 2
            bview = ctx.bpool[0][:, half * 1024: half * 1024 + 128]
            if ntr >= 2:
                p.tensor.wait(sTV, ntr - 1)
            p.tensor.op("transpose", bview,
                        k_bf[:, g * T + tt * 128: g * T + (tt + 1) * 128],
                        idt_sb, inc=[(sTT, 1)])
            ntr += 1
            p.vector.wait(sTT, ntr)
            p.vector.op("tensor_copy",
                        ktok[:, tt * 512 + g * 128: tt * 512 + (g + 1) * 128],
                        bview, inc=[(sTV, 1)])
            half = ntr % 2
            bview = ctx.bpool[0][:, half * 1024: half * 1024 + 128]
            if ntr >= 2:
                p.tensor.wait(sTV, ntr - 1)
            p.tensor.op("transpose", bview,
                        v_bf[:, g * T + tt * 128: g * T + (tt + 1) * 128],
                        idt_sb, inc=[(sTT, 1)])
            ntr += 1
            p.vector.wait(sTT, ntr)
            p.vector.op("tensor_copy",
                        vtok[:, tt * 512 + g * 128: tt * 512 + (g + 1) * 128],
                        bview, inc=[(sTV, 1)])
    p.barrier()
    for tt in range(8):
        p.sync.op("dma_start",
                  out=vtok_out[tt * 128:(tt + 1) * 128, :],
                  in_=vtok[:, tt * 512:(tt + 1) * 512], inc=[(sO, 16)])

    p.barrier()
    sn_tmp = ctx.scratch("sn_tmp", [128, T], F32)
    t1f = sn_tmp[:, 0:512]
    t2f = sn_tmp[:, 512:1024]
    mcol = ctx.sb("mcol", [128, 1], F32)
    zcol = ctx.sb("zcol", [128, 1], F32)
    for tt in range(8):
        sl = slice(tt * 512, (tt + 1) * 512)
        p.barrier()
        p.scalar.activation(t1f, ktok[:, sl], AF.Abs)
        p.barrier()
        p.scalar.activation(t1f, t1f, AF.Exp, scale=-1.0)
        p.barrier()
        p.scalar.activation(t1f, t1f, AF.Ln, bias=1.0, scale=1.0)
        p.barrier()
        p.scalar.activation(t2f, ktok[:, sl], AF.Relu, scale=-1.0)
        p.barrier()
        p.vector.tensor_add(t1f, t1f, t2f)
        p.vector.tensor_scalar_mul(gl_bf[:, sl], t1f, -1.0 / 16.0)
        for g in range(NG):
            gsl = slice(tt * 512 + g * 128, tt * 512 + (g + 1) * 128)
            p.barrier()
            p.vector.reduce_max(mcol, ktok[:, gsl], axis=AX.X, negate=True)
            p.barrier()
            p.scalar.activation(t1f[:, 0:128], ktok[:, gsl], AF.Exp, bias=mcol,
                                accum_out=zcol)
            p.barrier()
            p.vector.reciprocal(zcol, zcol)
            p.barrier()
            p.vector.tensor_single_scalar(kstok[:, gsl], t1f[:, 0:128], zcol,
                                          ALU.mult)
    p.barrier()

    emit_bound_softmax_fm(ctx, q_bf, NH, T, bsx)

    gz = p.sem("a0gz")
    p.gpsimd.op("memset", S_sb, 0.0, inc=[(gz, 1)])
    p.gpsimd.op("memset", S_bf, 0.0, inc=[(gz, 1)])
    p.gpsimd.op("memset", boff_sb, 0.0, inc=[(gz, 1)])
    p.barrier()
    for sc in range(8):
        chunk_start = (sc % 4 == 0)
        sl = slice(sc * 512, (sc + 1) * 512)
        if chunk_start and sc > 0:
            p.gpsimd.memset(S_sb, 0.0)
            p.gpsimd.memset(S_bf, 0.0)
            p.gpsimd.memset(boff_sb, 0.0)
            p.barrier()
        p.scalar.activation(boffe_sb[:, sc * NG:(sc + 1) * NG], boff_sb, AF.Exp)
        p.barrier()
        for g in range(NG):
            p.tensor.matmul(ctx.pfull(0)[:, g * 128:(g + 1) * 128],
                            gl_bf[:, sc * 512 + g * 128: sc * 512 + (g + 1) * 128],
                            cum_sb, start=True, stop=True)
        p.tensor.matmul(ctx.pfull(1)[:, 0:512], cum_sb, gl_bf[:, sl],
                        start=True, stop=True)
        p.tensor.matmul(ctx.pfull(2)[:, 0:512], suf_sb, gl_bf[:, sl],
                        start=True, stop=True)
        p.barrier()
        p.scalar.activation(expB, ctx.pfull(0)[:, 0:512], AF.Exp)
        p.scalar.activation(expnB, ctx.pfull(1)[:, 0:512], AF.Exp, scale=-1.0)
        p.scalar.activation(exps, ctx.pfull(2)[:, 0:512], AF.Exp)
        p.barrier()
        for g in range(NG):
            p.vector.tensor_add(boff_sb[:, g:g + 1], boff_sb[:, g:g + 1],
                                ctx.pfull(0)[:, g * 128 + 127: g * 128 + 128])
        p.vector.tensor_mul(C_tok, kstok[:, sl], expnB)
        p.vector.tensor_mul(Ktil, kstok[:, sl], exps)
        p.barrier()
        for g in range(NG):
            half = g % 2
            bview = ctx.bpool[0][:, half * 1024: half * 1024 + 128]
            p.tensor.op("transpose", bview, C_tok[:, g * 128:(g + 1) * 128],
                        idt_sb)
            p.barrier()
            p.vector.tensor_copy(C_dm[:, g * 128:(g + 1) * 128], bview)
            p.barrier()
        for h in range(NH):
            g = h // 4
            p.vector.tensor_mul(A_sc[:, h * 128:(h + 1) * 128],
                                q_bf[:, h * T + sc * 128: h * T + sc * 128 + 128],
                                expB[:, g * 128:(g + 1) * 128])
        p.barrier()
        p.sync.op("dma_start",
                  out=acache_out[sc].rearrange("h d t -> d h t"),
                  in_=A_sc, inc=[(sO, 16)])
        for h in range(NH):
            g = h // 4
            pb = ctx.pv(4 + (h % 2))
            p.tensor.matmul(pb[:, 0:128], C_dm[:, g * 128:(g + 1) * 128],
                            A_sc[:, h * 128:(h + 1) * 128], start=True, stop=True)
            p.barrier()
            p.vector.tensor_mul(bsx[:, 0:128], pb[:, 0:128], cum_sb)
            p.barrier()
            ob = ctx.pfull(1)[:, (h % 4) * 128:(h % 4) * 128 + 128]
            p.tensor.matmul(ob, vtok[:, sc * 512 + g * 128: sc * 512 + (g + 1) * 128],
                            bsx[:, 0:128], start=True, stop=chunk_start)
            if not chunk_start:
                p.tensor.matmul(ob, S_bf[:, g * 128:(g + 1) * 128],
                                A_sc[:, h * 128:(h + 1) * 128], start=False,
                                stop=True)
            p.barrier()
            p.vector.tensor_scalar_mul(o_sc[:, h * 128:(h + 1) * 128], ob, 0.5)
            p.barrier()
        p.sync.op("dma_start",
                  out=ogla_out[sc].rearrange("h d t -> d h t"),
                  in_=o_sc, inc=[(sO, 16)])
        for g in range(NG):
            p.tensor.matmul(ctx.pfull(2)[:, g * 128:(g + 1) * 128],
                            Ktil[:, g * 128:(g + 1) * 128],
                            vtok[:, sc * 512 + g * 128: sc * 512 + (g + 1) * 128],
                            start=True, stop=True)
        p.barrier()
        for g in range(NG):
            p.vector.scalar_tensor_tensor(
                S_sb[:, g * 128:(g + 1) * 128], S_sb[:, g * 128:(g + 1) * 128],
                expB[:, g * 128 + 127: g * 128 + 128],
                ctx.pfull(2)[:, g * 128:(g + 1) * 128], ALU.mult, ALU.add)
        p.barrier()
        p.vector.tensor_copy(S_bf, S_sb)
        p.barrier()
        if sc % 4 == 3:
            ci = sc // 4
            p.scalar.activation(t1f[:, 0:NG], boff_sb, AF.Exp)
            p.barrier()
            p.sync.op("dma_start", out=U_out[ci].rearrange("g d j -> d g j"),
                      in_=S_sb, inc=[(sO, 16)])
            p.sync.op("dma_start", out=D_out[ci],
                      in_=t1f[:, 0:NG], inc=[(sO, 16)])
            p.barrier()
    p.barrier()
    p.sync.op("dma_start", out=boff_out, in_=boffe_sb, inc=[(sO, 16)])
    p.sync.wait(sO)
    p.gpsimd.wait(sO)
    return ctx




SC128 = 1.0 / math.sqrt(128.0)


def emit_proj_resid(ctx, src_bf, W, xsh, xout, T, scale, sO, nj=DCH,
                    h2_in=None, h2_out=None, bias_col=None):
    p = ctx.p
    nt = T // 512
    nev = 2 if h2_in is not None else 1
    wsl = [ctx.scratch(f"pr_w{i}", [128, nj * 128], BF16) for i in range(2)]
    xin = ctx.scratch("pr_xin", [128, T], F32)
    xin2 = ctx.scratch("pr_xin2", [128, T], F32)
    xo = ctx.scratch("pr_xo", [128, T], F32)
    if h2_in is not None:
        hin = ctx.scratch("pr_hin", [128, T], F32)
        ho = ctx.scratch("pr_ho", [128, T], F32)
    sW = [p.sem("prw0"), p.sem("prw1")]
    sM = p.sem("prm")
    sV = p.sem("prv")
    sX = p.sem("prx")
    p.barrier()
    for ot in range(DCH):
        b = ot % 2
        buf = xin if b == 0 else xin2
        if ot >= 2:
            p.gpsimd.wait(sM, (ot - 1) * nt)
            p.gpsimd.wait(sV, (ot - 1) * nt * nev)
        p.gpsimd.op("dma_start", out=wsl[b],
                    in_=W[:, ot * 128:(ot + 1) * 128].rearrange(
                        "(c p) f -> p c f", p=128), inc=[(sW[b], 16)])
        p.gpsimd.op("dma_start", out=buf, in_=xsh[ot * 128:(ot + 1) * 128, :],
                    inc=[(sX, 16)])
        if h2_in is not None:
            if ot >= 1:
                p.gpsimd.wait(sV, ot * nt * nev)
            p.gpsimd.op("dma_start", out=hin,
                        in_=h2_in[ot * 128:(ot + 1) * 128, :], inc=[(sX, 16)])
        p.tensor.wait(sW[b])
        if ot >= 1:
            p.tensor.wait(sV, ot * nt * nev)
        for tt in range(nt):
            pb = ctx.pv((ot % 2) * 2 + tt)
            for c in range(nj):
                rsl = slice(c * T + tt * 512, c * T + tt * 512 + 512)
                p.tensor.op("matmul", pb, wsl[b][:, c * 128:(c + 1) * 128],
                            src_bf[:, rsl], start=(c == 0), stop=(c == nj - 1),
                            inc=[(sM, 1)] if c == nj - 1 else None)
        p.vector.wait(sM, (ot + 1) * nt)
        p.vector.wait(sX)
        if ot >= 1:
            p.vector.wait(sO, 16 * ot * nev)
        for tt in range(nt):
            pb = ctx.pv((ot % 2) * 2 + tt)
            ts = slice(tt * 512, (tt + 1) * 512)
            if bias_col is not None:
                p.vector.op("scalar_tensor_tensor", xo[:, ts], pb,
                            bias_col[:, ot:ot + 1], buf[:, ts],
                            ALU.add, ALU.add, inc=[(sV, 1)])
            else:
                p.vector.op("scalar_tensor_tensor", xo[:, ts], pb, scale,
                            buf[:, ts], ALU.mult, ALU.add, inc=[(sV, 1)])
            if h2_in is not None:
                p.vector.op("scalar_tensor_tensor", ho[:, ts], pb, 1.0,
                            hin[:, ts], ALU.mult, ALU.add, inc=[(sV, 1)])
        p.sync.wait(sV, (ot + 1) * nt * nev)
        p.sync.op("dma_start", out=xout[ot * 128:(ot + 1) * 128, :],
                  in_=xo, inc=[(sO, 16)])
        if h2_in is not None:
            p.sync.op("dma_start", out=h2_out[ot * 128:(ot + 1) * 128, :],
                      in_=ho, inc=[(sO, 16)])
    p.barrier()


def build_flash(mode="a0", ctx=None):
    if ctx is None:
        ctx = Ctx("flash_" + mode, bf16_banks=2)
    p = ctx.p
    T = TPC
    NH, NG = 16, 4
    xsh = ctx.din("xsh", [DM, T], F32)
    sq_in = ctx.din("sq_in", [DM, T], BF16)
    skall = ctx.din("skall", [NCORES, 512, T], BF16)
    vall = ctx.din("vall", [NCORES, T, 512], BF16)
    skown = ctx.din("skown", [512, T], BF16)
    vown = ctx.din("vown", [T, 512], BF16)
    if mode == "a0":
        ogla = ctx.din("ogla", [8, NH, 128, 128], BF16)
        acache = ctx.din("acache", [8, NH, 128, 128], BF16)
        boffe = ctx.din("boffe", [128, 32], F32)
        Dg = ctx.din("Dg", [NCORES, 2, 128, NG], F32)
        Ug = ctx.din("Ug", [NCORES, 2, NG, 128, 128], F32)
        selpf = ctx.din("selpf", [128, 32], F32)
    else:
        hidden_in = ctx.din("hidden_in", [DM, T], F32)
        h2_out = ctx.dout("h2_out", [DM, T], F32)
    kmx = ctx.din("kmx", [1, 32], F32)
    zsel = ctx.din("zsel", [128, 32], F32)
    DMASK = ctx.din("DMASK", [4, 128, 512], BF16)
    Wo = ctx.din("Wo", [DM, DM], F32)
    xout = ctx.dout("xout", [DM, T], F32)

    ctx.alloc_consts()
    sq_sb = ctx.sb("sq_sb", [128, NH * T], BF16)
    OG = ctx.sb("OG", [128, NH * T], BF16)
    sko_sb = ctx.sb("sko", [128, NG * T], BF16)
    vo_sb = ctx.sb("vo", [128, 8 * 512], BF16)
    kvr = [ctx.sb("kvr", [128, 4096], BF16) for _ in range(2)]
    Spf = ctx.sb("Spf", [128, 2 * 512], F32)
    Spf_bf = ctx.sb("Spf_bf", [128, 2 * 512], BF16)
    dld = [ctx.sb("dld", [128, NG], F32) for _ in range(2)]
    uld2 = [ctx.sb("uld", [128, NG * 128], F32) for _ in range(2)]
    og_t = ctx.sb("og_t", [128, NH * 128], BF16)
    ac_t = ctx.sb("ac_t", [128, NH * 128], BF16)
    bo_sb = ctx.sb("bo_sb", [128, 32], F32)
    spf_sb = ctx.sb("spf_sb", [128, 32], F32)
    a2_t = ctx.sb("a2_t", [128, 128], BF16)
    km_sb = ctx.sb("km_sb", [1, 32], F32)
    kr_sb = ctx.sb("kr_sb", [1, NG], F32)
    zs_sb = ctx.sb("zs_sb", [128, 32], F32)
    dmask_sb = ctx.sb("dmask_sb", [128, 4 * 512], BF16)
    bsx = ctx.sb("bsx", [128, 512], BF16)
    row1 = ctx.sb("row1", [1, 512], F32)
    mrow = ctx.sb("mrow", [1, 1], F32)
    mcol = ctx.sb("mcol", [128, 2], F32)
    bcol = ctx.sb("bcol", [128, 2 * 2], F32)
    Pbuf = ctx.sb("Pbuf", [128, 3 * 512], BF16)
    linv = ctx.sb("linv", [1, 512], F32)
    linv_bc = ctx.sb("linv_bc", [128, 512], F32)
    t1f = ctx.sb("t1f", [128, 512], F32)

    sL = p.sem("bL")
    sO = p.sem("bO")
    sUD = p.sem("bUD")

    for h2 in range(NH):
        p.sync.op("dma_start", out=sq_sb[:, h2 * T:(h2 + 1) * T],
                  in_=sq_in[h2 * 128:(h2 + 1) * 128, :], inc=[(sL, 16)])
    for g in range(NG):
        p.sync.op("dma_start", out=sko_sb[:, g * T:(g + 1) * T],
                  in_=skown[g * 128:(g + 1) * 128, :], inc=[(sL, 16)])
    p.sync.op("dma_start", out=vo_sb,
              in_=vown.rearrange("(s p) j -> p s j", p=128), inc=[(sL, 16)])
    if mode == "a0":
        p.sync.op("dma_start", out=bo_sb, in_=boffe, inc=[(sL, 16)])
        p.sync.op("dma_start", out=spf_sb, in_=selpf, inc=[(sL, 16)])
    p.sync.op("dma_start", out=km_sb, in_=kmx,
              allow_slow_non_contiguous=True, inc=[(sL, 16)])
    p.sync.op("dma_start", out=zs_sb, in_=zsel, inc=[(sL, 16)])
    p.sync.op("dma_start", out=dmask_sb,
              in_=DMASK.rearrange("s p t -> p s t"), inc=[(sL, 16)])
    p.sync.wait(sL)
    p.barrier()

    gz = p.sem("bgz")
    p.gpsimd.op("memset", Spf, 0.0, inc=[(gz, 1)])
    if mode != "a0":
        p.gpsimd.op("memset", OG, 0.0, inc=[(gz, 1)])
    p.barrier()
    for slot in (range(2) if mode == "a0" else []):
        for j in range(15):
            b = j % 2
            own = j if j <= 7 else 15 - j
            osl = 0 if j <= 7 else 1
            scol = spf_sb[:, slot * 16 + j: slot * 16 + j + 1]
            p.barrier()
            p.sync.op("dma_start", out=dld[b], in_=Dg[own, osl],
                      inc=[(sUD, 16)])
            p.sync.op("dma_start", out=uld2[b],
                      in_=Ug[own, osl].rearrange("g d j2 -> d g j2"),
                      inc=[(sUD, 16)])
            p.vector.wait(sUD)
            p.barrier()
            p.vector.tensor_scalar_add(dld[b], dld[b], -1.0)
            p.vector.tensor_single_scalar(dld[b], dld[b], scol, ALU.mult)
            p.vector.tensor_scalar_add(dld[b], dld[b], 1.0)
            p.vector.tensor_single_scalar(uld2[b], uld2[b], scol, ALU.mult)
            p.barrier()
            for g in range(NG):
                p.vector.scalar_tensor_tensor(
                    Spf[:, slot * 512 + g * 128: slot * 512 + (g + 1) * 128],
                    Spf[:, slot * 512 + g * 128: slot * 512 + (g + 1) * 128],
                    dld[b][:, g:g + 1],
                    uld2[b][:, g * 128:(g + 1) * 128], ALU.mult, ALU.add)
    p.barrier()
    p.vector.tensor_copy(Spf_bf, Spf)
    p.barrier()

    sC = p.sem("bC")
    for slot in (range(2) if mode == "a0" else []):
        for lsc in range(4):
            sc = slot * 4 + lsc
            p.barrier()
            p.sync.op("dma_start", out=og_t,
                      in_=ogla[sc].rearrange("h d t -> d h t"), inc=[(sC, 16)])
            p.sync.op("dma_start", out=ac_t,
                      in_=acache[sc].rearrange("h d t -> d h t"), inc=[(sC, 16)])
            p.vector.wait(sC)
            p.tensor.wait(sC)
            p.barrier()
            for h in range(NH):
                g = h // 4
                p.vector.tensor_single_scalar(a2_t, ac_t[:, h * 128:(h + 1) * 128],
                                              bo_sb[:, sc * NG + g: sc * NG + g + 1],
                                              ALU.mult)
                p.barrier()
                p.tensor.matmul(ctx.pv(0, 128),
                                Spf_bf[:, slot * 512 + g * 128: slot * 512 + (g + 1) * 128],
                                a2_t, start=True, stop=True)
                p.barrier()
                p.vector.scalar_tensor_tensor(
                    OG[:, h * T + sc * 128: h * T + (sc + 1) * 128],
                    ctx.pv(0, 128), 0.5, og_t[:, h * 128:(h + 1) * 128],
                    ALU.mult, ALU.add)
                p.barrier()
    p.barrier()

    p.vector.reduce_max(kr_sb, km_sb.rearrange("o (g c) -> o g c", c=8), axis=AX.X)
    p.barrier()
    p.scalar.activation(kr_sb, kr_sb, AF.Ln, bias=ctx.cbias(1e-12, 1), scale=1.0)
    p.scalar.activation(kr_sb, kr_sb, AF.Exp, scale=0.5)
    p.barrier()

    HGRPS = [[2 * i, 2 * i + 1] for i in range(8)]
    nQK = p.sem("bQK")
    nEX = p.sem("bEX")
    nMK = p.sem("bMK")
    nAV = p.sem("bAV")
    nBE = p.sem("bBE")
    nKV = p.sem("bKV")
    nEV = p.sem("bEV")

    for slot in range(2):
        nkc = 8 if slot == 0 else 16
        qsl = slice(slot * 512, slot * 512 + 512)
        for gi, grp in enumerate(HGRPS):
            p.barrier()
            for hi, h in enumerate(grp):
                g = h // 4
                p.barrier()
                p.scalar.activation(bsx, sq_sb[:, h * T + slot * 512:
                                               h * T + slot * 512 + 512], AF.Square)
                p.barrier()
                p.tensor.matmul(ctx.pv(4)[0:1, :], ctx.ones_col_bf, bsx,
                                start=True, stop=True)
                p.barrier()
                p.scalar.activation(row1, ctx.pv(4)[0:1, :], AF.Ln,
                                    bias=ctx.cbias(1e-12, 1), scale=1.0)
                p.scalar.activation(row1, row1, AF.Exp, scale=0.5)
                p.barrier()
                p.vector.tensor_single_scalar(row1, row1, kr_sb[:, g:g + 1],
                                              ALU.mult)
                p.barrier()
                p.vector.reduce_max(mrow, row1, axis=AX.X)
                p.barrier()
                p.tensor.matmul(ctx.pv(4)[:, 0:1], ctx.ones_row_f, mrow,
                                start=True, stop=True)
                p.barrier()
                p.vector.tensor_scalar_mul(mcol[:, hi:hi + 1], ctx.pv(4)[:, 0:1],
                                           -SC128)
            p.barrier()

            tiles = []
            for hi, h in enumerate(grp):
                for sub in range(4):
                    tiles.append(("diag", hi, h, sub))
            for kc in range(nkc):
                for hi, h in enumerate(grp):
                    for sub in range(4):
                        tiles.append(("main", hi, h, sub, kc))
            ntile = len(tiles)
            h_first = {}
            h_last = {}
            for idx, t_ in enumerate(tiles):
                hh = t_[2]
                if hh not in h_first:
                    h_first[hh] = idx
                h_last[hh] = idx

            qk0 = nQK.rel
            ex0 = nEX.rel
            av0 = nAV.rel
            be0 = nBE.rel
            kv0 = nKV.rel
            mk0 = nMK.rel
            nh = len(grp)

            for kc in range(nkc):
                b = kc % 2
                own = kc if kc <= 7 else 15 - kc
                osl = 0 if kc <= 7 else 1
                if kc >= 2:
                    p.gpsimd.wait(nAV, av0 + 4 * nh * kc)
                p.gpsimd.op("dma_start", out=kvr[b][:, 0:2048],
                            in_=skall[own][:, osl * 512:(osl + 1) * 512].rearrange(
                                "(g p) t -> p g t", p=128), inc=[(nKV, 16)])
                p.gpsimd.op("dma_start", out=kvr[b][:, 2048:4096],
                            in_=vall[own][osl * 512:(osl + 1) * 512, :].rearrange(
                                "(s p) j -> p s j", p=128), inc=[(nKV, 16)])

            def _cols(kc):
                ring = kc % 2
                if kc >= 2:
                    p.vector.wait(nQK, qk0 + 4 * nh * kc)
                    p.vector.wait(nEX, ex0 + 4 * nh * kc)
                for hi2, h2 in enumerate(grp):
                    p.vector.op("tensor_add",
                                bcol[:, ring * 2 + hi2: ring * 2 + hi2 + 1],
                                mcol[:, hi2:hi2 + 1],
                                zs_sb[:, slot * 16 + kc: slot * 16 + kc + 1],
                                inc=[(nBE, 1)])
            for kc in range(min(2, nkc)):
                _cols(kc)
            for idx in range(4 * nh):
                p.vector.wait(nEX, ex0 + idx + 1)
                sub = idx % 4
                p.vector.op("tensor_mul", Pbuf[:, (idx % 3) * 512:(idx % 3 + 1) * 512],
                            Pbuf[:, (idx % 3) * 512:(idx % 3 + 1) * 512],
                            dmask_sb[:, sub * 512:(sub + 1) * 512], inc=[(nMK, 1)])
            for kc in range(2, nkc):
                _cols(kc)

            for idx, t_ in enumerate(tiles):
                hi = t_[1]
                if t_[0] == "diag":
                    bias_ap = mcol[:, hi:hi + 1]
                else:
                    kc_ = t_[4]
                    bias_ap = bcol[:, (kc_ % 2) * 2 + hi: (kc_ % 2) * 2 + hi + 1]
                    p.scalar.wait(nBE, be0 + kc_ * nh + hi + 1)
                p.scalar.wait(nQK, qk0 + idx + 1)
                if idx >= 3:
                    p.scalar.wait(nAV, av0 + idx - 2)
                p.scalar.op("activation", Pbuf[:, (idx % 3) * 512:(idx % 3 + 1) * 512],
                            ctx.pv(4 + idx % 2), AF.Exp, scale=SC128,
                            bias=bias_ap, inc=[(nEX, 1)])

            ndiag = 4 * nh
            for idx, t_ in enumerate(tiles):
                kind, hi, h, sub = t_[0], t_[1], t_[2], t_[3]
                g = h // 4
                if kind == "diag":
                    lhs = sko_sb[:, g * T + slot * 512 + sub * 128:
                                 g * T + slot * 512 + (sub + 1) * 128]
                else:
                    kc = t_[4]
                    lhs = kvr[kc % 2][:, g * 512 + sub * 128:
                                      g * 512 + (sub + 1) * 128]
                    p.tensor.wait(nKV, kv0 + 32 * (kc + 1))
                if idx >= 2:
                    p.tensor.wait(nEX, ex0 + idx - 1)
                p.tensor.op("matmul", ctx.pv(4 + idx % 2), lhs,
                            sq_sb[:, h * T + qsl.start: h * T + qsl.start + 512],
                            start=True, stop=True, inc=[(nQK, 1)])
                for cidx in ([idx - 1] if idx >= 1 else []) + ([idx] if idx == ntile - 1 else []):
                    ct = tiles[cidx]
                    chi, ch = ct[1], ct[2]
                    cg = ch // 4
                    if cidx < ndiag:
                        p.tensor.wait(nMK, mk0 + cidx + 1)
                    else:
                        p.tensor.wait(nEX, ex0 + cidx + 1)
                    pb = Pbuf[:, (cidx % 3) * 512:(cidx % 3 + 1) * 512]
                    st = (cidx == h_first[ch])
                    sp = (cidx == h_last[ch])
                    p.tensor.matmul(ctx.pv(2 + chi)[0:1, :], ctx.ones_col_bf, pb,
                                    start=st, stop=sp)
                    if ct[0] == "diag":
                        vlhs = vo_sb[:, (slot * 4 + ct[3]) * 512 + cg * 128:
                                     (slot * 4 + ct[3]) * 512 + (cg + 1) * 128]
                    else:
                        vlhs = kvr[ct[4] % 2][:, 2048 + ct[3] * 512 + cg * 128:
                                              2048 + ct[3] * 512 + (cg + 1) * 128]
                    p.tensor.op("matmul", ctx.pv(chi), vlhs, pb,
                                start=st, stop=sp, inc=[(nAV, 1)])

            p.barrier()
            for hi, h in enumerate(grp):
                p.barrier()
                p.vector.reciprocal(linv, ctx.pv(2 + hi)[0:1, :])
                p.barrier()
                p.tensor.matmul(ctx.pv(4), ctx.ones_row_f, linv, start=True,
                                stop=True)
                p.barrier()
                p.vector.tensor_copy(linv_bc, ctx.pv(4))
                p.barrier()
                p.vector.tensor_mul(t1f, ctx.pv(hi), linv_bc)
                p.barrier()
                osl = slice(h * T + slot * 512, h * T + slot * 512 + 512)
                p.vector.scalar_tensor_tensor(OG[:, osl], t1f,
                                              0.5 if mode == "a0" else 1.0,
                                              OG[:, osl], ALU.mult, ALU.add)
            p.barrier()

    if mode == "a0":
        emit_proj_resid(ctx, OG, Wo, xsh, xout, T, 1.0, sO)
    else:
        emit_proj_resid(ctx, OG, Wo, xsh, xout, T, 0.5, sO,
                        h2_in=hidden_in, h2_out=h2_out)
    p.sync.wait(sO)
    p.gpsimd.wait(sO)
    return ctx


def build_a0b():
    return build_flash("a0")




def emit_headnorm(ctx, buf, nheads, T, w_col, eps=1e-6):
    p = ctx.p
    nt = T // 512
    bsx = ctx.scratch("hn_bsx", [128, 512], BF16)
    row1 = ctx.scratch("hn_row", [1, 512], F32)
    p.barrier()
    for h in range(nheads):
        for tt in range(nt):
            sl = slice(h * T + tt * 512, h * T + tt * 512 + 512)
            p.barrier()
            p.scalar.activation(bsx, buf[:, sl], AF.Square)
            p.barrier()
            p.tensor.matmul(ctx.prow(0)[:, 0:512], ctx.ones_col_bf, bsx,
                            start=True, stop=True)
            p.barrier()
            p.scalar.activation(row1, ctx.prow(0)[:, 0:512], AF.Ln,
                                bias=ctx.cbias(eps, 1), scale=1.0 / 128.0)
            p.scalar.activation(row1, row1, AF.Exp, scale=-0.5)
            p.barrier()
            p.tensor.matmul(ctx.pfull(1)[:, 0:512], ctx.ones_row_f, row1,
                            start=True, stop=True)
            p.barrier()
            p.vector.tensor_mul(buf[:, sl], buf[:, sl], ctx.pfull(1)[:, 0:512])
            p.vector.tensor_single_scalar(buf[:, sl], buf[:, sl], w_col,
                                          ALU.mult)
    p.barrier()


def build_f1a(ctx=None):
    if ctx is None:
        ctx = Ctx("f1a", bf16_banks=2)
    p = ctx.p
    T = TPC
    NH, NG = 16, 4
    xsh = ctx.din("xsh", [DM, T], F32)
    anw = ctx.din("anw", [128, DCH], F32)
    qnw = ctx.din("qnw", [128, 1], F32)
    knw = ctx.din("knw", [128, 1], F32)
    qwT = ctx.din("qwT", [DM, DM], F32)
    kwT = ctx.din("kwT", [DM, 512], F32)
    vwT = ctx.din("vwT", [DM, 512], F32)
    cosd = ctx.din("cosd", [128, T], BF16)
    sind = ctx.din("sind", [128, T], BF16)
    PSW = ctx.din("PSW", [128, 128], BF16)
    IDT = ctx.din("IDT", [128, 128], BF16)
    xout = ctx.dout("xout", [DM, T], F32)
    hidden_out = ctx.dout("hidden_out", [DM, T], F32)
    sq_out = ctx.dout("sq_out", [DM, T], BF16)
    sk_out = ctx.dout("sk_out", [512, T], BF16)
    vtok_out = ctx.dout("vtok_out", [T, 512], BF16)
    kmax_out = ctx.dout("kmax_out", [1, NG], F32)

    ctx.alloc_consts()
    h_bf = ctx.sb("h_bf", [128, DCH * T], BF16)
    h1_bf = ctx.sb("h1_bf", [128, DCH * T], BF16)
    q_bf = ctx.sb("q_bf", [128, NH * T], BF16)
    ctx.xsq_buf = q_bf[:, 0:DCH * T]
    k_bf = ctx.sb("k_bf", [128, NG * T], BF16)
    v_bf = ctx.sb("v_bf", [128, NG * T], BF16)
    sq_t = [ctx.sb("sq_t", [128, T], BF16) for _ in range(2)]
    sk_sb = ctx.sb("sk_sb", [128, NG * T], BF16)
    vtok = ctx.sb("vtok", [128, 8 * 512], BF16)
    cos_sb = ctx.sb("cos_sb", [128, T], BF16)
    sin_sb = ctx.sb("sin_sb", [128, T], BF16)
    psw_sb = ctx.sb("psw_sb", [128, 128], BF16)
    idt_sb = ctx.sb("idt_sb", [128, 128], BF16)
    anw_sb = ctx.sb("anw_sb", [128, DCH], F32)
    qnw_sb = ctx.sb("qnw_sb", [128, 1], F32)
    knw_sb = ctx.sb("knw_sb", [128, 1], F32)
    km_sb = ctx.sb("km_sb", [1, NG], F32)
    bsx = ctx.scratch("hn_bsx", [128, 512], BF16)
    kn_row = ctx.scratch("sn_mean", [1, T], F32)

    sL = p.sem("fL")
    sO = p.sem("fO")

    p.sync.op("dma_start", out=cos_sb, in_=cosd, inc=[(sL, 16)])
    p.sync.op("dma_start", out=sin_sb, in_=sind, inc=[(sL, 16)])
    p.sync.op("dma_start", out=psw_sb, in_=PSW, inc=[(sL, 16)])
    p.sync.op("dma_start", out=idt_sb, in_=IDT, inc=[(sL, 16)])
    p.sync.op("dma_start", out=anw_sb, in_=anw, inc=[(sL, 16)])
    p.sync.op("dma_start", out=qnw_sb, in_=qnw, inc=[(sL, 16)])
    p.sync.op("dma_start", out=knw_sb, in_=knw, inc=[(sL, 16)])
    p.sync.wait(sL)
    p.barrier()

    emit_stream_norm(ctx, xsh, h_bf, T, resid_xout=xout, resid_scale=0.5,
                     h_f32_out=hidden_out)
    emit_norm_fm(ctx, h_bf, h1_bf, T, rms_only=True, weight_col=anw_sb,
                 eps=1e-6)
    emit_proj(ctx, h1_bf, qwT, q_bf, DM, T, "q")
    emit_proj(ctx, h1_bf, kwT, k_bf, 512, T, "k")
    emit_proj(ctx, h1_bf, vwT, v_bf, 512, T, "v")
    emit_headnorm(ctx, q_bf, NH, T, qnw_sb)
    emit_headnorm(ctx, k_bf, NG, T, knw_sb)

    p.barrier()
    for g in range(NG):
        for tt in range(2):
            p.barrier()
            p.scalar.activation(bsx, k_bf[:, g * T + tt * 512:
                                          g * T + tt * 512 + 512], AF.Square)
            p.barrier()
            p.tensor.matmul(ctx.prow(0)[:, tt * 512:(tt + 1) * 512],
                            ctx.ones_col_bf, bsx, start=True, stop=True)
        p.barrier()
        p.vector.tensor_copy(kn_row, ctx.prow(0))
        p.barrier()
        p.vector.reduce_max(km_sb[:, g:g + 1], kn_row, axis=AX.X)
    p.barrier()
    p.sync.op("dma_start", out=kmax_out, in_=km_sb, inc=[(sO, 16)])

    for h in range(NH):
        b = h % 2
        if h >= 2:
            p.sync.wait(sO)
        emit_rope_head(ctx, q_bf[:, h * T:(h + 1) * T], sq_t[b], T,
                       cos_sb, sin_sb, psw_sb)
        p.sync.op("dma_start", out=sq_out[h * 128:(h + 1) * 128, :],
                  in_=sq_t[b], inc=[(sO, 16)])
    p.barrier()
    for g in range(NG):
        emit_rope_head(ctx, k_bf[:, g * T:(g + 1) * T],
                       sk_sb[:, g * T:(g + 1) * T], T, cos_sb, sin_sb, psw_sb)
        p.sync.op("dma_start", out=sk_out[g * 128:(g + 1) * 128, :],
                  in_=sk_sb[:, g * T:(g + 1) * T], inc=[(sO, 16)])

    p.barrier()
    ntr = 0
    sTT = p.sem("ftt")
    sTV = p.sem("ftv")
    for tt in range(8):
        for g in range(NG):
            half = ntr % 2
            bview = ctx.bpool[0][:, half * 1024: half * 1024 + 128]
            if ntr >= 2:
                p.tensor.wait(sTV, ntr - 1)
            p.tensor.op("transpose", bview,
                        v_bf[:, g * T + tt * 128: g * T + (tt + 1) * 128],
                        idt_sb, inc=[(sTT, 1)])
            ntr += 1
            p.vector.wait(sTT, ntr)
            p.vector.op("tensor_copy",
                        vtok[:, tt * 512 + g * 128: tt * 512 + (g + 1) * 128],
                        bview, inc=[(sTV, 1)])
    p.barrier()
    for tt in range(8):
        p.sync.op("dma_start", out=vtok_out[tt * 128:(tt + 1) * 128, :],
                  in_=vtok[:, tt * 512:(tt + 1) * 512], inc=[(sO, 16)])
    p.sync.wait(sO)
    p.gpsimd.wait(sO)
    return ctx


def build_moe(debug=False, ctx=None):
    if ctx is None:
        ctx = Ctx("moe", bf16_banks=2)
    p = ctx.p
    T = TPC
    E, NI = 8, 11
    xsh = ctx.din("xsh", [DM, T], F32)
    h2sh = ctx.din("h2sh", [DM, T], F32)
    fnw = ctx.din("fnw", [128, DCH], F32)
    gTf = ctx.din("gTf", [DM, E], F32)
    wgT = ctx.din("wgT", [E, DM, 1408], F32)
    wuT = ctx.din("wuT", [E, DM, 1408], F32)
    wdT = ctx.din("wdT", [E, 1408, DM], F32)
    IDTF = ctx.din("IDTF", [128, 128], F32)
    SEL = ctx.din("SEL", [E, E * 128], F32)
    xout = ctx.dout("xout", [DM, T], F32)
    if debug:
        dbg_lg = ctx.dout("dbg_lg", [128, 64], F32)
        dbg_c8 = ctx.dout("dbg_c8", [E, T], F32)
        dbg_acc = ctx.dout("dbg_acc", [DM, T], F32)
        dbg_h2n = ctx.dout("dbg_h2n", [DM, T], F32)

    ctx.alloc_consts()
    h2n_bf = ctx.sb("h2n_bf", [128, DCH * T], BF16)
    moe_acc = ctx.sb("moe_acc", [128, DCH * T], F32)
    mid = ctx.sb("mid", [128, NI * T], BF16)
    hin = [ctx.sb("hin", [128, T], F32) for _ in range(2)]
    h2n_f = ctx.sb("h2n_f", [128, T], F32)
    rstd_bc = ctx.sb("rstd_bc", [128, T], F32)
    gT_sb = ctx.sb("gT_sb", [128, DCH * E], F32)
    fnw_sb = ctx.sb("fnw_sb", [128, DCH], F32)
    idtf_sb = ctx.sb("idtf_sb", [128, 128], F32)
    idt_sb = ctx.sb("idt_sb", [128, 128], BF16)
    comb_bf = ctx.sb("comb_bf", [128, E], BF16)
    sel_sb = ctx.sb("sel_sb", [E, E * 128], F32)
    lgall = ctx.sb("lgall", [128, 8 * E], F32)
    c8 = ctx.sb("c8", [E, T], F32)
    cbc = ctx.sb("cbc", [128, T], BF16)
    wga = [ctx.sb("wga", [128, DCH * 128], BF16) for _ in range(2)]
    wua = [ctx.sb("wua", [128, DCH * 128], BF16) for _ in range(2)]
    wda = [ctx.sb("wda", [128, NI * 128], BF16) for _ in range(2)]
    s1 = ctx.sb("s1", [128, 512], BF16)
    eus = ctx.sb("eus", [128, 512], BF16)
    xsq = ctx.sb("msq", [128, T], BF16)
    rstd_r = ctx.scratch("sn_rstd", [1, T], F32)
    t8 = ctx.sb("t8", [128, E], F32)
    m1c = ctx.sb("m1c", [128, 1], F32)
    m2c = ctx.sb("m2c", [128, 1], F32)
    zc = ctx.sb("zc", [128, 1], F32)
    comb = ctx.sb("comb", [128, E], F32)

    sL = p.sem("mL")
    sX = p.sem("mX")
    sO = p.sem("mO")
    sW = [p.sem("mw0"), p.sem("mw1")]
    sWd = [p.sem("md0"), p.sem("md1")]

    p.sync.op("dma_start", out=gT_sb,
              in_=gTf.rearrange("(c p) e -> p c e", p=128), inc=[(sL, 16)])
    p.sync.op("dma_start", out=fnw_sb, in_=fnw, inc=[(sL, 16)])
    p.sync.op("dma_start", out=idtf_sb, in_=IDTF, inc=[(sL, 16)])
    p.gpsimd.op("dma_start", out=idt_sb, in_=IDTF, inc=[(sL, 16)])
    p.sync.op("dma_start", out=sel_sb, in_=SEL, inc=[(sL, 16)])
    p.sync.wait(sL)
    p.barrier()

    for c in range(DCH):
        b = c % 2
        p.sync.op("dma_start", out=hin[b], in_=h2sh[c * 128:(c + 1) * 128, :],
                  inc=[(sX, 16)])
        p.scalar.wait(sX, 16 * (c + 1))
        p.scalar.activation(xsq, hin[b], AF.Square)
        p.barrier()
        for tt in range(2):
            p.tensor.matmul(ctx.prow(0)[:, tt * 512:(tt + 1) * 512],
                            ctx.ones_col_bf, xsq[:, tt * 512:(tt + 1) * 512],
                            start=(c == 0), stop=(c == DCH - 1))
        p.barrier()
    p.vector.tensor_scalar_mul(rstd_r, ctx.prow(0), 1.0 / DM)
    p.barrier()
    p.scalar.activation(rstd_r, rstd_r, AF.Ln, bias=ctx.cbias(1e-6, 1), scale=1.0)
    p.scalar.activation(rstd_r, rstd_r, AF.Exp, scale=-0.5)
    p.barrier()
    for tt in range(2):
        p.tensor.matmul(ctx.pfull(0)[:, tt * 512:(tt + 1) * 512],
                        ctx.ones_row_f, rstd_r[:, tt * 512:(tt + 1) * 512],
                        start=True, stop=True)
    p.barrier()
    p.vector.tensor_copy(rstd_bc, ctx.pfull(0))
    p.barrier()

    for c in range(DCH):
        b = c % 2
        p.sync.op("dma_start", out=hin[b], in_=h2sh[c * 128:(c + 1) * 128, :],
                  inc=[(sX, 16)])
        p.vector.wait(sX, 16 * (DCH + c + 1))
        p.barrier()
        p.vector.tensor_mul(h2n_f, hin[b], rstd_bc)
        p.vector.tensor_single_scalar(h2n_f, h2n_f, fnw_sb[:, c:c + 1], ALU.mult)
        p.vector.tensor_copy(h2n_bf[:, c * T:(c + 1) * T], h2n_f)
        p.barrier()
        for ts in range(8):
            p.tensor.op("matmul", ctx.pv(0)[:, ts * E:(ts + 1) * E],
                        h2n_f[:, ts * 128:(ts + 1) * 128],
                        gT_sb[:, c * E:(c + 1) * E],
                        start=(c == 0 and ts == 0), stop=(c == DCH - 1),
                        skip_group_check=True)
        p.barrier()
    p.vector.tensor_copy(lgall, ctx.pv(0)[:, 0:64])
    p.barrier()

    nm1 = ctx.sb("nm1", [128, 1], F32)
    for ts in range(8):
        lg = lgall[:, ts * E:(ts + 1) * E]
        p.barrier()
        p.vector.reduce_max(nm1, lg, axis=AX.X, negate=True)
        p.vector.reduce_max(m1c, lg, axis=AX.X)
        p.barrier()
        p.scalar.activation(t8, lg, AF.Exp, bias=nm1)
        p.barrier()
        p.vector.reduce_sum(zc, t8, axis=AX.X)
        p.barrier()
        p.vector.reciprocal(zc, zc)
        p.barrier()
        p.vector.tensor_single_scalar(t8, t8, zc, ALU.mult)
        p.barrier()
        p.vector.tensor_single_scalar(comb, lg, m1c, ALU.is_ge)
        p.barrier()
        p.vector.scalar_tensor_tensor(comb, comb, -1e30, lg, ALU.mult, ALU.add)
        p.barrier()
        p.vector.reduce_max(m2c, comb, axis=AX.X)
        p.barrier()
        p.vector.tensor_single_scalar(comb, lg, m2c, ALU.is_ge)
        p.barrier()
        p.vector.tensor_mul(t8, t8, comb)
        p.barrier()
        p.vector.reduce_sum(zc, t8, axis=AX.X)
        p.barrier()
        p.vector.tensor_scalar_add(zc, zc, 1e-9)
        p.vector.reciprocal(zc, zc)
        p.barrier()
        p.vector.tensor_single_scalar(comb, t8, zc, ALU.mult)
        p.barrier()
        p.vector.tensor_copy(comb_bf, comb)
        p.barrier()
        p.tensor.op("transpose", ctx.bpool[0][0:8, 0:128], comb_bf, idt_sb)
        p.barrier()
        p.vector.tensor_copy(c8[:, ts * 128:(ts + 1) * 128],
                             ctx.bpool[0][0:8, 0:128])
        p.barrier()

    gz = p.sem("mgz")
    p.gpsimd.op("memset", moe_acc, 0.0, inc=[(gz, 1)])
    p.barrier()
    for e in range(E):
        p.barrier()
        for tt in range(2):
            p.tensor.matmul(ctx.pv(4), sel_sb[:, e * 128:(e + 1) * 128],
                            c8[:, tt * 512:(tt + 1) * 512], start=True, stop=True)
            p.barrier()
            p.vector.tensor_copy(cbc[:, tt * 512:(tt + 1) * 512], ctx.pv(4))
            p.barrier()
        for it in range(NI):
            b = it % 2
            p.gpsimd.op("dma_start", out=wga[b],
                        in_=wgT[e][:, it * 128:(it + 1) * 128].rearrange(
                            "(c p) f -> p c f", p=128), inc=[(sW[b], 16)])
            p.gpsimd.op("dma_start", out=wua[b],
                        in_=wuT[e][:, it * 128:(it + 1) * 128].rearrange(
                            "(c p) f -> p c f", p=128), inc=[(sW[b], 16)])
            p.tensor.wait(sW[b])
            p.barrier()
            for tt in range(2):
                for c in range(DCH):
                    rsl = slice(c * T + tt * 512, c * T + tt * 512 + 512)
                    p.tensor.matmul(ctx.pv(0 + tt), wga[b][:, c * 128:(c + 1) * 128],
                                    h2n_bf[:, rsl], start=(c == 0),
                                    stop=(c == DCH - 1))
                for c in range(DCH):
                    rsl = slice(c * T + tt * 512, c * T + tt * 512 + 512)
                    p.tensor.matmul(ctx.pv(2 + tt), wua[b][:, c * 128:(c + 1) * 128],
                                    h2n_bf[:, rsl], start=(c == 0),
                                    stop=(c == DCH - 1))
            p.barrier()
            for tt in range(2):
                p.scalar.activation(s1, ctx.pv(0 + tt), AF.Silu)
                p.barrier()
                p.vector.tensor_mul(eus, ctx.pv(2 + tt),
                                    cbc[:, tt * 512:(tt + 1) * 512])
                p.barrier()
                p.vector.tensor_mul(mid[:, it * T + tt * 512: it * T + tt * 512 + 512],
                                    s1, eus)
                p.barrier()
        for ot in range(DCH):
            b = ot % 2
            p.gpsimd.op("dma_start", out=wda[b],
                        in_=wdT[e][:, ot * 128:(ot + 1) * 128].rearrange(
                            "(i p) f -> p i f", p=128), inc=[(sWd[b], 16)])
            p.tensor.wait(sWd[b])
            p.barrier()
            for tt in range(2):
                for it in range(NI):
                    rsl = slice(it * T + tt * 512, it * T + tt * 512 + 512)
                    p.tensor.matmul(ctx.pv(4 + tt), wda[b][:, it * 128:(it + 1) * 128],
                                    mid[:, rsl], start=(it == 0),
                                    stop=(it == NI - 1))
            p.barrier()
            for tt in range(2):
                asl = slice(ot * T + tt * 512, ot * T + tt * 512 + 512)
                p.vector.scalar_tensor_tensor(moe_acc[:, asl], ctx.pv(4 + tt),
                                              1.0, moe_acc[:, asl],
                                              ALU.mult, ALU.add)
            p.barrier()
    p.barrier()
    if debug:
        p.sync.op("dma_start", out=dbg_lg, in_=lgall, inc=[(sO, 16)])
        p.sync.op("dma_start", out=dbg_c8, in_=c8, inc=[(sO, 16)])
        for c in range(DCH):
            p.sync.op("dma_start", out=dbg_acc[c * 128:(c + 1) * 128, :],
                      in_=moe_acc[:, c * T:(c + 1) * T], inc=[(sO, 16)])
        p.gpsimd.op("dma_start", out=dbg_h2n.rearrange("(c p) t -> p c t", p=128),
                    in_=h2n_bf, inc=[(sO, 16)])
        p.sync.wait(sO)
        p.barrier()
    for c in range(DCH):
        b = c % 2
        p.sync.op("dma_start", out=hin[b], in_=xsh[c * 128:(c + 1) * 128, :],
                  inc=[(sX, 16)])
        p.vector.wait(sX)
        if c >= 1:
            p.vector.wait(sO)
        p.vector.scalar_tensor_tensor(h2n_f, moe_acc[:, c * T:(c + 1) * T], 0.5,
                                      hin[b], ALU.mult, ALU.add)
        p.barrier()
        p.sync.op("dma_start", out=xout[c * 128:(c + 1) * 128, :], in_=h2n_f,
                  inc=[(sO, 16)])
    p.sync.wait(sO)
    p.gpsimd.wait(sO)
    return ctx




def build_a1a(ctx=None):
    if ctx is None:
        ctx = Ctx("a1a", bf16_banks=2)
    p = ctx.p
    T = TPC
    xsh = ctx.din("xsh", [DM, T], F32)
    Ws = ctx.din("Ws", [DM, 256], F32)
    Wa = ctx.din("Wa", [DM, 256], F32)
    IDT = ctx.din("IDT", [128, 128], BF16)
    sdm_out = ctx.dout("sdm_out", [256, T], BF16)
    adm_out = ctx.dout("adm_out", [256, T], BF16)
    htok_out = ctx.dout("htok_out", [T, DM], BF16)
    amax_out = ctx.dout("amax_out", [1, 1], F32)

    ctx.alloc_consts()
    h_bf = ctx.sb("h_bf", [128, DCH * T], BF16)
    s_bf = ctx.sb("s_bf", [128, 2 * T], BF16)
    a_bf = ctx.sb("a_bf", [128, 2 * T], BF16)
    htk = ctx.sb("htk", [128, DM], BF16)
    idt_sb = ctx.sb("idt_sb", [128, 128], BF16)
    am_sb = ctx.sb("am_sb", [1, 1], F32)
    an_row = ctx.scratch("sn_mean", [1, T], F32)
    bsx = ctx.sb("bsx", [128, 512], BF16)

    sL = p.sem("aL")
    sO = p.sem("aO")

    p.gpsimd.op("dma_start", out=idt_sb, in_=IDT, inc=[(sL, 16)])
    p.gpsimd.wait(sL)
    emit_stream_norm(ctx, xsh, h_bf, T)
    emit_proj(ctx, h_bf, Ws, s_bf, 256, T, "s")
    emit_proj(ctx, h_bf, Wa, a_bf, 256, T, "a")
    for r in range(2):
        p.sync.op("dma_start", out=sdm_out[r * 128:(r + 1) * 128, :],
                  in_=s_bf[:, r * T:(r + 1) * T], inc=[(sO, 16)])
        p.sync.op("dma_start", out=adm_out[r * 128:(r + 1) * 128, :],
                  in_=a_bf[:, r * T:(r + 1) * T], inc=[(sO, 16)])

    p.barrier()
    for tt in range(2):
        for r in range(2):
            p.barrier()
            p.scalar.activation(bsx, a_bf[:, r * T + tt * 512:
                                          r * T + tt * 512 + 512], AF.Square)
            p.barrier()
            p.tensor.matmul(ctx.prow(0)[:, tt * 512:(tt + 1) * 512],
                            ctx.ones_col_bf, bsx, start=(r == 0), stop=(r == 1))
    p.barrier()
    p.vector.tensor_copy(an_row, ctx.prow(0))
    p.barrier()
    p.vector.reduce_max(am_sb, an_row, axis=AX.X)
    p.barrier()
    p.sync.op("dma_start", out=amax_out, in_=am_sb, inc=[(sO, 16)])

    ntr = 0
    sTT = p.sem("att")
    sTV = p.sem("atv")
    p.barrier()
    for tt in range(8):
        if tt >= 1:
            p.tensor.wait(sO)
        for c in range(DCH):
            half = ntr % 2
            bview = ctx.bpool[0][:, half * 1024: half * 1024 + 128]
            if ntr >= 2:
                p.tensor.wait(sTV, ntr - 1)
            p.tensor.op("transpose", bview,
                        h_bf[:, c * T + tt * 128: c * T + (tt + 1) * 128],
                        idt_sb, inc=[(sTT, 1)])
            ntr += 1
            p.vector.wait(sTT, ntr)
            if tt >= 1 and c == 0:
                p.vector.wait(sO)
            p.vector.op("tensor_copy", htk[:, c * 128:(c + 1) * 128],
                        bview, inc=[(sTV, 1)])
        p.sync.wait(sTV, ntr)
        p.sync.op("dma_start", out=htok_out[tt * 128:(tt + 1) * 128, :],
                  in_=htk, inc=[(sO, 16)])
    p.sync.wait(sO)
    p.gpsimd.wait(sO)
    return ctx


def build_a1b(ctx=None):
    if ctx is None:
        ctx = Ctx("a1b", bf16_banks=2)
    p = ctx.p
    T = TPC
    SC16 = 1.0 / 16.0
    xsh = ctx.din("xsh", [DM, T], F32)
    sdm = ctx.din("sdm", [256, T], BF16)
    aall = ctx.din("aall", [NCORES, 256, T], BF16)
    hall = ctx.din("hall", [NCORES, T, DM], BF16)
    aown = ctx.din("aown", [256, T], BF16)
    hown = ctx.din("hown", [T, DM], BF16)
    amx = ctx.din("amx", [1, 8], F32)
    zsel = ctx.din("zsel", [128, 32], F32)
    DMASK = ctx.din("DMASK", [4, 128, 512], BF16)
    WoT = ctx.din("WoT", [DM, DM], F32)
    bo = ctx.din("bo", [128, DCH], F32)
    xout = ctx.dout("xout", [DM, T], F32)

    ctx.alloc_consts()
    s_sb = ctx.sb("s_sb", [128, 2 * T], BF16)
    OC = ctx.sb("OC", [128, DCH * T], BF16)
    ao_sb = ctx.sb("ao_sb", [128, 2 * T], BF16)
    ho_sb = ctx.sb("ho_sb", [128, 8 * DM], BF16)
    kvr = [ctx.sb("kvr", [128, 2 * 512 + 4 * DM], BF16) for _ in range(2)]
    zs_sb = ctx.sb("zs_sb", [128, 32], F32)
    am_sb = ctx.sb("am_sb", [1, 8], F32)
    ar_sb = ctx.sb("ar_sb", [1, 1], F32)
    dmask_sb = ctx.sb("dmask_sb", [128, 4 * 512], BF16)
    bsx = ctx.sb("bsx", [128, 512], BF16)
    row1 = ctx.sb("row1", [1, 512], F32)
    mrow = ctx.sb("mrow", [1, 1], F32)
    mcol = ctx.sb("mcol", [128, 1], F32)
    bcol = ctx.sb("bcol", [128, 2], F32)
    Pbuf = ctx.sb("Pbuf", [128, 3 * 512], BF16)
    linv = ctx.sb("linv", [1, 512], F32)
    linv_bc = ctx.sb("linv_bc", [128, 512], F32)
    t1f = ctx.sb("t1f", [128, 512], F32)

    sL = p.sem("cL")
    sO = p.sem("cO")

    for r in range(2):
        p.sync.op("dma_start", out=s_sb[:, r * T:(r + 1) * T],
                  in_=sdm[r * 128:(r + 1) * 128, :], inc=[(sL, 16)])
        p.sync.op("dma_start", out=ao_sb[:, r * T:(r + 1) * T],
                  in_=aown[r * 128:(r + 1) * 128, :], inc=[(sL, 16)])
    p.sync.op("dma_start", out=ho_sb,
              in_=hown.rearrange("(s p) j -> p s j", p=128), inc=[(sL, 16)])
    p.sync.op("dma_start", out=zs_sb, in_=zsel, inc=[(sL, 16)])
    p.sync.op("dma_start", out=am_sb, in_=amx, inc=[(sL, 16)])
    p.sync.op("dma_start", out=dmask_sb,
              in_=DMASK.rearrange("s p t -> p s t"), inc=[(sL, 16)])
    p.sync.wait(sL)
    p.barrier()
    p.vector.reduce_max(ar_sb, am_sb, axis=AX.X)
    p.barrier()
    p.scalar.activation(ar_sb, ar_sb, AF.Ln, bias=ctx.cbias(1e-12, 1), scale=1.0)
    p.scalar.activation(ar_sb, ar_sb, AF.Exp, scale=0.5)
    p.barrier()

    JG = [[0, 1, 2], [3, 4, 5], [6, 7, 8], [9, 10, 11], [12, 13, 14], [15]]
    nQK = p.sem("cQK")
    nEX = p.sem("cEX")
    nMK = p.sem("cMK")
    nAV = p.sem("cAV")
    nBE = p.sem("cBE")
    nKV = p.sem("cKV")

    for slot in range(2):
        nkc = 8 if slot == 0 else 16
        p.barrier()
        for r in range(2):
            p.barrier()
            p.scalar.activation(bsx, s_sb[:, r * T + slot * 512:
                                          r * T + slot * 512 + 512], AF.Square)
            p.barrier()
            p.tensor.matmul(ctx.pv(4)[0:1, :], ctx.ones_col_bf, bsx,
                            start=(r == 0), stop=(r == 1))
        p.barrier()
        p.scalar.activation(row1, ctx.pv(4)[0:1, :], AF.Ln,
                            bias=ctx.cbias(1e-12, 1), scale=1.0)
        p.scalar.activation(row1, row1, AF.Exp, scale=0.5)
        p.barrier()
        p.vector.tensor_single_scalar(row1, row1, ar_sb[:, 0:1], ALU.mult)
        p.barrier()
        p.vector.reduce_max(mrow, row1, axis=AX.X)
        p.barrier()
        p.tensor.matmul(ctx.pv(4)[:, 0:1], ctx.ones_row_f, mrow,
                        start=True, stop=True)
        p.barrier()
        p.vector.tensor_scalar_mul(mcol, ctx.pv(4)[:, 0:1], -SC16)
        p.barrier()

        for gi, grp in enumerate(JG):
            tiles = []
            for sub in range(4):
                tiles.append(("diag", sub, None))
            for kc in range(nkc):
                for sub in range(4):
                    tiles.append(("main", sub, kc))
            ntile = len(tiles)
            qk0 = nQK.rel
            ex0 = nEX.rel
            av0 = nAV.rel
            be0 = nBE.rel
            kv0 = nKV.rel
            mk0 = nMK.rel
            first = 0
            last = ntile - 1

            for kc in range(nkc):
                b = kc % 2
                own = kc if kc <= 7 else 15 - kc
                osl = 0 if kc <= 7 else 1
                if kc >= 2:
                    p.gpsimd.wait(nAV, av0 + len(grp) * 4 * (kc - 1) + len(grp) * 4)
                p.gpsimd.op("dma_start", out=kvr[b][:, 0:1024],
                            in_=aall[own][:, osl * 512:(osl + 1) * 512].rearrange(
                                "(r p) t -> p r t", p=128), inc=[(nKV, 16)])
                p.gpsimd.op("dma_start", out=kvr[b][:, 1024:1024 + 4 * DM],
                            in_=hall[own][osl * 512:(osl + 1) * 512, :].rearrange(
                                "(s p) j -> p s j", p=128), inc=[(nKV, 16)])

            def _cols(kc):
                ring = kc % 2
                if kc >= 2:
                    p.vector.wait(nQK, qk0 + 4 * kc)
                    p.vector.wait(nEX, ex0 + 4 * kc)
                p.vector.op("tensor_add", bcol[:, ring:ring + 1], mcol,
                            zs_sb[:, slot * 16 + kc: slot * 16 + kc + 1],
                            inc=[(nBE, 1)])
            for kc in range(min(2, nkc)):
                _cols(kc)
            for idx in range(4):
                p.vector.wait(nEX, ex0 + idx + 1)
                p.vector.op("tensor_mul", Pbuf[:, (idx % 3) * 512:(idx % 3 + 1) * 512],
                            Pbuf[:, (idx % 3) * 512:(idx % 3 + 1) * 512],
                            dmask_sb[:, idx * 512:(idx + 1) * 512], inc=[(nMK, 1)])
            for kc in range(2, nkc):
                _cols(kc)

            for idx in range(ntile):
                t_ = tiles[idx]
                if t_[0] == "diag":
                    bias_ap = mcol
                else:
                    kc_ = t_[2]
                    bias_ap = bcol[:, (kc_ % 2):(kc_ % 2) + 1]
                    p.scalar.wait(nBE, be0 + kc_ + 1)
                p.scalar.wait(nQK, qk0 + idx + 1)
                if idx >= 3:
                    p.scalar.wait(nAV, av0 + (idx - 2) * len(grp))
                p.scalar.op("activation", Pbuf[:, (idx % 3) * 512:(idx % 3 + 1) * 512],
                            ctx.pv(4 + idx % 2), AF.Exp, scale=SC16,
                            bias=bias_ap, inc=[(nEX, 1)])

            for idx, t_ in enumerate(tiles):
                kind, sub, kc = t_
                if kind == "diag":
                    qlhs = [ao_sb[:, r * T + slot * 512 + sub * 128:
                                  r * T + slot * 512 + (sub + 1) * 128]
                            for r in range(2)]
                else:
                    qlhs = [kvr[kc % 2][:, r * 512 + sub * 128:
                                        r * 512 + (sub + 1) * 128]
                            for r in range(2)]
                    p.tensor.wait(nKV, kv0 + 32 * (kc + 1))
                if idx >= 2:
                    p.tensor.wait(nEX, ex0 + idx - 1)
                p.tensor.matmul(ctx.pv(4 + idx % 2), qlhs[0],
                                s_sb[:, 0 * T + slot * 512:
                                     0 * T + slot * 512 + 512],
                                start=True, stop=False)
                p.tensor.op("matmul", ctx.pv(4 + idx % 2), qlhs[1],
                            s_sb[:, 1 * T + slot * 512:
                                 1 * T + slot * 512 + 512],
                            start=False, stop=True, inc=[(nQK, 1)])
                for cidx in ([idx - 1] if idx >= 1 else []) + ([idx] if idx == ntile - 1 else []):
                    ct = tiles[cidx]
                    if cidx < 4:
                        p.tensor.wait(nMK, mk0 + cidx + 1)
                    else:
                        p.tensor.wait(nEX, ex0 + cidx + 1)
                    pb = Pbuf[:, (cidx % 3) * 512:(cidx % 3 + 1) * 512]
                    st = (cidx == first)
                    sp = (cidx == last)
                    if gi == 0:
                        p.tensor.matmul(ctx.pv(3)[0:1, :], ctx.ones_col_bf, pb,
                                        start=st, stop=sp)
                    for ji, jc in enumerate(grp):
                        if ct[0] == "diag":
                            vlhs = ho_sb[:, (slot * 4 + ct[1]) * DM + jc * 128:
                                         (slot * 4 + ct[1]) * DM + (jc + 1) * 128]
                        else:
                            vlhs = kvr[ct[2] % 2][:, 1024 + ct[1] * DM + jc * 128:
                                                  1024 + ct[1] * DM + (jc + 1) * 128]
                        mm_inc = [(nAV, 1)]
                        p.tensor.op("matmul", ctx.pv(ji), vlhs, pb,
                                    start=st, stop=sp, inc=mm_inc)

            p.barrier()
            if gi == 0:
                p.vector.reciprocal(linv, ctx.pv(3)[0:1, :])
                p.barrier()
                p.tensor.matmul(ctx.pv(4), ctx.ones_row_f, linv, start=True,
                                stop=True)
                p.barrier()
                p.vector.tensor_copy(linv_bc, ctx.pv(4))
                p.barrier()
            for ji, jc in enumerate(grp):
                p.barrier()
                p.vector.tensor_mul(t1f, ctx.pv(ji), linv_bc)
                p.barrier()
                p.vector.tensor_copy(OC[:, jc * T + slot * 512:
                                        jc * T + slot * 512 + 512], t1f)
            p.barrier()

    bo_sb = ctx.sb("bo_sb", [128, DCH], F32)
    sB = p.sem("cB")
    p.gpsimd.op("dma_start", out=bo_sb, in_=bo, inc=[(sB, 16)])
    p.vector.wait(sB)
    p.barrier()
    emit_proj_resid(ctx, OC, WoT, xsh, xout, T, 1.0, sO, bias_col=bo_sb)
    p.sync.wait(sO)
    p.gpsimd.wait(sO)
    return ctx



def host_prep(I):
    f32 = lambda a: np.ascontiguousarray(np.asarray(a, np.float32))
    bfa = lambda a: np.ascontiguousarray(np.asarray(a).astype(BF))
    H = {}
    W1T = f32(I["L0_ffn_W1"]).T
    W3T = f32(I["L0_ffn_W3"]).T

    def pack_w(WT):
        w = np.ascontiguousarray(WT).reshape(DCH, 128, 64, 64, 2)
        w = np.transpose(w, (3, 0, 1, 4, 2))
        return np.ascontiguousarray(w.reshape(64, DCH, 128, 128).astype(np.float32))

    H["W1p"] = pack_w(W1T)
    H["W3p"] = pack_w(W3T)
    H["W2T"] = np.ascontiguousarray(f32(I["L0_ffn_W2"]).T)
    H["H128"] = np.ascontiguousarray(_hadamard(128).astype(np.float32))
    H64 = _hadamard(64) / 64.0
    H64B = np.zeros((128, 128), np.float64)
    H64B[:64, :64] = H64
    H64B[64:, 64:] = H64
    H["H64B"] = np.ascontiguousarray(H64B.astype(np.float32))

    PSW = np.zeros((128, 128), np.float32)
    for m in range(128):
        PSW[(m + 64) % 128, m] = 1.0
    H["PSW"] = bfa(PSW)
    s_le_t = np.tril(np.ones((128, 128), np.float32)).T
    H["CUM"] = bfa(s_le_t)
    H["SUF"] = bfa(1.0 - s_le_t)
    H["IDT"] = np.eye(128, dtype=BF)
    H["IDTF"] = np.eye(128, dtype=np.float32)
    msk = []
    for sub in range(4):
        m_ = np.zeros((128, 512), np.float32)
        for ii in range(128):
            m_[ii, sub * 128 + ii:] = 1.0
        msk.append(m_)
    H["DMASK"] = bfa(np.stack(msk))
    H["zsel"] = []
    H["selpf"] = []
    for i in range(NCORES):
        c0, c1 = CORE_CHUNKS[i]
        z = np.zeros((128, 32), np.float32)
        sel = np.zeros((128, 32), np.float32)
        for slot, cch in ((0, c0), (1, c1)):
            for kc in range(16):
                if kc >= cch:
                    z[:, slot * 16 + kc] = -1e9
            for j in range(15):
                if j < cch:
                    sel[:, slot * 16 + j] = 1.0
        H["zsel"].append(z)
        H["selpf"].append(np.ascontiguousarray(sel))

    for nm, theta in (("r0", 10000.0), ("r1", 1000000.0)):
        cos, sin = _rope_tables(SEQ, 128, theta)
        cdup = bfa(np.concatenate([cos.T, cos.T], axis=0))
        sdup = bfa(np.concatenate([-sin.T, sin.T], axis=0))
        H[f"cos_{nm}"] = [np.ascontiguousarray(cdup[:, core_tokens(i)])
                          for i in range(NCORES)]
        H[f"sin_{nm}"] = [np.ascontiguousarray(sdup[:, core_tokens(i)])
                          for i in range(NCORES)]

    H["Wq"] = f32(I["L0_attn_Wq"])
    H["Wk"] = f32(I["L0_attn_Wk"])
    H["Wv"] = f32(I["L0_attn_Wv"])
    H["Wo0"] = f32(I["L0_attn_Wo"])
    H["anw"] = np.ascontiguousarray(f32(I["L1_ffn_attn_norm_w"]).reshape(DCH, 128).T)
    H["qnw"] = f32(I["L1_ffn_q_norm_w"]).reshape(128, 1)
    H["knw"] = f32(I["L1_ffn_k_norm_w"]).reshape(128, 1)
    H["qwT"] = np.ascontiguousarray(f32(I["L1_ffn_q_w"]).T)
    H["kwT"] = np.ascontiguousarray(f32(I["L1_ffn_k_w"]).T)
    H["vwT"] = np.ascontiguousarray(f32(I["L1_ffn_v_w"]).T)
    H["owT"] = np.ascontiguousarray(f32(I["L1_ffn_o_w"]).T)
    H["fnw"] = np.ascontiguousarray(f32(I["L1_ffn_ffn_norm_w"]).reshape(DCH, 128).T)
    H["gTf"] = np.ascontiguousarray(f32(I["L1_ffn_gate_w"]).T)
    H["wgT"] = np.ascontiguousarray(np.transpose(f32(I["L1_ffn_wg"]), (0, 2, 1)))
    H["wuT"] = np.ascontiguousarray(np.transpose(f32(I["L1_ffn_wu"]), (0, 2, 1)))
    H["wdT"] = np.ascontiguousarray(np.transpose(f32(I["L1_ffn_wd"]), (0, 2, 1)))
    SEL = np.zeros((8, 8 * 128), np.float32)
    for e in range(8):
        SEL[e, e * 128:(e + 1) * 128] = 1.0
    H["SEL"] = SEL
    H["Ws1"] = f32(I["L1_attn_Ws"])
    H["Wa1"] = f32(I["L1_attn_Wa"])
    H["WoT1"] = np.ascontiguousarray(f32(I["L1_attn_Wo"]).T)
    H["bo1"] = np.ascontiguousarray(f32(I["L1_attn_bo"]).reshape(DCH, 128).T)
    return H




def _emit_gather(ctx, src2d, dst2d, name):
    p = ctx.p
    cc = p.sem("cc_" + name)
    p.barrier()
    p.gpsimd.op("collective_compute", "AllGather", ALU.bypass,
                replica_groups=[list(range(NCORES))],
                ins=[src2d], outs=[dst2d], inc=[(cc, 1)])
    p.gpsimd.wait(cc)
    p.barrier()


def build_fused():
    ctx = Ctx("fused", bf16_banks=2)
    nc, p = ctx.nc, ctx.p
    T = TPC

    EXT = {}

    def ein(name, shape, dtype=F32):
        EXT[name] = nc.dram_tensor(name, shape, dtype, kind="ExternalInput").ap()
        return EXT[name]

    xsh0 = ein("xsh", [DM, T])
    xout_e = nc.dram_tensor("xout", [DM, T], F32, kind="ExternalOutput").ap()
    for nm, shp, dt in [
        ("W2T", [SEQ, DM], F32), ("H128", [128, 128], F32),
        ("W1p", [64, DCH, 128, 128], F32), ("W3p", [64, DCH, 128, 128], F32),
        ("H64B", [128, 128], F32),
        ("Wq", [DM, DM], F32), ("Wk", [DM, 512], F32), ("Wv", [DM, 512], F32),
        ("Wo0", [DM, DM], F32),
        ("cos_r0", [128, T], BF16), ("sin_r0", [128, T], BF16),
        ("cos_r1", [128, T], BF16), ("sin_r1", [128, T], BF16),
        ("CUM", [128, 128], BF16), ("SUF", [128, 128], BF16),
        ("IDT", [128, 128], BF16), ("PSW", [128, 128], BF16),
        ("IDTF", [128, 128], F32),
        ("DMASK", [4, 128, 512], BF16), ("zsel", [128, 32], F32),
        ("selpf", [128, 32], F32),
        ("anw", [128, DCH], F32), ("qnw", [128, 1], F32), ("knw", [128, 1], F32),
        ("qwT", [DM, DM], F32), ("kwT", [DM, 512], F32), ("vwT", [DM, 512], F32),
        ("owT", [DM, DM], F32), ("fnw", [128, DCH], F32), ("gTf", [DM, 8], F32),
        ("wgT", [8, DM, 1408], F32), ("wuT", [8, DM, 1408], F32),
        ("wdT", [8, 1408, DM], F32), ("SEL", [8, 8 * 128], F32),
        ("Ws1", [DM, 256], F32), ("Wa1", [DM, 256], F32),
        ("WoT1", [DM, DM], F32), ("bo1", [128, DCH], F32),
    ]:
        ein(nm, shp, dt)

    def dint(name, shape, dtype):
        return nc.dram_tensor(name, shape, dtype, kind="Internal").ap()

    def dint_sh(name, shape, dtype):
        return nc.dram_tensor(name, shape, dtype, kind="Internal",
                              addr_space="Shared").ap()

    G2b = dint("G2b", [16, 64, 128, 128], BF16)
    xA = dint("xA", [DM, T], F32)
    xB = dint("xB", [DM, T], F32)
    hidb = dint("hidb", [DM, T], F32)
    h2b = dint("h2b", [DM, T], F32)
    sqL = dint("sqL", [DM, T], BF16)
    skL = dint("skL", [512, T], BF16)
    vtL = dint("vtL", [T, 512], BF16)
    kmL = dint("kmL", [1, 4], F32)
    oglaL = dint("oglaL", [8, 16, 128, 128], BF16)
    acL = dint("acL", [8, 16, 128, 128], BF16)
    boffL = dint("boffL", [128, 32], F32)
    DL = dint("DL", [2, 128, 4], F32)
    UL = dint("UL", [2, 4, 128, 128], F32)
    skg = dint_sh("skg", [NCORES * 512, T], BF16)
    vg = dint_sh("vg", [NCORES * T, 512], BF16)
    kmg = dint_sh("kmg", [NCORES, 4], F32)
    Dgf = dint_sh("Dgf", [NCORES * 2 * 128, 4], F32)
    Ugf = dint_sh("Ugf", [NCORES * 2 * 4 * 128, 128], F32)
    aL = dint("aL", [256, T], BF16)
    htL = dint("htL", [T, DM], BF16)
    amL = dint("amL", [1, 1], F32)
    ag = dint_sh("ag", [NCORES * 256, T], BF16)
    hg = dint_sh("hg", [NCORES * T, DM], BF16)
    amg = dint_sh("amg", [NCORES, 1], F32)

    ctx.alloc_consts()
    for v in (1e-5, 1e-6, 1e-12):
        ctx.cbias(v, 1)
    ctx.enable_arena(206 * 1024)

    kmx_view = bass.AP(kmg.tensor, 0, [[1, 4], [4, 8]])
    skg_v = skg.rearrange("(n r) t -> n r t", n=NCORES)
    vg_v = vg.rearrange("(n r) t -> n r t", n=NCORES)
    Dg_v = Dgf.rearrange("(n a b) c -> n a b c", n=NCORES, a=2)
    Ug_v = Ugf.rearrange("(n a g b) c -> n a g b c", n=NCORES, a=2, g=4)
    ag_v = ag.rearrange("(n r) t -> n r t", n=NCORES)
    hg_v = hg.rearrange("(n r) t -> n r t", n=NCORES)
    amx_view = amg.rearrange("c o -> (c o)")

    def phase(tag, builder, iomap, *args):
        ctx.phase_reset(tag)
        ctx.iomap = iomap
        builder(*args, ctx=ctx)
        ctx.iomap = None
        p.barrier()

    shared0 = {k: EXT[k] for k in ("CUM", "SUF", "IDT", "PSW", "DMASK",
                                   "zsel", "selpf", "IDTF", "SEL")}

    phase("g", build_gprep, {"W2T": EXT["W2T"], "H128": EXT["H128"], "G2": G2b})

    def f0(tag, xin, xo):
        phase(tag, build_f0, {"xsh": xin, "W1p": EXT["W1p"], "W3p": EXT["W3p"],
                              "G2": G2b, "H64B": EXT["H64B"], "xout": xo})

    def a0(tag, xin, xo):
        phase(tag + "a", build_a0a,
              {"xsh": xin, "Wq": EXT["Wq"], "Wk": EXT["Wk"], "Wv": EXT["Wv"],
               "cosd": EXT["cos_r0"], "sind": EXT["sin_r0"], **shared0,
               "sq_out": sqL, "sk_out": skL, "vtok_out": vtL,
               "ogla_out": oglaL, "acache_out": acL, "boff_out": boffL,
               "D_out": DL, "U_out": UL, "kmax_out": kmL})
        _emit_gather(ctx, skL, skg, tag + "sk")
        _emit_gather(ctx, vtL, vg, tag + "v")
        _emit_gather(ctx, kmL, kmg, tag + "km")
        _emit_gather(ctx, DL.rearrange("a b c -> (a b) c"), Dgf, tag + "D")
        _emit_gather(ctx, UL.rearrange("a g b c -> (a g b) c"), Ugf, tag + "U")
        phase(tag + "b", lambda ctx=None: build_flash("a0", ctx=ctx),
              {"xsh": xin, "sq_in": sqL, "skall": skg_v, "vall": vg_v,
               "skown": skL, "vown": vtL, "ogla": oglaL, "acache": acL,
               "boffe": boffL, "Dg": Dg_v, "Ug": Ug_v, "kmx": kmx_view,
               **shared0, "Wo": EXT["Wo0"], "xout": xo})

    def f1(tag, xin, xmid1, xmid2, xo):
        phase(tag + "a", build_f1a,
              {"xsh": xin, "anw": EXT["anw"], "qnw": EXT["qnw"],
               "knw": EXT["knw"], "qwT": EXT["qwT"], "kwT": EXT["kwT"],
               "vwT": EXT["vwT"], "cosd": EXT["cos_r1"], "sind": EXT["sin_r1"],
               **shared0, "xout": xmid1, "hidden_out": hidb, "sq_out": sqL,
               "sk_out": skL, "vtok_out": vtL, "kmax_out": kmL})
        _emit_gather(ctx, skL, skg, tag + "sk")
        _emit_gather(ctx, vtL, vg, tag + "v")
        _emit_gather(ctx, kmL, kmg, tag + "km")
        phase(tag + "b", lambda ctx=None: build_flash("f1", ctx=ctx),
              {"xsh": xmid1, "sq_in": sqL, "skall": skg_v, "vall": vg_v,
               "skown": skL, "vown": vtL, "hidden_in": hidb, "h2_out": h2b,
               "kmx": kmx_view, **shared0, "Wo": EXT["owT"], "xout": xmid2})
        phase(tag + "c", lambda ctx=None: build_moe(False, ctx=ctx),
              {"xsh": xmid2, "h2sh": h2b, "fnw": EXT["fnw"], "gTf": EXT["gTf"],
               "wgT": EXT["wgT"], "wuT": EXT["wuT"], "wdT": EXT["wdT"],
               **shared0, "xout": xo})

    def a1(tag, xin, xo):
        phase(tag + "a", build_a1a,
              {"xsh": xin, "Ws": EXT["Ws1"], "Wa": EXT["Wa1"], **shared0,
               "sdm_out": sqL[0:256, :] if False else aL,
               "adm_out": aL, "htok_out": htL, "amax_out": amL})
        pass

    sdmL = dint("sdmL", [256, T], BF16)

    def a1(tag, xin, xo):
        phase(tag + "a", build_a1a,
              {"xsh": xin, "Ws": EXT["Ws1"], "Wa": EXT["Wa1"], **shared0,
               "sdm_out": sdmL, "adm_out": aL, "htok_out": htL,
               "amax_out": amL})
        _emit_gather(ctx, aL, ag, tag + "a")
        _emit_gather(ctx, htL, hg, tag + "h")
        _emit_gather(ctx, amL, amg, tag + "am")
        phase(tag + "b", build_a1b,
              {"xsh": xin, "sdm": sdmL, "aall": ag_v, "hall": hg_v,
               "aown": aL, "hown": htL, "amx": amx_view, **shared0,
               "WoT": EXT["WoT1"], "bo": EXT["bo1"], "xout": xo})

    f0("p1", xsh0, xA)
    a0("p2", xA, xB)
    f0("p3", xB, xA)
    f1("p4", xA, xB, xA, xB)
    a1("p5", xB, xA)
    f1("p6", xA, xB, xA, xout_e)
    return ctx



_CTX_CACHE = {}


def _get(name, builder, *args):
    if name not in _CTX_CACHE:
        ctx = builder(*args)
        ctx.p.replay(ctx.nc)
        _CTX_CACHE[name] = ctx
    return _CTX_CACHE[name]


def _run(ctx, in_maps):
    return run_bass_kernel_spmd(ctx.nc, in_maps, core_ids=list(range(NCORES)))


def _run_built(name, builder, in_maps):
    ctx = _get(name, builder)
    return run_bass_kernel_spmd(ctx.nc, in_maps, core_ids=list(range(NCORES)))


def kernel(**I):
    return kernel_fused(**I)


def kernel_fused(**I):
    x = np.ascontiguousarray(np.asarray(I["L0_attn_hidden_states"],
                                        np.float32)[0])
    H = host_prep(I)
    shards = shard_x(x)
    ctx = _get("fused", build_fused)
    in_maps = []
    for i in range(NCORES):
        in_maps.append({
            "xsh": shards[i], "W2T": H["W2T"], "H128": H["H128"],
            "W1p": H["W1p"], "W3p": H["W3p"], "H64B": H["H64B"],
            "Wq": H["Wq"], "Wk": H["Wk"], "Wv": H["Wv"], "Wo0": H["Wo0"],
            "cos_r0": H["cos_r0"][i], "sin_r0": H["sin_r0"][i],
            "cos_r1": H["cos_r1"][i], "sin_r1": H["sin_r1"][i],
            "CUM": H["CUM"], "SUF": H["SUF"], "IDT": H["IDT"],
            "PSW": H["PSW"], "IDTF": H["IDTF"], "DMASK": H["DMASK"],
            "zsel": H["zsel"][i], "selpf": H["selpf"][i],
            "anw": H["anw"], "qnw": H["qnw"], "knw": H["knw"],
            "qwT": H["qwT"], "kwT": H["kwT"], "vwT": H["vwT"],
            "owT": H["owT"], "fnw": H["fnw"], "gTf": H["gTf"],
            "wgT": H["wgT"], "wuT": H["wuT"], "wdT": H["wdT"],
            "SEL": H["SEL"], "Ws1": H["Ws1"], "Wa1": H["Wa1"],
            "WoT1": H["WoT1"], "bo1": H["bo1"],
        })
    r = run_bass_kernel_spmd(ctx.nc, in_maps, core_ids=list(range(NCORES)))
    xo = unshard_x([np.asarray(r.results[i]["xout"]) for i in range(NCORES)])
    return np.ascontiguousarray(xo).reshape(1, SEQ, DM)


def kernel_multilaunch(**I):
    x = np.ascontiguousarray(np.asarray(I["L0_attn_hidden_states"],
                                        np.float32)[0])
    H = host_prep(I)
    asbf = np.asarray

    r = _run_built("gprep", build_gprep,
                   [{"W2T": H["W2T"], "H128": H["H128"]}] * NCORES)
    G2 = asbf(r.results[0]["G2"])

    def run_f0(shards):
        in_maps = [{"xsh": shards[i], "W1p": H["W1p"], "W3p": H["W3p"],
                    "G2": G2, "H64B": H["H64B"]} for i in range(NCORES)]
        r = _run_built("f0", build_f0, in_maps)
        return [asbf(r.results[i]["xout"]) for i in range(NCORES)]

    def assemble_kv(ra):
        skall = np.stack([asbf(ra.results[i]["sk_out"]) for i in range(NCORES)])
        vall = np.stack([asbf(ra.results[i]["vtok_out"]) for i in range(NCORES)])
        km = np.stack([asbf(ra.results[i]["kmax_out"])[0] for i in range(NCORES)])
        km_in = np.zeros((1, 32), np.float32)
        for g in range(4):
            km_in[0, g * 8:(g + 1) * 8] = km[:, g]
        return skall, vall, km_in

    def run_a0(shards):
        in_maps = [{"xsh": shards[i], "Wq": H["Wq"], "Wk": H["Wk"],
                    "Wv": H["Wv"], "cosd": H["cos_r0"][i], "sind": H["sin_r0"][i],
                    "CUM": H["CUM"], "SUF": H["SUF"], "IDT": H["IDT"],
                    "PSW": H["PSW"]} for i in range(NCORES)]
        ra = _run_built("a0a", build_a0a, in_maps)
        skall, vall, km_in = assemble_kv(ra)
        Dg = np.stack([asbf(ra.results[i]["D_out"]) for i in range(NCORES)])
        Ug = np.stack([asbf(ra.results[i]["U_out"]) for i in range(NCORES)])
        in_maps_b = []
        for i in range(NCORES):
            rr = ra.results[i]
            in_maps_b.append({
                "xsh": shards[i], "sq_in": asbf(rr["sq_out"]),
                "skall": skall, "vall": vall,
                "skown": asbf(rr["sk_out"]), "vown": asbf(rr["vtok_out"]),
                "ogla": asbf(rr["ogla_out"]), "acache": asbf(rr["acache_out"]),
                "boffe": asbf(rr["boff_out"]),
                "Dg": Dg, "Ug": Ug, "selpf": H["selpf"][i],
                "kmx": km_in, "zsel": H["zsel"][i], "DMASK": H["DMASK"],
                "Wo": H["Wo0"]})
        rb = _run_built("a0b", build_a0b, in_maps_b)
        return [asbf(rb.results[i]["xout"]) for i in range(NCORES)]

    def run_f1(shards):
        in_maps = [{"xsh": shards[i], "anw": H["anw"], "qnw": H["qnw"],
                    "knw": H["knw"], "qwT": H["qwT"], "kwT": H["kwT"],
                    "vwT": H["vwT"], "cosd": H["cos_r1"][i],
                    "sind": H["sin_r1"][i], "PSW": H["PSW"], "IDT": H["IDT"]}
                   for i in range(NCORES)]
        ra = _run_built("f1a", build_f1a, in_maps)
        skall, vall, km_in = assemble_kv(ra)
        in_maps_b = []
        for i in range(NCORES):
            rr = ra.results[i]
            in_maps_b.append({
                "xsh": asbf(rr["xout"]), "sq_in": asbf(rr["sq_out"]),
                "skall": skall, "vall": vall,
                "skown": asbf(rr["sk_out"]), "vown": asbf(rr["vtok_out"]),
                "hidden_in": asbf(rr["hidden_out"]),
                "kmx": km_in, "zsel": H["zsel"][i], "DMASK": H["DMASK"],
                "Wo": H["owT"]})
        rb = _run_built("f1flash", lambda: build_flash("f1"), in_maps_b)
        in_maps_c = [{"xsh": asbf(rb.results[i]["xout"]),
                      "h2sh": asbf(rb.results[i]["h2_out"]),
                      "fnw": H["fnw"], "gTf": H["gTf"], "wgT": H["wgT"],
                      "wuT": H["wuT"], "wdT": H["wdT"], "IDTF": H["IDTF"],
                      "SEL": H["SEL"]} for i in range(NCORES)]
        rc = _run_built("moe", build_moe, in_maps_c)
        return [asbf(rc.results[i]["xout"]) for i in range(NCORES)]

    def run_a1(shards):
        in_maps = [{"xsh": shards[i], "Ws": H["Ws1"], "Wa": H["Wa1"],
                    "IDT": H["IDT"]} for i in range(NCORES)]
        ra = _run_built("a1a", build_a1a, in_maps)
        aall = np.stack([asbf(ra.results[i]["adm_out"]) for i in range(NCORES)])
        hall = np.stack([asbf(ra.results[i]["htok_out"]) for i in range(NCORES)])
        amx = np.zeros((1, 8), np.float32)
        for i in range(NCORES):
            amx[0, i] = asbf(ra.results[i]["amax_out"])[0, 0]
        in_maps_b = []
        for i in range(NCORES):
            rr = ra.results[i]
            in_maps_b.append({
                "xsh": shards[i], "sdm": asbf(rr["sdm_out"]),
                "aall": aall, "hall": hall,
                "aown": asbf(rr["adm_out"]), "hown": asbf(rr["htok_out"]),
                "amx": amx, "zsel": H["zsel"][i], "DMASK": H["DMASK"],
                "WoT": H["WoT1"], "bo": H["bo1"]})
        rb = _run_built("a1b", build_a1b, in_maps_b)
        return [asbf(rb.results[i]["xout"]) for i in range(NCORES)]

    sh = shard_x(x)
    sh = run_f0(sh)
    sh = run_a0(sh)
    sh = run_f0(sh)
    sh = run_f1(sh)
    sh = run_a1(sh)
    sh = run_f1(sh)
    x = unshard_x(sh)
    return np.ascontiguousarray(x).reshape(1, SEQ, DM)


# version 07
# speedup vs baseline: 1.3267x; 1.1925x over previous
import math
from contextlib import ExitStack

import numpy as np
import ml_dtypes

import concourse.bass as bass
import concourse.mybir as mybir
from concourse.bass_utils import run_bass_kernel_spmd

F32 = mybir.dt.float32
BF16 = mybir.dt.bfloat16
AF = mybir.ActivationFunctionType
ALU = mybir.AluOpType
AX = mybir.AxisListType

NCORES = 8
SEQ = 8192
DM = 2048
DCH = DM // 128
NCH = 16
CHT = SEQ // NCH
TPC = 2 * CHT
CORE_CHUNKS = [(i, NCH - 1 - i) for i in range(NCORES)]

BF = ml_dtypes.bfloat16



def _hadamard(n):
    h = np.array([[1.0]], dtype=np.float64)
    while h.shape[0] < n:
        h = np.block([[h, h], [h, -h]])
    return h


def _rope_tables(n, d, theta):
    inv = 1.0 / theta ** (np.arange(0, d, 2, dtype=np.float64) / d)
    f = np.arange(n, dtype=np.float64)[:, None] * inv[None, :]
    return np.cos(f), np.sin(f)


def core_tokens(core):
    a, b = CORE_CHUNKS[core]
    return np.r_[a * CHT:(a + 1) * CHT, b * CHT:(b + 1) * CHT]


def shard_x(x):
    return [np.ascontiguousarray(x[core_tokens(i)].T) for i in range(NCORES)]


def unshard_x(shards):
    x = np.empty((SEQ, DM), np.float32)
    for i in range(NCORES):
        x[core_tokens(i)] = np.asarray(shards[i]).T
    return x




class SemC:
    __slots__ = ("name", "n", "h", "base")

    def __init__(self, name):
        self.name = name
        self.n = 0
        self.base = 0
        self.h = None

    @property
    def rel(self):
        return self.n - self.base


class _Eng:
    def __init__(self, prog, name):
        self._p = prog
        self._n = name

    def op(self, meth, *args, inc=None, **kw):
        if inc:
            for s, v in inc:
                s.n += v

        def closure(e, _m=meth, _a=args, _k=kw, _i=inc):
            r = getattr(e, _m)(*_a, **_k)
            if _i:
                for s, v in _i:
                    r.then_inc(s.h, v)
            return r

        self._p.ops.append((self._n, closure))

    def wait(self, sem, val=None):
        v = sem.n if val is None else (sem.base + val)
        self._p.ops.append((self._n, lambda e, _s=sem, _v=v: e.wait_ge(_s.h, _v)))

    def __getattr__(self, meth):
        def call(*args, inc=None, **kw):
            self.op(meth, *args, inc=inc, **kw)
        return call


class Prog:
    def __init__(self):
        self.ops = []
        self.sems = {}
        self.prefix = ""
        self.sync = _Eng(self, "sync")
        self.scalar = _Eng(self, "scalar")
        self.vector = _Eng(self, "vector")
        self.tensor = _Eng(self, "tensor")
        self.gpsimd = _Eng(self, "gpsimd")

    def sem(self, name):
        if name not in self.sems:
            self.sems[name] = SemC(name)
        return self.sems[name]

    def barrier(self):
        self.ops.append(("BARRIER", None))

    def replay(self, nc):
        segs = [[]]
        for item in self.ops:
            if item[0] == "BARRIER":
                if segs[-1]:
                    segs.append([])
            else:
                segs[-1].append(item)
        if segs and not segs[-1]:
            segs.pop()
        with ExitStack() as st:
            for s in self.sems.values():
                s.h = st.enter_context(nc.semaphore(s.name))
            for seg in segs:
                per = {}
                for eng, fn in seg:
                    per.setdefault(eng, []).append(fn)
                with nc.Block() as blk:
                    for eng, lst in per.items():
                        def make(lst_):
                            def body(e):
                                for f in lst_:
                                    f(e)
                            return body
                        getattr(blk, eng)(make(lst))


class Ctx:

    def __init__(self, name, bf16_banks=0):
        self.nc = bass.Bass("TRN2", target_bir_lowering=False)
        self.p = Prog()
        self.uid = 0
        self.name = name
        self._scratch = {}
        nf = 4 - (bf16_banks + 1) // 2
        self.pool = [self.nc.alloc_psum_tensor(f"pool{i}", [128, 1024], F32).ap()
                     for i in range(nf)]
        self.bpool = [self.nc.alloc_psum_tensor(f"bpool{i}", [128, 2048], BF16).ap()
                      for i in range((bf16_banks + 1) // 2)]

    def pv(self, bank, width=512):
        t = self.pool[bank // 2]
        off = (bank % 2) * 512
        return t[:, off:off + width]

    def prow(self, i):
        return self.pool[i][0:1, 0:1024]

    def pfull(self, i):
        return self.pool[i]

    def nid(self, base):
        self.uid += 1
        return f"{base}_{self.uid}"

    def enable_arena(self, nbytes):
        res = self.nc.bump_sbuf(nbytes)
        assert res is not None, "no SBUF left for arena"
        self.arena_base = res[0]
        self.arena_bytes = nbytes
        self.arena_mode = True
        self.aoff = 0
        self.amax_seen = 0

    def phase_reset(self, tag):
        self._scratch = {}
        self.aoff = 0
        for s in self.p.sems.values():
            s.base = s.n
        if hasattr(self, "xsq_buf"):
            del self.xsq_buf

    def sb(self, name, shape, dtype):
        if not getattr(self, "arena_mode", False):
            return self.nc.alloc_sbuf_tensor(self.nid(name), shape, dtype).ap()
        n = int(np.prod(shape[1:]))
        esz = 2 if dtype == BF16 else 4
        nb = (n * esz + 31) & ~31
        off = self.aoff
        assert off + n * esz <= self.arena_bytes, (
            f"arena overflow {name} {shape} {dtype} at {off}")
        self.aoff = off + nb
        self.amax_seen = max(self.amax_seen, self.aoff)
        t = self.nc.alloc_sbuf_tensor_at(self.nid("ar_" + name),
                                         [shape[0], n], dtype,
                                         offset=self.arena_base + off)
        return t.ap()

    def scratch(self, name, shape, dtype):
        key = (name, tuple(shape), dtype)
        if key not in self._scratch:
            self._scratch[key] = self.sb(name, shape, dtype)
        return self._scratch[key]

    def din(self, name, shape, dtype=F32):
        if getattr(self, "iomap", None) is not None:
            return self.iomap[name]
        return self.nc.dram_tensor(name, shape, dtype, kind="ExternalInput").ap()

    def dout(self, name, shape, dtype=F32):
        if getattr(self, "iomap", None) is not None:
            return self.iomap[name]
        return self.nc.dram_tensor(name, shape, dtype, kind="ExternalOutput").ap()

    def dint(self, name, shape, dtype=F32):
        return self.nc.dram_tensor(self.nid(name), shape, dtype, kind="Internal").ap()

    def alloc_consts(self):
        g = self.p.gpsimd
        self.ones_col_bf = self.sb("ones_col_bf", [128, 1], BF16)
        self.ones_row_bf = self.sb("ones_row_bf", [1, 128], BF16)
        self.ones_row_f = self.sb("ones_row_f", [1, 128], F32)
        g.memset(self.ones_col_bf, 1.0)
        g.memset(self.ones_row_bf, 1.0)
        g.memset(self.ones_row_f, 1.0)
        self._cbias = {}

    def cbias(self, value, parts=128):
        key = (float(value), parts)
        if key not in self._cbias:
            t = self.sb("cbias", [parts, 1], F32)
            self.p.gpsimd.memset(t, float(value))
            self._cbias[key] = t
        return self._cbias[key]



def emit_norm_fm(ctx, x_sb, h_out, T, rms_only=False, weight_col=None,
                 eps=1e-5):
    p = ctx.p
    nt = T // 512
    xbf = ctx.scratch("sn_xbf", [128, DCH * T], BF16) if not rms_only else None
    xsq = getattr(ctx, "xsq_buf", None)
    if xsq is None:
        xsq = ctx.scratch("sn_xsq2", [128, DCH * T], BF16)
    mean_r = ctx.scratch("sn_mean", [1, T], F32)
    rstd_r = ctx.scratch("sn_rstd", [1, T], F32)
    var_r = ctx.scratch("sn_var", [1, T], F32)
    tmp = ctx.scratch("sn_tmp", [128, T], F32)
    p.barrier()
    for c in range(DCH):
        sl = slice(c * T, (c + 1) * T)
        if not rms_only:
            p.vector.tensor_copy(xbf[:, sl], x_sb[:, sl])
        p.scalar.activation(xsq[:, sl], x_sb[:, sl], AF.Square)
    p.barrier()
    for c in range(DCH):
        for tt in range(nt):
            tsl = slice(c * T + tt * 512, c * T + tt * 512 + 512)
            osl = slice(tt * 512, tt * 512 + 512)
            if not rms_only:
                p.tensor.matmul(ctx.prow(0)[:, osl], ctx.ones_col_bf, xbf[:, tsl],
                                start=(c == 0), stop=(c == DCH - 1))
            p.tensor.matmul(ctx.prow(1)[:, osl], ctx.ones_col_bf, xsq[:, tsl],
                            start=(c == 0), stop=(c == DCH - 1))
    p.barrier()
    if rms_only:
        p.vector.tensor_scalar_mul(var_r, ctx.prow(1), 1.0 / DM)
    else:
        p.vector.tensor_scalar_mul(mean_r, ctx.prow(0), 1.0 / DM)
        p.vector.tensor_scalar_mul(var_r, ctx.prow(1), 1.0 / DM)
        p.vector.tensor_mul(rstd_r, mean_r, mean_r)
        p.vector.tensor_sub(var_r, var_r, rstd_r)
    p.barrier()
    p.scalar.activation(rstd_r, var_r, AF.Ln, bias=ctx.cbias(eps, 1), scale=1.0)
    p.scalar.activation(rstd_r, rstd_r, AF.Exp, scale=-0.5)
    p.barrier()
    for tt in range(nt):
        osl = slice(tt * 512, tt * 512 + 512)
        if not rms_only:
            p.tensor.matmul(ctx.pfull(0)[:, osl], ctx.ones_row_f, mean_r[:, osl],
                            start=True, stop=True)
        p.tensor.matmul(ctx.pfull(1)[:, osl], ctx.ones_row_f, rstd_r[:, osl],
                        start=True, stop=True)
    p.barrier()
    for c in range(DCH):
        sl = slice(c * T, (c + 1) * T)
        if rms_only:
            p.vector.tensor_mul(tmp, x_sb[:, sl], ctx.pfull(1))
        else:
            p.vector.tensor_sub(tmp, x_sb[:, sl], ctx.pfull(0))
            p.vector.tensor_mul(tmp, tmp, ctx.pfull(1))
        if weight_col is not None:
            p.vector.tensor_single_scalar(tmp, tmp, weight_col[:, c:c + 1],
                                          ALU.mult)
        p.vector.tensor_copy(h_out[:, sl], tmp)
    p.barrier()


def emit_load_x(ctx, xsh, x_sb, T):
    p = ctx.p
    sX = p.sem(ctx.nid("sx"))
    for c in range(DCH):
        p.sync.op("dma_start", out=x_sb[:, c * T:(c + 1) * T],
                  in_=xsh[c * 128:(c + 1) * 128, :], inc=[(sX, 16)])
    p.sync.wait(sX)
    p.barrier()


def emit_store_x(ctx, x_sb, xout, T, sO):
    p = ctx.p
    p.barrier()
    for c in range(DCH):
        p.sync.op("dma_start", out=xout[c * 128:(c + 1) * 128, :],
                  in_=x_sb[:, c * T:(c + 1) * T], inc=[(sO, 16)])
    p.sync.wait(sO)



def build_gprep(ctx=None):
    if ctx is None:
        ctx = Ctx("gprep")
    p = ctx.p
    W2T = ctx.din("W2T", [SEQ, DM], F32)
    H128 = ctx.din("H128", [128, 128], F32)
    G2 = ctx.dout("G2", [16, 64, 128, 128], BF16)

    h_sb = ctx.sb("h128", [128, 128], BF16)
    w_sb = [ctx.sb("w2c", [128, DM], BF16) for _ in range(2)]
    g_sb = [ctx.sb("gc", [128, DM], BF16) for _ in range(2)]

    sL = p.sem("ld")
    sW = [p.sem("w0"), p.sem("w1")]
    sM = p.sem("mm")
    sV = p.sem("vv")
    sO = p.sem("st")

    p.gpsimd.op("dma_start", out=h_sb, in_=H128, inc=[(sL, 16)])
    p.tensor.wait(sL)

    n_mm = 0
    n_vv = 0
    for ap_i in range(64):
        b = ap_i % 2
        if ap_i >= 2:
            p.gpsimd.wait(sM, (ap_i - 1) * 4)
        p.gpsimd.op("dma_start", out=w_sb[b],
                    in_=W2T[ap_i * 128:(ap_i + 1) * 128, :], inc=[(sW[b], 16)])
        p.tensor.wait(sW[b])
        for ot in range(4):
            p.tensor.op("matmul", ctx.pv(ot), h_sb,
                        w_sb[b][:, ot * 512:(ot + 1) * 512],
                        start=True, stop=True, inc=[(sM, 1)])
            n_mm += 1
        p.vector.wait(sM, n_mm)
        if ap_i >= 2:
            p.vector.wait(sO, 16 * (ap_i - 1))
        for ot in range(4):
            p.vector.op("tensor_copy", g_sb[b][:, ot * 512:(ot + 1) * 512],
                        ctx.pv(ot), inc=[(sV, 1)])
            n_vv += 1
        p.gpsimd.wait(sV, n_vv)
        out_ap = bass.AP(G2.tensor, ap_i * 128,
                         [[16384, 64], [8192, 2], [1048576, 16], [1, 128]])
        p.gpsimd.op("dma_start", out=out_ap, in_=g_sb[b], inc=[(sO, 16)])
    p.gpsimd.wait(sO)
    return ctx



def build_f0(ctx=None):
    if ctx is None:
        ctx = Ctx("f0", bf16_banks=2)
    p = ctx.p
    T = TPC
    nt = T // 512
    xsh = ctx.din("xsh", [DM, T], F32)
    W1p = ctx.din("W1p", [64, DCH, 128, 128], F32)
    W3p = ctx.din("W3p", [64, DCH, 128, 128], F32)
    G2 = ctx.din("G2", [16, 64, 128, 128], BF16)
    H64B = ctx.din("H64B", [128, 128], F32)
    xout = ctx.dout("xout", [DM, T], F32)

    ctx.alloc_consts()
    h64 = ctx.sb("h64", [128, 128], BF16)
    sA = ctx.sb("sA", [128, 64 * T], BF16)
    xsq = sA[:, 0:DCH * T]
    h_bf = ctx.sb("h_bf", [128, DCH * T], BF16)
    w1sb = [ctx.sb("w1sb", [128, DCH * 128], BF16) for _ in range(2)]
    w3sb = [ctx.sb("w3sb", [128, DCH * 128], BF16) for _ in range(2)]
    gsb = [ctx.sb("gsb", [128, 8 * 128], BF16) for _ in range(2)]
    s1 = ctx.sb("s1", [128, T], BF16)
    mid = s1
    xin = [ctx.sb("xin", [128, T], F32) for _ in range(2)]
    mean_r = ctx.sb("f0_mean", [1, T], F32)
    rstd_r = ctx.sb("f0_rstd", [1, T], F32)
    tmp = ctx.sb("f0_tmp", [128, T], F32)
    var_r = tmp[0:1, 0:T]
    xo_sb = tmp

    sX = p.sem("sxl")
    sW = [p.sem("sw0"), p.sem("sw1")]
    sG2w = [p.sem("sg0"), p.sem("sg1")]
    sM = p.sem("sm")
    sSc = p.sem("ssc")
    sV = p.sem("sv")
    sM2 = p.sem("sm2")
    sV2 = p.sem("sv2")
    sO = p.sem("so")
    sH = p.sem("sh")
    sLNv = p.sem("slnv")

    ctx.p.gpsimd.op("dma_start", out=h64, in_=H64B, inc=[(sH, 16)])

    for c in range(DCH):
        b = c % 2
        if c >= 2:
            p.sync.wait(sLNv, 2 * (c - 1))
        p.sync.op("dma_start", out=xin[b], in_=xsh[c * 128:(c + 1) * 128, :],
                  inc=[(sX, 16)])
        sl = slice(c * T, (c + 1) * T)
        p.vector.wait(sX, 16 * (c + 1))
        p.vector.op("tensor_copy", h_bf[:, sl], xin[b], inc=[(sLNv, 1)])
        p.scalar.wait(sX, 16 * (c + 1))
        p.scalar.op("activation", xsq[:, sl], xin[b], AF.Square, inc=[(sLNv, 1)])
    p.barrier()
    for c in range(DCH):
        for tt in range(nt):
            tsl = slice(c * T + tt * 512, c * T + tt * 512 + 512)
            osl = slice(tt * 512, tt * 512 + 512)
            p.tensor.matmul(ctx.prow(0)[:, osl], ctx.ones_col_bf, h_bf[:, tsl],
                            start=(c == 0), stop=(c == DCH - 1))
            p.tensor.matmul(ctx.prow(1)[:, osl], ctx.ones_col_bf, xsq[:, tsl],
                            start=(c == 0), stop=(c == DCH - 1))
    p.barrier()
    p.vector.tensor_scalar_mul(mean_r, ctx.prow(0), 1.0 / DM)
    p.vector.tensor_scalar_mul(var_r, ctx.prow(1), 1.0 / DM)
    p.vector.tensor_mul(rstd_r, mean_r, mean_r)
    p.vector.tensor_sub(var_r, var_r, rstd_r)
    p.barrier()
    p.scalar.activation(rstd_r, var_r, AF.Ln, bias=ctx.cbias(1e-5, 1), scale=1.0)
    p.scalar.activation(rstd_r, rstd_r, AF.Exp, scale=-0.5)
    p.barrier()
    for tt in range(nt):
        osl = slice(tt * 512, tt * 512 + 512)
        p.tensor.matmul(ctx.pfull(0)[:, osl], ctx.ones_row_f, mean_r[:, osl],
                        start=True, stop=True)
        p.tensor.matmul(ctx.pfull(1)[:, osl], ctx.ones_row_f, rstd_r[:, osl],
                        start=True, stop=True)
    p.barrier()
    for c in range(DCH):
        sl = slice(c * T, (c + 1) * T)
        p.vector.tensor_sub(tmp, h_bf[:, sl], ctx.pfull(0))
        p.vector.tensor_mul(tmp, tmp, ctx.pfull(1))
        p.vector.tensor_copy(h_bf[:, sl], tmp)
    p.barrier()
    p.tensor.wait(sH)

    GRP = 3 * nt
    nsc = 0
    nv = 0
    for jp in range(64):
        b = jp % 2
        if jp >= 2:
            p.gpsimd.wait(sM, (jp - 1) * GRP)
        p.gpsimd.op("dma_start", out=w1sb[b],
                    in_=W1p[jp].rearrange("c p f -> p c f"), inc=[(sW[b], 16)])
        p.gpsimd.op("dma_start", out=w3sb[b],
                    in_=W3p[jp].rearrange("c p f -> p c f"), inc=[(sW[b], 16)])
        p.tensor.wait(sW[b])
        if jp >= 1:
            p.tensor.wait(sSc, jp * nt)
            p.tensor.wait(sV, jp * 2 * nt)
        for tt in range(nt):
            for c in range(DCH):
                rsl = slice(c * T + tt * 512, c * T + tt * 512 + 512)
                p.tensor.op("matmul", ctx.pv(0 + tt), w1sb[b][:, c * 128:(c + 1) * 128],
                            h_bf[:, rsl], start=(c == 0), stop=(c == DCH - 1),
                            inc=[(sM, 1)] if c == DCH - 1 else None)
            for c in range(DCH):
                rsl = slice(c * T + tt * 512, c * T + tt * 512 + 512)
                p.tensor.op("matmul", ctx.pv(2 + tt), w3sb[b][:, c * 128:(c + 1) * 128],
                            h_bf[:, rsl], start=(c == 0), stop=(c == DCH - 1),
                            inc=[(sM, 1)] if c == DCH - 1 else None)
        p.scalar.wait(sM, jp * GRP + 2 * nt)
        if jp >= 1:
            p.scalar.wait(sV, (jp - 1) * 2 * nt + nt)
        for tt in range(nt):
            p.scalar.op("activation", s1[:, tt * 512:(tt + 1) * 512],
                        ctx.pv(0 + tt), AF.Silu, inc=[(sSc, 1)])
            nsc += 1
        p.vector.wait(sSc, nsc)
        for tt in range(nt):
            p.vector.op("tensor_mul", mid[:, tt * 512:(tt + 1) * 512],
                        s1[:, tt * 512:(tt + 1) * 512], ctx.pv(2 + tt),
                        inc=[(sV, 1)])
            nv += 1
        p.tensor.wait(sV, nv)
        for tt in range(nt):
            p.tensor.op("matmul", ctx.pv(4 + tt), h64,
                        mid[:, tt * 512:(tt + 1) * 512], start=True, stop=True,
                        inc=[(sM, 1)])
        p.vector.wait(sM, jp * GRP + 3 * nt)
        for tt in range(nt):
            p.vector.op("tensor_copy",
                        sA[:, jp * T + tt * 512: jp * T + tt * 512 + 512],
                        ctx.pv(4 + tt), inc=[(sV, 1)])
            nv += 1
    p.barrier()

    nv2 = 0
    sQ = p.sem("sq")
    for ot in range(DCH):
        if ot >= 2:
            p.gpsimd.wait(sV2, (ot - 1) * nt)
        p.gpsimd.op("dma_start", out=xin[ot % 2],
                    in_=xsh[ot * 128:(ot + 1) * 128, :], inc=[(sX, 16)])
        for q in range(8):
            qidx = ot * 8 + q
            b = qidx % 2
            if qidx >= 2:
                p.sync.wait(sQ, qidx - 1)
            p.sync.op("dma_start", out=gsb[b],
                      in_=G2[ot, q * 8:(q + 1) * 8].rearrange("jp p f -> p jp f"),
                      inc=[(sG2w[b], 16)])
            p.tensor.wait(sG2w[b])
            if q == 0 and ot >= 1:
                p.tensor.wait(sV2, ot * nt)
            for tt in range(nt):
                for jj in range(8):
                    jp = q * 8 + jj
                    last = (tt == nt - 1 and jj == 7)
                    p.tensor.op("matmul", ctx.pv(0 + tt),
                                gsb[b][:, jj * 128:(jj + 1) * 128],
                                sA[:, jp * T + tt * 512: jp * T + tt * 512 + 512],
                                start=(jp == 0), stop=(jp == 63),
                                inc=[(sQ, 1)] if last else None)
        p.vector.wait(sQ, (ot + 1) * 8)
        p.vector.wait(sX, 16 * (DCH + ot + 1))
        if ot >= 1:
            p.vector.wait(sO, 16 * ot)
        for tt in range(nt):
            p.vector.op("scalar_tensor_tensor",
                        xo_sb[:, tt * 512:(tt + 1) * 512], ctx.pv(0 + tt),
                        0.5 / math.sqrt(2.0),
                        xin[ot % 2][:, tt * 512:(tt + 1) * 512],
                        ALU.mult, ALU.add, inc=[(sV2, 1)])
            nv2 += 1
        p.sync.wait(sV2, nv2)
        p.sync.op("dma_start", out=xout[ot * 128:(ot + 1) * 128, :],
                  in_=xo_sb, inc=[(sO, 16)])
    p.sync.wait(sO)
    p.gpsimd.wait(sO)
    return ctx




def emit_stream_norm(ctx, xsh, h_bf, T, resid_xout=None, resid_scale=0.5,
                     eps=1e-5, h_f32_out=None):
    p = ctx.p
    nt = T // 512
    xin = ctx.scratch("sn_xin", [128, T], F32)
    xin2 = ctx.scratch("sn_xin2", [128, T], F32)
    xsq = getattr(ctx, "xsq_buf", None)
    if xsq is None:
        xsq = ctx.scratch("sn_xsq", [128, DCH * T], BF16)
    mean_r = ctx.scratch("sn_mean", [1, T], F32)
    rstd_r = ctx.scratch("sn_rstd", [1, T], F32)
    tmp = ctx.scratch("sn_tmp", [128, T], F32)
    var_r = ctx.scratch("sn_var", [1, T], F32)
    sX = ctx.p.sem("snx")
    sC = ctx.p.sem("snc")

    for c in range(DCH):
        b = c % 2
        buf = xin if b == 0 else xin2
        if c >= 2:
            p.sync.wait(sC, 2 * (c - 1))
        p.sync.op("dma_start", out=buf, in_=xsh[c * 128:(c + 1) * 128, :],
                  inc=[(sX, 16)])
        sl = slice(c * T, (c + 1) * T)
        p.vector.wait(sX, 16 * (c + 1))
        p.vector.op("tensor_copy", h_bf[:, sl], buf, inc=[(sC, 1)])
        p.scalar.wait(sX, 16 * (c + 1))
        p.scalar.op("activation", xsq[:, sl], buf, AF.Square, inc=[(sC, 1)])
    p.barrier()
    for c in range(DCH):
        for tt in range(nt):
            tsl = slice(c * T + tt * 512, c * T + tt * 512 + 512)
            osl = slice(tt * 512, tt * 512 + 512)
            p.tensor.matmul(ctx.prow(0)[:, osl], ctx.ones_col_bf, h_bf[:, tsl],
                            start=(c == 0), stop=(c == DCH - 1))
            p.tensor.matmul(ctx.prow(1)[:, osl], ctx.ones_col_bf, xsq[:, tsl],
                            start=(c == 0), stop=(c == DCH - 1))
    p.barrier()
    p.vector.tensor_scalar_mul(mean_r, ctx.prow(0), 1.0 / DM)
    p.vector.tensor_scalar_mul(var_r, ctx.prow(1), 1.0 / DM)
    p.vector.tensor_mul(rstd_r, mean_r, mean_r)
    p.vector.tensor_sub(var_r, var_r, rstd_r)
    p.barrier()
    p.scalar.activation(rstd_r, var_r, AF.Ln, bias=ctx.cbias(eps, 1), scale=1.0)
    p.scalar.activation(rstd_r, rstd_r, AF.Exp, scale=-0.5)
    p.barrier()
    for tt in range(nt):
        osl = slice(tt * 512, tt * 512 + 512)
        p.tensor.matmul(ctx.pfull(0)[:, osl], ctx.ones_row_f, mean_r[:, osl],
                        start=True, stop=True)
        p.tensor.matmul(ctx.pfull(1)[:, osl], ctx.ones_row_f, rstd_r[:, osl],
                        start=True, stop=True)
    p.barrier()
    if resid_xout is None:
        for c in range(DCH):
            sl = slice(c * T, (c + 1) * T)
            p.vector.tensor_sub(tmp, h_bf[:, sl], ctx.pfull(0))
            p.vector.tensor_mul(tmp, tmp, ctx.pfull(1))
            p.vector.tensor_copy(h_bf[:, sl], tmp)
    else:
        sX2 = ctx.p.sem("snx2")
        sV2 = ctx.p.sem("snv2")
        sO2 = ctx.p.sem("sno2")
        for c in range(DCH):
            b = c % 2
            buf = xin if b == 0 else xin2
            if c >= 2:
                p.sync.wait(sV2, 2 * (c - 1))
            p.sync.op("dma_start", out=buf, in_=xsh[c * 128:(c + 1) * 128, :],
                      inc=[(sX2, 16)])
            sl = slice(c * T, (c + 1) * T)
            p.vector.wait(sX2, 16 * (c + 1))
            if c >= 1 and h_f32_out is not None:
                p.vector.wait(sO2)
            p.vector.tensor_sub(tmp, buf, ctx.pfull(0))
            p.vector.tensor_mul(tmp, tmp, ctx.pfull(1))
            p.vector.tensor_copy(h_bf[:, sl], tmp)
            p.vector.op("scalar_tensor_tensor", buf, tmp, resid_scale, buf,
                        ALU.mult, ALU.add, inc=[(sV2, 2)])
            p.sync.wait(sV2, 2 * (c + 1))
            p.sync.op("dma_start", out=resid_xout[c * 128:(c + 1) * 128, :],
                      in_=buf, inc=[(sO2, 16)])
            if h_f32_out is not None:
                p.sync.op("dma_start", out=h_f32_out[c * 128:(c + 1) * 128, :],
                          in_=tmp, inc=[(sO2, 16)])
        p.sync.wait(sO2)
    p.barrier()


def emit_proj(ctx, h_bf, W, out_bf, dout, T, name):
    p = ctx.p
    nt = T // 512
    not_ = dout // 128
    wsl = [ctx.scratch(f"pj_w{i}", [128, DCH * 128], BF16) for i in range(2)]
    sW = [p.sem("pjw0"), p.sem("pjw1")]
    sM = p.sem("pjm")
    sV = p.sem("pjv")
    m0 = sM.rel
    v0 = sV.rel
    p.barrier()
    for ot in range(not_):
        b = ot % 2
        if ot >= 2:
            p.gpsimd.wait(sM, m0 + (ot - 1) * nt)
        p.gpsimd.op("dma_start", out=wsl[b],
                    in_=W[:, ot * 128:(ot + 1) * 128].rearrange(
                        "(c p) f -> p c f", p=128),
                    inc=[(sW[b], 16)])
        p.tensor.wait(sW[b])
        if ot >= 2:
            p.tensor.wait(sV, v0 + (ot - 1) * nt)
        for tt in range(nt):
            pb = ctx.pv((ot % 2) * 2 + tt)
            for c in range(DCH):
                rsl = slice(c * T + tt * 512, c * T + tt * 512 + 512)
                p.tensor.op("matmul", pb, wsl[b][:, c * 128:(c + 1) * 128],
                            h_bf[:, rsl], start=(c == 0), stop=(c == DCH - 1),
                            inc=[(sM, 1)] if c == DCH - 1 else None)
        p.vector.wait(sM, m0 + (ot + 1) * nt)
        for tt in range(nt):
            pb = ctx.pv((ot % 2) * 2 + tt)
            p.vector.op("tensor_copy",
                        out_bf[:, ot * T + tt * 512: ot * T + tt * 512 + 512],
                        pb, inc=[(sV, 1)])
    p.barrier()


def emit_rope_head(ctx, src, dst, T, cdup_sb, sdup_sb, pswap_sb, bank=4):
    p = ctx.p
    nt = T // 512
    rt1 = ctx.scratch("rp_t1", [128, T], BF16)
    rt2 = ctx.scratch("rp_t2", [128, T], BF16)
    p.barrier()
    for tt in range(nt):
        p.tensor.matmul(ctx.pv(bank + tt), pswap_sb,
                        src[:, tt * 512:(tt + 1) * 512], start=True, stop=True)
    p.vector.tensor_mul(rt1, src, cdup_sb)
    p.barrier()
    for tt in range(nt):
        sl = slice(tt * 512, (tt + 1) * 512)
        p.vector.tensor_mul(rt2[:, sl], ctx.pv(bank + tt), sdup_sb[:, sl])
    p.barrier()
    p.vector.tensor_add(dst, rt1, rt2)
    p.barrier()


def emit_bound_softmax_fm(ctx, q_bf, nheads, T, scratch_bf):
    p = ctx.p
    nt = T // 512
    row1 = ctx.scratch("bs_row1", [1, 512], F32)
    row2 = ctx.scratch("bs_row2", [1, 512], F32)
    p.barrier()
    for h in range(nheads):
        for tt in range(nt):
            sl = slice(h * T + tt * 512, h * T + tt * 512 + 512)
            p.barrier()
            p.scalar.activation(scratch_bf, q_bf[:, sl], AF.Square)
            p.barrier()
            p.tensor.matmul(ctx.prow(0)[:, 0:512], ctx.ones_col_bf, scratch_bf,
                            start=True, stop=True)
            p.barrier()
            p.scalar.activation(row1, ctx.prow(0)[:, 0:512], AF.Ln,
                                bias=ctx.cbias(1e-12, 1), scale=1.0)
            p.scalar.activation(row1, row1, AF.Exp, scale=0.5)
            p.barrier()
            p.vector.tensor_scalar_mul(row1, row1, -1.0)
            p.barrier()
            p.tensor.matmul(ctx.pfull(1)[:, 0:512], ctx.ones_row_f, row1,
                            start=True, stop=True)
            p.barrier()
            p.vector.tensor_add(scratch_bf, q_bf[:, sl], ctx.pfull(1)[:, 0:512])
            p.barrier()
            p.scalar.activation(scratch_bf, scratch_bf, AF.Exp)
            p.barrier()
            p.tensor.matmul(ctx.prow(0)[:, 0:512], ctx.ones_col_bf, scratch_bf,
                            start=True, stop=True)
            p.barrier()
            p.vector.reciprocal(row2, ctx.prow(0)[:, 0:512])
            p.barrier()
            p.tensor.matmul(ctx.pfull(1)[:, 0:512], ctx.ones_row_f, row2,
                            start=True, stop=True)
            p.barrier()
            p.vector.tensor_mul(q_bf[:, sl], scratch_bf, ctx.pfull(1)[:, 0:512])
    p.barrier()



def build_a0a(ctx=None):
    if ctx is None:
        ctx = Ctx("a0a", bf16_banks=2)
    p = ctx.p
    T = TPC
    nt = T // 512
    NH, NG = 16, 4
    xsh = ctx.din("xsh", [DM, T], F32)
    Wq = ctx.din("Wq", [DM, DM], F32)
    Wk = ctx.din("Wk", [DM, 512], F32)
    Wv = ctx.din("Wv", [DM, 512], F32)
    cosd = ctx.din("cosd", [128, T], BF16)
    sind = ctx.din("sind", [128, T], BF16)
    PSW = ctx.din("PSW", [128, 128], BF16)
    CUM = ctx.din("CUM", [128, 128], BF16)
    SUF = ctx.din("SUF", [128, 128], BF16)
    IDT = ctx.din("IDT", [128, 128], BF16)

    sq_out = ctx.dout("sq_out", [DM, T], BF16)
    sk_out = ctx.dout("sk_out", [512, T], BF16)
    vtok_out = ctx.dout("vtok_out", [T, 512], BF16)
    ogla_out = ctx.dout("ogla_out", [8, NH, 128, 128], BF16)
    acache_out = ctx.dout("acache_out", [8, NH, 128, 128], BF16)
    boff_out = ctx.dout("boff_out", [128, 8 * NG], F32)
    D_out = ctx.dout("D_out", [2, 128, NG], F32)
    U_out = ctx.dout("U_out", [2, NG, 128, 128], F32)
    kmax_out = ctx.dout("kmax_out", [1, NG], F32)

    ctx.alloc_consts()
    h_bf = ctx.sb("h_bf", [128, DCH * T], BF16)
    q_bf = ctx.sb("q_bf", [128, NH * T], BF16)
    ctx.xsq_buf = q_bf[:, 0:DCH * T]
    k_bf = ctx.sb("k_bf", [128, NG * T], BF16)
    v_bf = ctx.sb("v_bf", [128, NG * T], BF16)
    sq_t = [ctx.sb("sq_t", [128, T], BF16) for _ in range(2)]
    sk_sb = ctx.sb("sk_sb", [128, NG * T], BF16)
    cos_sb = ctx.sb("cos_sb", [128, T], BF16)
    sin_sb = ctx.sb("sin_sb", [128, T], BF16)
    psw_sb = ctx.sb("psw_sb", [128, 128], BF16)
    cum_sb = ctx.sb("cum_sb", [128, 128], BF16)
    suf_sb = ctx.sb("suf_sb", [128, 128], BF16)
    idt_sb = ctx.sb("idt_sb", [128, 128], BF16)
    ktok = ctx.sb("ktok", [128, 8 * 512], BF16)
    vtok = ctx.sb("vtok", [128, 8 * 512], BF16)
    kstok = ctx.sb("kstok", [128, 8 * 512], BF16)
    gl_bf = ctx.sb("gl_bf", [128, 8 * 512], BF16)
    bsx = ctx.sb("bsx", [128, 512], BF16)
    expB = ctx.sb("expB", [128, 512], BF16)
    expnB = ctx.sb("expnB", [128, 512], BF16)
    exps = ctx.sb("exps", [128, 512], BF16)
    C_tok = ctx.sb("C_tok", [128, 512], BF16)
    Ktil = ctx.sb("Ktil", [128, 512], BF16)
    C_dm = ctx.sb("C_dm", [128, 512], BF16)
    A_sc = ctx.sb("A_sc", [128, NH * 128], BF16)
    o_sc = ctx.sb("o_sc", [128, NH * 128], BF16)
    S_sb = ctx.sb("S_sb", [128, 512], F32)
    S_bf = ctx.sb("S_bf", [128, 512], BF16)
    boff_sb = ctx.sb("boff_sb", [128, NG], F32)
    boffe_sb = ctx.sb("boffe_sb", [128, 8 * NG], F32)
    km_sb = ctx.sb("km_sb", [1, NG], F32)
    kn_row = ctx.scratch("sn_mean", [1, T], F32)

    sL = p.sem("a0l")
    sO = p.sem("a0o")

    p.sync.op("dma_start", out=cos_sb, in_=cosd, inc=[(sL, 16)])
    p.sync.op("dma_start", out=sin_sb, in_=sind, inc=[(sL, 16)])
    p.sync.op("dma_start", out=psw_sb, in_=PSW, inc=[(sL, 16)])
    p.sync.op("dma_start", out=cum_sb, in_=CUM, inc=[(sL, 16)])
    p.sync.op("dma_start", out=suf_sb, in_=SUF, inc=[(sL, 16)])
    p.sync.op("dma_start", out=idt_sb, in_=IDT, inc=[(sL, 16)])
    p.sync.wait(sL)

    emit_stream_norm(ctx, xsh, h_bf, T)
    emit_proj(ctx, h_bf, Wq, q_bf, DM, T, "q")
    emit_proj(ctx, h_bf, Wk, k_bf, 512, T, "k")
    emit_proj(ctx, h_bf, Wv, v_bf, 512, T, "v")

    for h in range(NH):
        b = h % 2
        if h >= 2:
            p.sync.wait(sO)
        emit_rope_head(ctx, q_bf[:, h * T:(h + 1) * T], sq_t[b], T,
                       cos_sb, sin_sb, psw_sb)
        p.sync.op("dma_start", out=sq_out[h * 128:(h + 1) * 128, :],
                  in_=sq_t[b], inc=[(sO, 16)])
    p.barrier()
    for g in range(NG):
        emit_rope_head(ctx, k_bf[:, g * T:(g + 1) * T],
                       sk_sb[:, g * T:(g + 1) * T], T, cos_sb, sin_sb, psw_sb)
        p.sync.op("dma_start", out=sk_out[g * 128:(g + 1) * 128, :],
                  in_=sk_sb[:, g * T:(g + 1) * T], inc=[(sO, 16)])

    p.barrier()
    for g in range(NG):
        p.barrier()
        p.scalar.activation(bsx, k_bf[:, g * T:g * T + 512], AF.Square)
        p.scalar.activation(gl_bf[:, 0:512], k_bf[:, g * T + 512:g * T + 1024],
                            AF.Square)
        p.barrier()
        p.tensor.matmul(ctx.prow(0)[:, 0:512], ctx.ones_col_bf, bsx,
                        start=True, stop=True)
        p.tensor.matmul(ctx.prow(0)[:, 512:1024], ctx.ones_col_bf,
                        gl_bf[:, 0:512], start=True, stop=True)
        p.barrier()
        p.vector.tensor_copy(kn_row, ctx.prow(0))
        p.barrier()
        p.vector.reduce_max(km_sb[:, g:g + 1], kn_row, axis=AX.X)
    p.barrier()
    p.sync.op("dma_start", out=kmax_out, in_=km_sb, inc=[(sO, 16)])

    p.barrier()
    ntr = 0
    sTT = p.sem("a0tt")
    sTV = p.sem("a0tv")
    for tt in range(8):
        for g in range(NG):
            half = ntr % 2
            bview = ctx.bpool[0][:, half * 1024: half * 1024 + 128]
            if ntr >= 2:
                p.tensor.wait(sTV, ntr - 1)
            p.tensor.op("transpose", bview,
                        k_bf[:, g * T + tt * 128: g * T + (tt + 1) * 128],
                        idt_sb, inc=[(sTT, 1)])
            ntr += 1
            p.vector.wait(sTT, ntr)
            p.vector.op("tensor_copy",
                        ktok[:, tt * 512 + g * 128: tt * 512 + (g + 1) * 128],
                        bview, inc=[(sTV, 1)])
            half = ntr % 2
            bview = ctx.bpool[0][:, half * 1024: half * 1024 + 128]
            if ntr >= 2:
                p.tensor.wait(sTV, ntr - 1)
            p.tensor.op("transpose", bview,
                        v_bf[:, g * T + tt * 128: g * T + (tt + 1) * 128],
                        idt_sb, inc=[(sTT, 1)])
            ntr += 1
            p.vector.wait(sTT, ntr)
            p.vector.op("tensor_copy",
                        vtok[:, tt * 512 + g * 128: tt * 512 + (g + 1) * 128],
                        bview, inc=[(sTV, 1)])
    p.barrier()
    for tt in range(8):
        p.sync.op("dma_start",
                  out=vtok_out[tt * 128:(tt + 1) * 128, :],
                  in_=vtok[:, tt * 512:(tt + 1) * 512], inc=[(sO, 16)])

    p.barrier()
    sn_tmp = ctx.scratch("sn_tmp", [128, T], F32)
    t1f = sn_tmp[:, 0:512]
    t2f = sn_tmp[:, 512:1024]
    mcol = ctx.sb("mcol", [128, 1], F32)
    zcol = ctx.sb("zcol", [128, 1], F32)
    for tt in range(8):
        sl = slice(tt * 512, (tt + 1) * 512)
        p.barrier()
        p.scalar.activation(t1f, ktok[:, sl], AF.Abs)
        p.barrier()
        p.scalar.activation(t1f, t1f, AF.Exp, scale=-1.0)
        p.barrier()
        p.scalar.activation(t1f, t1f, AF.Ln, bias=1.0, scale=1.0)
        p.barrier()
        p.scalar.activation(t2f, ktok[:, sl], AF.Relu, scale=-1.0)
        p.barrier()
        p.vector.tensor_add(t1f, t1f, t2f)
        p.vector.tensor_scalar_mul(gl_bf[:, sl], t1f, -1.0 / 16.0)
        for g in range(NG):
            gsl = slice(tt * 512 + g * 128, tt * 512 + (g + 1) * 128)
            p.barrier()
            p.vector.reduce_max(mcol, ktok[:, gsl], axis=AX.X, negate=True)
            p.barrier()
            p.scalar.activation(t1f[:, 0:128], ktok[:, gsl], AF.Exp, bias=mcol,
                                accum_out=zcol)
            p.barrier()
            p.vector.reciprocal(zcol, zcol)
            p.barrier()
            p.vector.tensor_single_scalar(kstok[:, gsl], t1f[:, 0:128], zcol,
                                          ALU.mult)
    p.barrier()

    emit_bound_softmax_fm(ctx, q_bf, NH, T, bsx)

    gz = p.sem("a0gz")
    p.gpsimd.op("memset", S_sb, 0.0, inc=[(gz, 1)])
    p.gpsimd.op("memset", S_bf, 0.0, inc=[(gz, 1)])
    p.gpsimd.op("memset", boff_sb, 0.0, inc=[(gz, 1)])
    p.barrier()
    for sc in range(8):
        chunk_start = (sc % 4 == 0)
        sl = slice(sc * 512, (sc + 1) * 512)
        if chunk_start and sc > 0:
            p.gpsimd.memset(S_sb, 0.0)
            p.gpsimd.memset(S_bf, 0.0)
            p.gpsimd.memset(boff_sb, 0.0)
            p.barrier()
        p.scalar.activation(boffe_sb[:, sc * NG:(sc + 1) * NG], boff_sb, AF.Exp)
        p.barrier()
        for g in range(NG):
            p.tensor.matmul(ctx.pfull(0)[:, g * 128:(g + 1) * 128],
                            gl_bf[:, sc * 512 + g * 128: sc * 512 + (g + 1) * 128],
                            cum_sb, start=True, stop=True)
        p.tensor.matmul(ctx.pfull(1)[:, 0:512], cum_sb, gl_bf[:, sl],
                        start=True, stop=True)
        p.tensor.matmul(ctx.pfull(2)[:, 0:512], suf_sb, gl_bf[:, sl],
                        start=True, stop=True)
        p.barrier()
        p.scalar.activation(expB, ctx.pfull(0)[:, 0:512], AF.Exp)
        p.scalar.activation(expnB, ctx.pfull(1)[:, 0:512], AF.Exp, scale=-1.0)
        p.scalar.activation(exps, ctx.pfull(2)[:, 0:512], AF.Exp)
        p.barrier()
        for g in range(NG):
            p.vector.tensor_add(boff_sb[:, g:g + 1], boff_sb[:, g:g + 1],
                                ctx.pfull(0)[:, g * 128 + 127: g * 128 + 128])
        p.vector.tensor_mul(C_tok, kstok[:, sl], expnB)
        p.vector.tensor_mul(Ktil, kstok[:, sl], exps)
        p.barrier()
        for g in range(NG):
            half = g % 2
            bview = ctx.bpool[0][:, half * 1024: half * 1024 + 128]
            p.tensor.op("transpose", bview, C_tok[:, g * 128:(g + 1) * 128],
                        idt_sb)
            p.barrier()
            p.vector.tensor_copy(C_dm[:, g * 128:(g + 1) * 128], bview)
            p.barrier()
        for h in range(NH):
            g = h // 4
            p.vector.tensor_mul(A_sc[:, h * 128:(h + 1) * 128],
                                q_bf[:, h * T + sc * 128: h * T + sc * 128 + 128],
                                expB[:, g * 128:(g + 1) * 128])
        p.barrier()
        p.sync.op("dma_start",
                  out=acache_out[sc].rearrange("h d t -> d h t"),
                  in_=A_sc, inc=[(sO, 16)])
        for h in range(NH):
            g = h // 4
            pb = ctx.pv(4 + (h % 2))
            p.tensor.matmul(pb[:, 0:128], C_dm[:, g * 128:(g + 1) * 128],
                            A_sc[:, h * 128:(h + 1) * 128], start=True, stop=True)
            p.barrier()
            p.vector.tensor_mul(bsx[:, 0:128], pb[:, 0:128], cum_sb)
            p.barrier()
            ob = ctx.pfull(1)[:, (h % 4) * 128:(h % 4) * 128 + 128]
            p.tensor.matmul(ob, vtok[:, sc * 512 + g * 128: sc * 512 + (g + 1) * 128],
                            bsx[:, 0:128], start=True, stop=chunk_start)
            if not chunk_start:
                p.tensor.matmul(ob, S_bf[:, g * 128:(g + 1) * 128],
                                A_sc[:, h * 128:(h + 1) * 128], start=False,
                                stop=True)
            p.barrier()
            p.vector.tensor_scalar_mul(o_sc[:, h * 128:(h + 1) * 128], ob, 0.5)
            p.barrier()
        p.sync.op("dma_start",
                  out=ogla_out[sc].rearrange("h d t -> d h t"),
                  in_=o_sc, inc=[(sO, 16)])
        for g in range(NG):
            p.tensor.matmul(ctx.pfull(2)[:, g * 128:(g + 1) * 128],
                            Ktil[:, g * 128:(g + 1) * 128],
                            vtok[:, sc * 512 + g * 128: sc * 512 + (g + 1) * 128],
                            start=True, stop=True)
        p.barrier()
        for g in range(NG):
            p.vector.scalar_tensor_tensor(
                S_sb[:, g * 128:(g + 1) * 128], S_sb[:, g * 128:(g + 1) * 128],
                expB[:, g * 128 + 127: g * 128 + 128],
                ctx.pfull(2)[:, g * 128:(g + 1) * 128], ALU.mult, ALU.add)
        p.barrier()
        p.vector.tensor_copy(S_bf, S_sb)
        p.barrier()
        if sc % 4 == 3:
            ci = sc // 4
            p.scalar.activation(t1f[:, 0:NG], boff_sb, AF.Exp)
            p.barrier()
            p.sync.op("dma_start", out=U_out[ci].rearrange("g d j -> d g j"),
                      in_=S_sb, inc=[(sO, 16)])
            p.sync.op("dma_start", out=D_out[ci],
                      in_=t1f[:, 0:NG], inc=[(sO, 16)])
            p.barrier()
    p.barrier()
    p.sync.op("dma_start", out=boff_out, in_=boffe_sb, inc=[(sO, 16)])
    p.sync.wait(sO)
    p.gpsimd.wait(sO)
    return ctx




SC128 = 1.0 / math.sqrt(128.0)


def emit_proj_resid(ctx, src_bf, W, xsh, xout, T, scale, sO, nj=DCH,
                    h2_in=None, h2_out=None, bias_col=None):
    p = ctx.p
    nt = T // 512
    nev = 2 if h2_in is not None else 1
    wsl = [ctx.scratch(f"pr_w{i}", [128, nj * 128], BF16) for i in range(2)]
    xin = ctx.scratch("pr_xin", [128, T], F32)
    xin2 = ctx.scratch("pr_xin2", [128, T], F32)
    xo = ctx.scratch("pr_xo", [128, T], F32)
    if h2_in is not None:
        hin = ctx.scratch("pr_hin", [128, T], F32)
        ho = ctx.scratch("pr_ho", [128, T], F32)
    sW = [p.sem("prw0"), p.sem("prw1")]
    sM = p.sem("prm")
    sV = p.sem("prv")
    sX = p.sem("prx")
    p.barrier()
    for ot in range(DCH):
        b = ot % 2
        buf = xin if b == 0 else xin2
        if ot >= 2:
            p.gpsimd.wait(sM, (ot - 1) * nt)
            p.gpsimd.wait(sV, (ot - 1) * nt * nev)
        p.gpsimd.op("dma_start", out=wsl[b],
                    in_=W[:, ot * 128:(ot + 1) * 128].rearrange(
                        "(c p) f -> p c f", p=128), inc=[(sW[b], 16)])
        p.gpsimd.op("dma_start", out=buf, in_=xsh[ot * 128:(ot + 1) * 128, :],
                    inc=[(sX, 16)])
        if h2_in is not None:
            if ot >= 1:
                p.gpsimd.wait(sV, ot * nt * nev)
            p.gpsimd.op("dma_start", out=hin,
                        in_=h2_in[ot * 128:(ot + 1) * 128, :], inc=[(sX, 16)])
        p.tensor.wait(sW[b])
        if ot >= 1:
            p.tensor.wait(sV, ot * nt * nev)
        for tt in range(nt):
            pb = ctx.pv((ot % 2) * 2 + tt)
            for c in range(nj):
                rsl = slice(c * T + tt * 512, c * T + tt * 512 + 512)
                p.tensor.op("matmul", pb, wsl[b][:, c * 128:(c + 1) * 128],
                            src_bf[:, rsl], start=(c == 0), stop=(c == nj - 1),
                            inc=[(sM, 1)] if c == nj - 1 else None)
        p.vector.wait(sM, (ot + 1) * nt)
        p.vector.wait(sX)
        if ot >= 1:
            p.vector.wait(sO, 16 * ot * nev)
        for tt in range(nt):
            pb = ctx.pv((ot % 2) * 2 + tt)
            ts = slice(tt * 512, (tt + 1) * 512)
            if bias_col is not None:
                p.vector.op("scalar_tensor_tensor", xo[:, ts], pb,
                            bias_col[:, ot:ot + 1], buf[:, ts],
                            ALU.add, ALU.add, inc=[(sV, 1)])
            else:
                p.vector.op("scalar_tensor_tensor", xo[:, ts], pb, scale,
                            buf[:, ts], ALU.mult, ALU.add, inc=[(sV, 1)])
            if h2_in is not None:
                p.vector.op("scalar_tensor_tensor", ho[:, ts], pb, 1.0,
                            hin[:, ts], ALU.mult, ALU.add, inc=[(sV, 1)])
        p.sync.wait(sV, (ot + 1) * nt * nev)
        p.sync.op("dma_start", out=xout[ot * 128:(ot + 1) * 128, :],
                  in_=xo, inc=[(sO, 16)])
        if h2_in is not None:
            p.sync.op("dma_start", out=h2_out[ot * 128:(ot + 1) * 128, :],
                      in_=ho, inc=[(sO, 16)])
    p.barrier()


def build_flash(mode="a0", ctx=None):
    if ctx is None:
        ctx = Ctx("flash_" + mode, bf16_banks=2)
    p = ctx.p
    T = TPC
    NH, NG = 16, 4
    xsh = ctx.din("xsh", [DM, T], F32)
    sq_in = ctx.din("sq_in", [DM, T], BF16)
    skall = ctx.din("skall", [NCORES, 512, T], BF16)
    vall = ctx.din("vall", [NCORES, T, 512], BF16)
    skown = ctx.din("skown", [512, T], BF16)
    vown = ctx.din("vown", [T, 512], BF16)
    if mode == "a0":
        ogla = ctx.din("ogla", [8, NH, 128, 128], BF16)
        acache = ctx.din("acache", [8, NH, 128, 128], BF16)
        boffe = ctx.din("boffe", [128, 32], F32)
        Dg = ctx.din("Dg", [NCORES, 2, 128, NG], F32)
        Ug = ctx.din("Ug", [NCORES, 2, NG, 128, 128], F32)
        selpf = ctx.din("selpf", [128, 32], F32)
    else:
        hidden_in = ctx.din("hidden_in", [DM, T], F32)
        h2_out = ctx.dout("h2_out", [DM, T], F32)
    kmx = ctx.din("kmx", [1, 32], F32)
    zsel = ctx.din("zsel", [128, 32], F32)
    DMASK = ctx.din("DMASK", [4, 128, 512], BF16)
    Wo = ctx.din("Wo", [DM, DM], F32)
    xout = ctx.dout("xout", [DM, T], F32)

    ctx.alloc_consts()
    sq_sb = ctx.sb("sq_sb", [128, NH * T], BF16)
    OG = ctx.sb("OG", [128, NH * T], BF16)
    sko_sb = ctx.sb("sko", [128, NG * T], BF16)
    vo_sb = ctx.sb("vo", [128, 8 * 512], BF16)
    kvr = [ctx.sb("kvr", [128, 4096], BF16) for _ in range(2)]
    Spf = ctx.sb("Spf", [128, 2 * 512], F32)
    Spf_bf = ctx.sb("Spf_bf", [128, 2 * 512], BF16)
    dld = [ctx.sb("dld", [128, NG], F32) for _ in range(2)]
    uld2 = [ctx.sb("uld", [128, NG * 128], F32) for _ in range(2)]
    og_t = ctx.sb("og_t", [128, NH * 128], BF16)
    ac_t = ctx.sb("ac_t", [128, NH * 128], BF16)
    bo_sb = ctx.sb("bo_sb", [128, 32], F32)
    spf_sb = ctx.sb("spf_sb", [128, 32], F32)
    a2_t = ctx.sb("a2_t", [128, 128], BF16)
    km_sb = ctx.sb("km_sb", [1, 32], F32)
    kr_sb = ctx.sb("kr_sb", [1, NG], F32)
    zs_sb = ctx.sb("zs_sb", [128, 32], F32)
    dmask_sb = ctx.sb("dmask_sb", [128, 4 * 512], BF16)
    bsx = ctx.sb("bsx", [128, 512], BF16)
    row1 = ctx.sb("row1", [1, 512], F32)
    mrow = ctx.sb("mrow", [1, 1], F32)
    mcol = ctx.sb("mcol", [128, 2], F32)
    bcol = ctx.sb("bcol", [128, 2 * 2], F32)
    Pbuf = ctx.sb("Pbuf", [128, 3 * 512], BF16)
    linv = ctx.sb("linv", [1, 512], F32)
    linv_bc = ctx.sb("linv_bc", [128, 512], F32)
    t1f = ctx.sb("t1f", [128, 512], F32)

    sL = p.sem("bL")
    sO = p.sem("bO")
    sUD = p.sem("bUD")

    for h2 in range(NH):
        p.sync.op("dma_start", out=sq_sb[:, h2 * T:(h2 + 1) * T],
                  in_=sq_in[h2 * 128:(h2 + 1) * 128, :], inc=[(sL, 16)])
    for g in range(NG):
        p.sync.op("dma_start", out=sko_sb[:, g * T:(g + 1) * T],
                  in_=skown[g * 128:(g + 1) * 128, :], inc=[(sL, 16)])
    p.sync.op("dma_start", out=vo_sb,
              in_=vown.rearrange("(s p) j -> p s j", p=128), inc=[(sL, 16)])
    if mode == "a0":
        p.sync.op("dma_start", out=bo_sb, in_=boffe, inc=[(sL, 16)])
        p.sync.op("dma_start", out=spf_sb, in_=selpf, inc=[(sL, 16)])
    p.sync.op("dma_start", out=km_sb, in_=kmx,
              allow_slow_non_contiguous=True, inc=[(sL, 16)])
    p.sync.op("dma_start", out=zs_sb, in_=zsel, inc=[(sL, 16)])
    p.sync.op("dma_start", out=dmask_sb,
              in_=DMASK.rearrange("s p t -> p s t"), inc=[(sL, 16)])
    p.sync.wait(sL)
    p.barrier()

    gz = p.sem("bgz")
    p.gpsimd.op("memset", Spf, 0.0, inc=[(gz, 1)])
    if mode != "a0":
        p.gpsimd.op("memset", OG, 0.0, inc=[(gz, 1)])
    p.barrier()
    for slot in (range(2) if mode == "a0" else []):
        for j in range(15):
            b = j % 2
            own = j if j <= 7 else 15 - j
            osl = 0 if j <= 7 else 1
            scol = spf_sb[:, slot * 16 + j: slot * 16 + j + 1]
            p.barrier()
            p.sync.op("dma_start", out=dld[b], in_=Dg[own, osl],
                      inc=[(sUD, 16)])
            p.sync.op("dma_start", out=uld2[b],
                      in_=Ug[own, osl].rearrange("g d j2 -> d g j2"),
                      inc=[(sUD, 16)])
            p.vector.wait(sUD)
            p.barrier()
            p.vector.tensor_scalar_add(dld[b], dld[b], -1.0)
            p.vector.tensor_single_scalar(dld[b], dld[b], scol, ALU.mult)
            p.vector.tensor_scalar_add(dld[b], dld[b], 1.0)
            p.vector.tensor_single_scalar(uld2[b], uld2[b], scol, ALU.mult)
            p.barrier()
            for g in range(NG):
                p.vector.scalar_tensor_tensor(
                    Spf[:, slot * 512 + g * 128: slot * 512 + (g + 1) * 128],
                    Spf[:, slot * 512 + g * 128: slot * 512 + (g + 1) * 128],
                    dld[b][:, g:g + 1],
                    uld2[b][:, g * 128:(g + 1) * 128], ALU.mult, ALU.add)
    p.barrier()
    p.vector.tensor_copy(Spf_bf, Spf)
    p.barrier()

    sC = p.sem("bC")
    for slot in (range(2) if mode == "a0" else []):
        for lsc in range(4):
            sc = slot * 4 + lsc
            p.barrier()
            p.sync.op("dma_start", out=og_t,
                      in_=ogla[sc].rearrange("h d t -> d h t"), inc=[(sC, 16)])
            p.sync.op("dma_start", out=ac_t,
                      in_=acache[sc].rearrange("h d t -> d h t"), inc=[(sC, 16)])
            p.vector.wait(sC)
            p.tensor.wait(sC)
            p.barrier()
            for h in range(NH):
                g = h // 4
                p.vector.tensor_single_scalar(a2_t, ac_t[:, h * 128:(h + 1) * 128],
                                              bo_sb[:, sc * NG + g: sc * NG + g + 1],
                                              ALU.mult)
                p.barrier()
                p.tensor.matmul(ctx.pv(0, 128),
                                Spf_bf[:, slot * 512 + g * 128: slot * 512 + (g + 1) * 128],
                                a2_t, start=True, stop=True)
                p.barrier()
                p.vector.scalar_tensor_tensor(
                    OG[:, h * T + sc * 128: h * T + (sc + 1) * 128],
                    ctx.pv(0, 128), 0.5, og_t[:, h * 128:(h + 1) * 128],
                    ALU.mult, ALU.add)
                p.barrier()
    p.barrier()

    p.vector.reduce_max(kr_sb, km_sb.rearrange("o (g c) -> o g c", c=8), axis=AX.X)
    p.barrier()
    p.scalar.activation(kr_sb, kr_sb, AF.Ln, bias=ctx.cbias(1e-12, 1), scale=1.0)
    p.scalar.activation(kr_sb, kr_sb, AF.Exp, scale=0.5)
    p.barrier()

    HGRPS = [[2 * i, 2 * i + 1] for i in range(8)]
    nQK = p.sem("bQK")
    nEX = p.sem("bEX")
    nMK = p.sem("bMK")
    nAV = p.sem("bAV")
    nBE = p.sem("bBE")
    nKV = p.sem("bKV")
    nEV = p.sem("bEV")

    for slot in range(2):
        nkc = 8 if slot == 0 else 16
        qsl = slice(slot * 512, slot * 512 + 512)
        for gi, grp in enumerate(HGRPS):
            p.barrier()
            for hi, h in enumerate(grp):
                g = h // 4
                p.barrier()
                p.scalar.activation(bsx, sq_sb[:, h * T + slot * 512:
                                               h * T + slot * 512 + 512], AF.Square)
                p.barrier()
                p.tensor.matmul(ctx.pv(4)[0:1, :], ctx.ones_col_bf, bsx,
                                start=True, stop=True)
                p.barrier()
                p.scalar.activation(row1, ctx.pv(4)[0:1, :], AF.Ln,
                                    bias=ctx.cbias(1e-12, 1), scale=1.0)
                p.scalar.activation(row1, row1, AF.Exp, scale=0.5)
                p.barrier()
                p.vector.tensor_single_scalar(row1, row1, kr_sb[:, g:g + 1],
                                              ALU.mult)
                p.barrier()
                p.vector.reduce_max(mrow, row1, axis=AX.X)
                p.barrier()
                p.tensor.matmul(ctx.pv(4)[:, 0:1], ctx.ones_row_f, mrow,
                                start=True, stop=True)
                p.barrier()
                p.vector.tensor_scalar_mul(mcol[:, hi:hi + 1], ctx.pv(4)[:, 0:1],
                                           -SC128)
            p.barrier()

            tiles = []
            for hi, h in enumerate(grp):
                for sub in range(4):
                    tiles.append(("diag", hi, h, sub))
            for kc in range(nkc):
                for hi, h in enumerate(grp):
                    for sub in range(4):
                        tiles.append(("main", hi, h, sub, kc))
            ntile = len(tiles)
            h_first = {}
            h_last = {}
            for idx, t_ in enumerate(tiles):
                hh = t_[2]
                if hh not in h_first:
                    h_first[hh] = idx
                h_last[hh] = idx

            qk0 = nQK.rel
            ex0 = nEX.rel
            av0 = nAV.rel
            be0 = nBE.rel
            kv0 = nKV.rel
            mk0 = nMK.rel
            nh = len(grp)

            for kc in range(nkc):
                b = kc % 2
                own = kc if kc <= 7 else 15 - kc
                osl = 0 if kc <= 7 else 1
                if kc >= 2:
                    p.gpsimd.wait(nAV, av0 + 4 * nh * kc)
                p.gpsimd.op("dma_start", out=kvr[b][:, 0:2048],
                            in_=skall[own][:, osl * 512:(osl + 1) * 512].rearrange(
                                "(g p) t -> p g t", p=128), inc=[(nKV, 16)])
                p.gpsimd.op("dma_start", out=kvr[b][:, 2048:4096],
                            in_=vall[own][osl * 512:(osl + 1) * 512, :].rearrange(
                                "(s p) j -> p s j", p=128), inc=[(nKV, 16)])

            def _cols(kc):
                ring = kc % 2
                if kc >= 2:
                    p.vector.wait(nQK, qk0 + 4 * nh * kc)
                    p.vector.wait(nEX, ex0 + 4 * nh * kc)
                for hi2, h2 in enumerate(grp):
                    p.vector.op("tensor_add",
                                bcol[:, ring * 2 + hi2: ring * 2 + hi2 + 1],
                                mcol[:, hi2:hi2 + 1],
                                zs_sb[:, slot * 16 + kc: slot * 16 + kc + 1],
                                inc=[(nBE, 1)])
            for kc in range(min(2, nkc)):
                _cols(kc)
            for idx in range(4 * nh):
                p.vector.wait(nEX, ex0 + idx + 1)
                sub = idx % 4
                p.vector.op("tensor_mul", Pbuf[:, (idx % 3) * 512:(idx % 3 + 1) * 512],
                            Pbuf[:, (idx % 3) * 512:(idx % 3 + 1) * 512],
                            dmask_sb[:, sub * 512:(sub + 1) * 512], inc=[(nMK, 1)])
            for kc in range(2, nkc):
                _cols(kc)

            for idx, t_ in enumerate(tiles):
                hi = t_[1]
                if t_[0] == "diag":
                    bias_ap = mcol[:, hi:hi + 1]
                else:
                    kc_ = t_[4]
                    bias_ap = bcol[:, (kc_ % 2) * 2 + hi: (kc_ % 2) * 2 + hi + 1]
                    p.scalar.wait(nBE, be0 + kc_ * nh + hi + 1)
                p.scalar.wait(nQK, qk0 + idx + 1)
                if idx >= 3:
                    p.scalar.wait(nAV, av0 + idx - 2)
                p.scalar.op("activation", Pbuf[:, (idx % 3) * 512:(idx % 3 + 1) * 512],
                            ctx.pv(4 + idx % 2), AF.Exp, scale=SC128,
                            bias=bias_ap, inc=[(nEX, 1)])

            ndiag = 4 * nh
            for idx, t_ in enumerate(tiles):
                kind, hi, h, sub = t_[0], t_[1], t_[2], t_[3]
                g = h // 4
                if kind == "diag":
                    lhs = sko_sb[:, g * T + slot * 512 + sub * 128:
                                 g * T + slot * 512 + (sub + 1) * 128]
                else:
                    kc = t_[4]
                    lhs = kvr[kc % 2][:, g * 512 + sub * 128:
                                      g * 512 + (sub + 1) * 128]
                    p.tensor.wait(nKV, kv0 + 32 * (kc + 1))
                if idx >= 2:
                    p.tensor.wait(nEX, ex0 + idx - 1)
                p.tensor.op("matmul", ctx.pv(4 + idx % 2), lhs,
                            sq_sb[:, h * T + qsl.start: h * T + qsl.start + 512],
                            start=True, stop=True, inc=[(nQK, 1)])
                for cidx in ([idx - 1] if idx >= 1 else []) + ([idx] if idx == ntile - 1 else []):
                    ct = tiles[cidx]
                    chi, ch = ct[1], ct[2]
                    cg = ch // 4
                    if cidx < ndiag:
                        p.tensor.wait(nMK, mk0 + cidx + 1)
                    else:
                        p.tensor.wait(nEX, ex0 + cidx + 1)
                    pb = Pbuf[:, (cidx % 3) * 512:(cidx % 3 + 1) * 512]
                    st = (cidx == h_first[ch])
                    sp = (cidx == h_last[ch])
                    p.tensor.matmul(ctx.pv(2 + chi)[0:1, :], ctx.ones_col_bf, pb,
                                    start=st, stop=sp)
                    if ct[0] == "diag":
                        vlhs = vo_sb[:, (slot * 4 + ct[3]) * 512 + cg * 128:
                                     (slot * 4 + ct[3]) * 512 + (cg + 1) * 128]
                    else:
                        vlhs = kvr[ct[4] % 2][:, 2048 + ct[3] * 512 + cg * 128:
                                              2048 + ct[3] * 512 + (cg + 1) * 128]
                    p.tensor.op("matmul", ctx.pv(chi), vlhs, pb,
                                start=st, stop=sp, inc=[(nAV, 1)])

            p.barrier()
            for hi, h in enumerate(grp):
                p.barrier()
                p.vector.reciprocal(linv, ctx.pv(2 + hi)[0:1, :])
                p.barrier()
                p.tensor.matmul(ctx.pv(4), ctx.ones_row_f, linv, start=True,
                                stop=True)
                p.barrier()
                p.vector.tensor_copy(linv_bc, ctx.pv(4))
                p.barrier()
                p.vector.tensor_mul(t1f, ctx.pv(hi), linv_bc)
                p.barrier()
                osl = slice(h * T + slot * 512, h * T + slot * 512 + 512)
                p.vector.scalar_tensor_tensor(OG[:, osl], t1f,
                                              0.5 if mode == "a0" else 1.0,
                                              OG[:, osl], ALU.mult, ALU.add)
            p.barrier()

    if mode == "a0":
        emit_proj_resid(ctx, OG, Wo, xsh, xout, T, 1.0, sO)
    else:
        emit_proj_resid(ctx, OG, Wo, xsh, xout, T, 0.5, sO,
                        h2_in=hidden_in, h2_out=h2_out)
    p.sync.wait(sO)
    p.gpsimd.wait(sO)
    return ctx


def build_a0b():
    return build_flash("a0")




def emit_headnorm(ctx, buf, nheads, T, w_col, eps=1e-6):
    p = ctx.p
    nt = T // 512
    bsx = ctx.scratch("hn_bsx", [128, 512], BF16)
    row1 = ctx.scratch("hn_row", [1, 512], F32)
    p.barrier()
    for h in range(nheads):
        for tt in range(nt):
            sl = slice(h * T + tt * 512, h * T + tt * 512 + 512)
            p.barrier()
            p.scalar.activation(bsx, buf[:, sl], AF.Square)
            p.barrier()
            p.tensor.matmul(ctx.prow(0)[:, 0:512], ctx.ones_col_bf, bsx,
                            start=True, stop=True)
            p.barrier()
            p.scalar.activation(row1, ctx.prow(0)[:, 0:512], AF.Ln,
                                bias=ctx.cbias(eps, 1), scale=1.0 / 128.0)
            p.scalar.activation(row1, row1, AF.Exp, scale=-0.5)
            p.barrier()
            p.tensor.matmul(ctx.pfull(1)[:, 0:512], ctx.ones_row_f, row1,
                            start=True, stop=True)
            p.barrier()
            p.vector.tensor_mul(buf[:, sl], buf[:, sl], ctx.pfull(1)[:, 0:512])
            p.vector.tensor_single_scalar(buf[:, sl], buf[:, sl], w_col,
                                          ALU.mult)
    p.barrier()


def build_f1a(ctx=None):
    if ctx is None:
        ctx = Ctx("f1a", bf16_banks=2)
    p = ctx.p
    T = TPC
    NH, NG = 16, 4
    xsh = ctx.din("xsh", [DM, T], F32)
    anw = ctx.din("anw", [128, DCH], F32)
    qnw = ctx.din("qnw", [128, 1], F32)
    knw = ctx.din("knw", [128, 1], F32)
    qwT = ctx.din("qwT", [DM, DM], F32)
    kwT = ctx.din("kwT", [DM, 512], F32)
    vwT = ctx.din("vwT", [DM, 512], F32)
    cosd = ctx.din("cosd", [128, T], BF16)
    sind = ctx.din("sind", [128, T], BF16)
    PSW = ctx.din("PSW", [128, 128], BF16)
    IDT = ctx.din("IDT", [128, 128], BF16)
    xout = ctx.dout("xout", [DM, T], F32)
    hidden_out = ctx.dout("hidden_out", [DM, T], F32)
    sq_out = ctx.dout("sq_out", [DM, T], BF16)
    sk_out = ctx.dout("sk_out", [512, T], BF16)
    vtok_out = ctx.dout("vtok_out", [T, 512], BF16)
    kmax_out = ctx.dout("kmax_out", [1, NG], F32)

    ctx.alloc_consts()
    h_bf = ctx.sb("h_bf", [128, DCH * T], BF16)
    h1_bf = ctx.sb("h1_bf", [128, DCH * T], BF16)
    q_bf = ctx.sb("q_bf", [128, NH * T], BF16)
    ctx.xsq_buf = q_bf[:, 0:DCH * T]
    k_bf = ctx.sb("k_bf", [128, NG * T], BF16)
    v_bf = ctx.sb("v_bf", [128, NG * T], BF16)
    sq_t = [ctx.sb("sq_t", [128, T], BF16) for _ in range(2)]
    sk_sb = ctx.sb("sk_sb", [128, NG * T], BF16)
    vtok = ctx.sb("vtok", [128, 8 * 512], BF16)
    cos_sb = ctx.sb("cos_sb", [128, T], BF16)
    sin_sb = ctx.sb("sin_sb", [128, T], BF16)
    psw_sb = ctx.sb("psw_sb", [128, 128], BF16)
    idt_sb = ctx.sb("idt_sb", [128, 128], BF16)
    anw_sb = ctx.sb("anw_sb", [128, DCH], F32)
    qnw_sb = ctx.sb("qnw_sb", [128, 1], F32)
    knw_sb = ctx.sb("knw_sb", [128, 1], F32)
    km_sb = ctx.sb("km_sb", [1, NG], F32)
    bsx = ctx.scratch("hn_bsx", [128, 512], BF16)
    kn_row = ctx.scratch("sn_mean", [1, T], F32)

    sL = p.sem("fL")
    sO = p.sem("fO")

    p.sync.op("dma_start", out=cos_sb, in_=cosd, inc=[(sL, 16)])
    p.sync.op("dma_start", out=sin_sb, in_=sind, inc=[(sL, 16)])
    p.sync.op("dma_start", out=psw_sb, in_=PSW, inc=[(sL, 16)])
    p.sync.op("dma_start", out=idt_sb, in_=IDT, inc=[(sL, 16)])
    p.sync.op("dma_start", out=anw_sb, in_=anw, inc=[(sL, 16)])
    p.sync.op("dma_start", out=qnw_sb, in_=qnw, inc=[(sL, 16)])
    p.sync.op("dma_start", out=knw_sb, in_=knw, inc=[(sL, 16)])
    p.sync.wait(sL)
    p.barrier()

    emit_stream_norm(ctx, xsh, h_bf, T, resid_xout=xout, resid_scale=0.5,
                     h_f32_out=hidden_out)
    emit_norm_fm(ctx, h_bf, h1_bf, T, rms_only=True, weight_col=anw_sb,
                 eps=1e-6)
    emit_proj(ctx, h1_bf, qwT, q_bf, DM, T, "q")
    emit_proj(ctx, h1_bf, kwT, k_bf, 512, T, "k")
    emit_proj(ctx, h1_bf, vwT, v_bf, 512, T, "v")
    emit_headnorm(ctx, q_bf, NH, T, qnw_sb)
    emit_headnorm(ctx, k_bf, NG, T, knw_sb)

    p.barrier()
    for g in range(NG):
        for tt in range(2):
            p.barrier()
            p.scalar.activation(bsx, k_bf[:, g * T + tt * 512:
                                          g * T + tt * 512 + 512], AF.Square)
            p.barrier()
            p.tensor.matmul(ctx.prow(0)[:, tt * 512:(tt + 1) * 512],
                            ctx.ones_col_bf, bsx, start=True, stop=True)
        p.barrier()
        p.vector.tensor_copy(kn_row, ctx.prow(0))
        p.barrier()
        p.vector.reduce_max(km_sb[:, g:g + 1], kn_row, axis=AX.X)
    p.barrier()
    p.sync.op("dma_start", out=kmax_out, in_=km_sb, inc=[(sO, 16)])

    for h in range(NH):
        b = h % 2
        if h >= 2:
            p.sync.wait(sO)
        emit_rope_head(ctx, q_bf[:, h * T:(h + 1) * T], sq_t[b], T,
                       cos_sb, sin_sb, psw_sb)
        p.sync.op("dma_start", out=sq_out[h * 128:(h + 1) * 128, :],
                  in_=sq_t[b], inc=[(sO, 16)])
    p.barrier()
    for g in range(NG):
        emit_rope_head(ctx, k_bf[:, g * T:(g + 1) * T],
                       sk_sb[:, g * T:(g + 1) * T], T, cos_sb, sin_sb, psw_sb)
        p.sync.op("dma_start", out=sk_out[g * 128:(g + 1) * 128, :],
                  in_=sk_sb[:, g * T:(g + 1) * T], inc=[(sO, 16)])

    p.barrier()
    ntr = 0
    sTT = p.sem("ftt")
    sTV = p.sem("ftv")
    for tt in range(8):
        for g in range(NG):
            half = ntr % 2
            bview = ctx.bpool[0][:, half * 1024: half * 1024 + 128]
            if ntr >= 2:
                p.tensor.wait(sTV, ntr - 1)
            p.tensor.op("transpose", bview,
                        v_bf[:, g * T + tt * 128: g * T + (tt + 1) * 128],
                        idt_sb, inc=[(sTT, 1)])
            ntr += 1
            p.vector.wait(sTT, ntr)
            p.vector.op("tensor_copy",
                        vtok[:, tt * 512 + g * 128: tt * 512 + (g + 1) * 128],
                        bview, inc=[(sTV, 1)])
    p.barrier()
    for tt in range(8):
        p.sync.op("dma_start", out=vtok_out[tt * 128:(tt + 1) * 128, :],
                  in_=vtok[:, tt * 512:(tt + 1) * 512], inc=[(sO, 16)])
    p.sync.wait(sO)
    p.gpsimd.wait(sO)
    return ctx


def build_moe(debug=False, ctx=None):
    if ctx is None:
        ctx = Ctx("moe", bf16_banks=2)
    p = ctx.p
    T = TPC
    E, NI = 8, 11
    xsh = ctx.din("xsh", [DM, T], F32)
    h2sh = ctx.din("h2sh", [DM, T], F32)
    fnw = ctx.din("fnw", [128, DCH], F32)
    gTf = ctx.din("gTf", [DM, E], F32)
    wgT = ctx.din("wgT", [E, DM, 1408], F32)
    wuT = ctx.din("wuT", [E, DM, 1408], F32)
    wdT = ctx.din("wdT", [E, 1408, DM], F32)
    IDTF = ctx.din("IDTF", [128, 128], F32)
    SEL = ctx.din("SEL", [E, E * 128], F32)
    xout = ctx.dout("xout", [DM, T], F32)
    if debug:
        dbg_lg = ctx.dout("dbg_lg", [128, 64], F32)
        dbg_c8 = ctx.dout("dbg_c8", [E, T], F32)
        dbg_acc = ctx.dout("dbg_acc", [DM, T], F32)
        dbg_h2n = ctx.dout("dbg_h2n", [DM, T], F32)

    ctx.alloc_consts()
    h2n_bf = ctx.sb("h2n_bf", [128, DCH * T], BF16)
    moe_acc = ctx.sb("moe_acc", [128, DCH * T], F32)
    mid = ctx.sb("mid", [128, NI * T], BF16)
    hin = [ctx.sb("hin", [128, T], F32) for _ in range(2)]
    h2n_f = ctx.sb("h2n_f", [128, T], F32)
    rstd_bc = ctx.sb("rstd_bc", [128, T], F32)
    gT_sb = ctx.sb("gT_sb", [128, DCH * E], F32)
    fnw_sb = ctx.sb("fnw_sb", [128, DCH], F32)
    idtf_sb = ctx.sb("idtf_sb", [128, 128], F32)
    idt_sb = ctx.sb("idt_sb", [128, 128], BF16)
    comb_bf = ctx.sb("comb_bf", [128, E], BF16)
    sel_sb = ctx.sb("sel_sb", [E, E * 128], F32)
    lgall = ctx.sb("lgall", [128, 8 * E], F32)
    c8 = ctx.sb("c8", [E, T], F32)
    cbc = ctx.sb("cbc", [128, T], BF16)
    wga = [ctx.sb("wga", [128, DCH * 128], BF16) for _ in range(2)]
    wua = [ctx.sb("wua", [128, DCH * 128], BF16) for _ in range(2)]
    wda = [ctx.sb("wda", [128, NI * 128], BF16) for _ in range(2)]
    s1 = ctx.sb("s1", [128, 1024], BF16)
    eus = ctx.sb("eus", [128, 512], BF16)
    xsq = ctx.sb("msq", [128, T], BF16)
    rstd_r = ctx.scratch("sn_rstd", [1, T], F32)
    t8 = ctx.sb("t8", [128, E], F32)
    m1c = ctx.sb("m1c", [128, 1], F32)
    m2c = ctx.sb("m2c", [128, 1], F32)
    zc = ctx.sb("zc", [128, 1], F32)
    comb = ctx.sb("comb", [128, E], F32)

    sL = p.sem("mL")
    sX = p.sem("mX")
    sO = p.sem("mO")
    sW = [p.sem("mw0"), p.sem("mw1")]
    sWd = [p.sem("md0"), p.sem("md1")]

    p.sync.op("dma_start", out=gT_sb,
              in_=gTf.rearrange("(c p) e -> p c e", p=128), inc=[(sL, 16)])
    p.sync.op("dma_start", out=fnw_sb, in_=fnw, inc=[(sL, 16)])
    p.sync.op("dma_start", out=idtf_sb, in_=IDTF, inc=[(sL, 16)])
    p.gpsimd.op("dma_start", out=idt_sb, in_=IDTF, inc=[(sL, 16)])
    p.sync.op("dma_start", out=sel_sb, in_=SEL, inc=[(sL, 16)])
    p.sync.wait(sL)
    p.barrier()

    for c in range(DCH):
        b = c % 2
        p.sync.op("dma_start", out=hin[b], in_=h2sh[c * 128:(c + 1) * 128, :],
                  inc=[(sX, 16)])
        p.scalar.wait(sX, 16 * (c + 1))
        p.scalar.activation(xsq, hin[b], AF.Square)
        p.barrier()
        for tt in range(2):
            p.tensor.matmul(ctx.prow(0)[:, tt * 512:(tt + 1) * 512],
                            ctx.ones_col_bf, xsq[:, tt * 512:(tt + 1) * 512],
                            start=(c == 0), stop=(c == DCH - 1))
        p.barrier()
    p.vector.tensor_scalar_mul(rstd_r, ctx.prow(0), 1.0 / DM)
    p.barrier()
    p.scalar.activation(rstd_r, rstd_r, AF.Ln, bias=ctx.cbias(1e-6, 1), scale=1.0)
    p.scalar.activation(rstd_r, rstd_r, AF.Exp, scale=-0.5)
    p.barrier()
    for tt in range(2):
        p.tensor.matmul(ctx.pfull(0)[:, tt * 512:(tt + 1) * 512],
                        ctx.ones_row_f, rstd_r[:, tt * 512:(tt + 1) * 512],
                        start=True, stop=True)
    p.barrier()
    p.vector.tensor_copy(rstd_bc, ctx.pfull(0))
    p.barrier()

    for c in range(DCH):
        b = c % 2
        p.sync.op("dma_start", out=hin[b], in_=h2sh[c * 128:(c + 1) * 128, :],
                  inc=[(sX, 16)])
        p.vector.wait(sX, 16 * (DCH + c + 1))
        p.barrier()
        p.vector.tensor_mul(h2n_f, hin[b], rstd_bc)
        p.vector.tensor_single_scalar(h2n_f, h2n_f, fnw_sb[:, c:c + 1], ALU.mult)
        p.vector.tensor_copy(h2n_bf[:, c * T:(c + 1) * T], h2n_f)
        p.barrier()
        for ts in range(8):
            p.tensor.op("matmul", ctx.pv(0)[:, ts * E:(ts + 1) * E],
                        h2n_f[:, ts * 128:(ts + 1) * 128],
                        gT_sb[:, c * E:(c + 1) * E],
                        start=(c == 0 and ts == 0), stop=(c == DCH - 1),
                        skip_group_check=True)
        p.barrier()
    p.vector.tensor_copy(lgall, ctx.pv(0)[:, 0:64])
    p.barrier()

    nm1 = ctx.sb("nm1", [128, 1], F32)
    for ts in range(8):
        lg = lgall[:, ts * E:(ts + 1) * E]
        p.barrier()
        p.vector.reduce_max(nm1, lg, axis=AX.X, negate=True)
        p.vector.reduce_max(m1c, lg, axis=AX.X)
        p.barrier()
        p.scalar.activation(t8, lg, AF.Exp, bias=nm1)
        p.barrier()
        p.vector.reduce_sum(zc, t8, axis=AX.X)
        p.barrier()
        p.vector.reciprocal(zc, zc)
        p.barrier()
        p.vector.tensor_single_scalar(t8, t8, zc, ALU.mult)
        p.barrier()
        p.vector.tensor_single_scalar(comb, lg, m1c, ALU.is_ge)
        p.barrier()
        p.vector.scalar_tensor_tensor(comb, comb, -1e30, lg, ALU.mult, ALU.add)
        p.barrier()
        p.vector.reduce_max(m2c, comb, axis=AX.X)
        p.barrier()
        p.vector.tensor_single_scalar(comb, lg, m2c, ALU.is_ge)
        p.barrier()
        p.vector.tensor_mul(t8, t8, comb)
        p.barrier()
        p.vector.reduce_sum(zc, t8, axis=AX.X)
        p.barrier()
        p.vector.tensor_scalar_add(zc, zc, 1e-9)
        p.vector.reciprocal(zc, zc)
        p.barrier()
        p.vector.tensor_single_scalar(comb, t8, zc, ALU.mult)
        p.barrier()
        p.vector.tensor_copy(comb_bf, comb)
        p.barrier()
        p.tensor.op("transpose", ctx.bpool[0][0:8, 0:128], comb_bf, idt_sb)
        p.barrier()
        p.vector.tensor_copy(c8[:, ts * 128:(ts + 1) * 128],
                             ctx.bpool[0][0:8, 0:128])
        p.barrier()

    gz = p.sem("mgz")
    p.gpsimd.op("memset", moe_acc, 0.0, inc=[(gz, 1)])
    p.barrier()
    for e in range(E):
        p.barrier()
        for tt in range(2):
            p.tensor.matmul(ctx.pv(4), sel_sb[:, e * 128:(e + 1) * 128],
                            c8[:, tt * 512:(tt + 1) * 512], start=True, stop=True)
            p.barrier()
            p.vector.tensor_copy(cbc[:, tt * 512:(tt + 1) * 512], ctx.pv(4))
            p.barrier()
        mM = p.sem("moM")
        mS = p.sem("moS")
        mV = p.sem("moV")
        m0 = mM.rel
        s0 = mS.rel
        v0 = mV.rel
        for it in range(NI):
            b = it % 2
            if it >= 2:
                p.gpsimd.wait(mM, m0 + 4 * (it - 1))
            p.gpsimd.op("dma_start", out=wga[b],
                        in_=wgT[e][:, it * 128:(it + 1) * 128].rearrange(
                            "(c p) f -> p c f", p=128), inc=[(sW[b], 16)])
            p.gpsimd.op("dma_start", out=wua[b],
                        in_=wuT[e][:, it * 128:(it + 1) * 128].rearrange(
                            "(c p) f -> p c f", p=128), inc=[(sW[b], 16)])
            p.tensor.wait(sW[b])
            if it >= 1:
                p.tensor.wait(mS, s0 + 2 * it)
                p.tensor.wait(mV, v0 + 4 * it)
            for tt in range(2):
                for c in range(DCH):
                    rsl = slice(c * T + tt * 512, c * T + tt * 512 + 512)
                    p.tensor.op("matmul", ctx.pv(0 + tt),
                                wga[b][:, c * 128:(c + 1) * 128],
                                h2n_bf[:, rsl], start=(c == 0),
                                stop=(c == DCH - 1),
                                inc=[(mM, 1)] if c == DCH - 1 else None)
                for c in range(DCH):
                    rsl = slice(c * T + tt * 512, c * T + tt * 512 + 512)
                    p.tensor.op("matmul", ctx.pv(2 + tt),
                                wua[b][:, c * 128:(c + 1) * 128],
                                h2n_bf[:, rsl], start=(c == 0),
                                stop=(c == DCH - 1),
                                inc=[(mM, 1)] if c == DCH - 1 else None)
            for tt in range(2):
                p.scalar.wait(mM, m0 + it * 4 + 2 * tt + 1)
                p.scalar.op("activation", s1[:, tt * 512:(tt + 1) * 512],
                            ctx.pv(0 + tt), AF.Silu, inc=[(mS, 1)])
                p.vector.wait(mM, m0 + it * 4 + 2 * tt + 2)
                p.vector.op("tensor_mul", eus, ctx.pv(2 + tt),
                            cbc[:, tt * 512:(tt + 1) * 512], inc=[(mV, 1)])
                p.vector.wait(mS, s0 + 2 * it + tt + 1)
                p.vector.op("tensor_mul",
                            mid[:, it * T + tt * 512: it * T + tt * 512 + 512],
                            s1[:, tt * 512:(tt + 1) * 512], eus, inc=[(mV, 1)])
        p.barrier()
        wM = p.sem("mwM")
        wV = p.sem("mwV")
        w0 = wM.rel
        x0 = wV.rel
        for ot in range(DCH):
            b = ot % 2
            if ot >= 2:
                p.gpsimd.wait(wM, w0 + 2 * (ot - 1))
            p.gpsimd.op("dma_start", out=wda[b],
                        in_=wdT[e][:, ot * 128:(ot + 1) * 128].rearrange(
                            "(i p) f -> p i f", p=128), inc=[(sWd[b], 16)])
            p.tensor.wait(sWd[b])
            if ot >= 1:
                p.tensor.wait(wV, x0 + 2 * ot)
            for tt in range(2):
                for it in range(NI):
                    rsl = slice(it * T + tt * 512, it * T + tt * 512 + 512)
                    p.tensor.op("matmul", ctx.pv(4 + tt),
                                wda[b][:, it * 128:(it + 1) * 128],
                                mid[:, rsl], start=(it == 0),
                                stop=(it == NI - 1),
                                inc=[(wM, 1)] if it == NI - 1 else None)
            for tt in range(2):
                asl = slice(ot * T + tt * 512, ot * T + tt * 512 + 512)
                p.vector.wait(wM, w0 + 2 * ot + tt + 1)
                p.vector.op("scalar_tensor_tensor", moe_acc[:, asl],
                            ctx.pv(4 + tt), 1.0, moe_acc[:, asl],
                            ALU.mult, ALU.add, inc=[(wV, 1)])
            p.barrier() if False else None
    p.barrier()
    if debug:
        p.sync.op("dma_start", out=dbg_lg, in_=lgall, inc=[(sO, 16)])
        p.sync.op("dma_start", out=dbg_c8, in_=c8, inc=[(sO, 16)])
        for c in range(DCH):
            p.sync.op("dma_start", out=dbg_acc[c * 128:(c + 1) * 128, :],
                      in_=moe_acc[:, c * T:(c + 1) * T], inc=[(sO, 16)])
        p.gpsimd.op("dma_start", out=dbg_h2n.rearrange("(c p) t -> p c t", p=128),
                    in_=h2n_bf, inc=[(sO, 16)])
        p.sync.wait(sO)
        p.barrier()
    for c in range(DCH):
        b = c % 2
        p.sync.op("dma_start", out=hin[b], in_=xsh[c * 128:(c + 1) * 128, :],
                  inc=[(sX, 16)])
        p.vector.wait(sX)
        if c >= 1:
            p.vector.wait(sO)
        p.vector.scalar_tensor_tensor(h2n_f, moe_acc[:, c * T:(c + 1) * T], 0.5,
                                      hin[b], ALU.mult, ALU.add)
        p.barrier()
        p.sync.op("dma_start", out=xout[c * 128:(c + 1) * 128, :], in_=h2n_f,
                  inc=[(sO, 16)])
    p.sync.wait(sO)
    p.gpsimd.wait(sO)
    return ctx




def build_a1a(ctx=None):
    if ctx is None:
        ctx = Ctx("a1a", bf16_banks=2)
    p = ctx.p
    T = TPC
    xsh = ctx.din("xsh", [DM, T], F32)
    Ws = ctx.din("Ws", [DM, 256], F32)
    Wa = ctx.din("Wa", [DM, 256], F32)
    IDT = ctx.din("IDT", [128, 128], BF16)
    sdm_out = ctx.dout("sdm_out", [256, T], BF16)
    adm_out = ctx.dout("adm_out", [256, T], BF16)
    htok_out = ctx.dout("htok_out", [T, DM], BF16)
    amax_out = ctx.dout("amax_out", [1, 1], F32)

    ctx.alloc_consts()
    h_bf = ctx.sb("h_bf", [128, DCH * T], BF16)
    s_bf = ctx.sb("s_bf", [128, 2 * T], BF16)
    a_bf = ctx.sb("a_bf", [128, 2 * T], BF16)
    htk = ctx.sb("htk", [128, DM], BF16)
    idt_sb = ctx.sb("idt_sb", [128, 128], BF16)
    am_sb = ctx.sb("am_sb", [1, 1], F32)
    an_row = ctx.scratch("sn_mean", [1, T], F32)
    bsx = ctx.sb("bsx", [128, 512], BF16)

    sL = p.sem("aL")
    sO = p.sem("aO")

    p.gpsimd.op("dma_start", out=idt_sb, in_=IDT, inc=[(sL, 16)])
    p.gpsimd.wait(sL)
    emit_stream_norm(ctx, xsh, h_bf, T)
    emit_proj(ctx, h_bf, Ws, s_bf, 256, T, "s")
    emit_proj(ctx, h_bf, Wa, a_bf, 256, T, "a")
    for r in range(2):
        p.sync.op("dma_start", out=sdm_out[r * 128:(r + 1) * 128, :],
                  in_=s_bf[:, r * T:(r + 1) * T], inc=[(sO, 16)])
        p.sync.op("dma_start", out=adm_out[r * 128:(r + 1) * 128, :],
                  in_=a_bf[:, r * T:(r + 1) * T], inc=[(sO, 16)])

    p.barrier()
    for tt in range(2):
        for r in range(2):
            p.barrier()
            p.scalar.activation(bsx, a_bf[:, r * T + tt * 512:
                                          r * T + tt * 512 + 512], AF.Square)
            p.barrier()
            p.tensor.matmul(ctx.prow(0)[:, tt * 512:(tt + 1) * 512],
                            ctx.ones_col_bf, bsx, start=(r == 0), stop=(r == 1))
    p.barrier()
    p.vector.tensor_copy(an_row, ctx.prow(0))
    p.barrier()
    p.vector.reduce_max(am_sb, an_row, axis=AX.X)
    p.barrier()
    p.sync.op("dma_start", out=amax_out, in_=am_sb, inc=[(sO, 16)])

    ntr = 0
    sTT = p.sem("att")
    sTV = p.sem("atv")
    p.barrier()
    for tt in range(8):
        if tt >= 1:
            p.tensor.wait(sO)
        for c in range(DCH):
            half = ntr % 2
            bview = ctx.bpool[0][:, half * 1024: half * 1024 + 128]
            if ntr >= 2:
                p.tensor.wait(sTV, ntr - 1)
            p.tensor.op("transpose", bview,
                        h_bf[:, c * T + tt * 128: c * T + (tt + 1) * 128],
                        idt_sb, inc=[(sTT, 1)])
            ntr += 1
            p.vector.wait(sTT, ntr)
            if tt >= 1 and c == 0:
                p.vector.wait(sO)
            p.vector.op("tensor_copy", htk[:, c * 128:(c + 1) * 128],
                        bview, inc=[(sTV, 1)])
        p.sync.wait(sTV, ntr)
        p.sync.op("dma_start", out=htok_out[tt * 128:(tt + 1) * 128, :],
                  in_=htk, inc=[(sO, 16)])
    p.sync.wait(sO)
    p.gpsimd.wait(sO)
    return ctx


def build_a1b(ctx=None):
    if ctx is None:
        ctx = Ctx("a1b", bf16_banks=2)
    p = ctx.p
    T = TPC
    SC16 = 1.0 / 16.0
    xsh = ctx.din("xsh", [DM, T], F32)
    sdm = ctx.din("sdm", [256, T], BF16)
    aall = ctx.din("aall", [NCORES, 256, T], BF16)
    hall = ctx.din("hall", [NCORES, T, DM], BF16)
    aown = ctx.din("aown", [256, T], BF16)
    hown = ctx.din("hown", [T, DM], BF16)
    amx = ctx.din("amx", [1, 8], F32)
    zsel = ctx.din("zsel", [128, 32], F32)
    DMASK = ctx.din("DMASK", [4, 128, 512], BF16)
    WoT = ctx.din("WoT", [DM, DM], F32)
    bo = ctx.din("bo", [128, DCH], F32)
    xout = ctx.dout("xout", [DM, T], F32)

    ctx.alloc_consts()
    s_sb = ctx.sb("s_sb", [128, 2 * T], BF16)
    OC = ctx.sb("OC", [128, DCH * T], BF16)
    ao_sb = ctx.sb("ao_sb", [128, 2 * T], BF16)
    ho_sb = ctx.sb("ho_sb", [128, 8 * DM], BF16)
    kvr = [ctx.sb("kvr", [128, 2 * 512 + 4 * DM], BF16) for _ in range(2)]
    zs_sb = ctx.sb("zs_sb", [128, 32], F32)
    am_sb = ctx.sb("am_sb", [1, 8], F32)
    ar_sb = ctx.sb("ar_sb", [1, 1], F32)
    dmask_sb = ctx.sb("dmask_sb", [128, 4 * 512], BF16)
    bsx = ctx.sb("bsx", [128, 512], BF16)
    row1 = ctx.sb("row1", [1, 512], F32)
    mrow = ctx.sb("mrow", [1, 1], F32)
    mcol = ctx.sb("mcol", [128, 1], F32)
    bcol = ctx.sb("bcol", [128, 2], F32)
    Pbuf = ctx.sb("Pbuf", [128, 3 * 512], BF16)
    linv = ctx.sb("linv", [1, 512], F32)
    linv_bc = ctx.sb("linv_bc", [128, 512], F32)
    t1f = ctx.sb("t1f", [128, 512], F32)

    sL = p.sem("cL")
    sO = p.sem("cO")

    for r in range(2):
        p.sync.op("dma_start", out=s_sb[:, r * T:(r + 1) * T],
                  in_=sdm[r * 128:(r + 1) * 128, :], inc=[(sL, 16)])
        p.sync.op("dma_start", out=ao_sb[:, r * T:(r + 1) * T],
                  in_=aown[r * 128:(r + 1) * 128, :], inc=[(sL, 16)])
    p.sync.op("dma_start", out=ho_sb,
              in_=hown.rearrange("(s p) j -> p s j", p=128), inc=[(sL, 16)])
    p.sync.op("dma_start", out=zs_sb, in_=zsel, inc=[(sL, 16)])
    p.sync.op("dma_start", out=am_sb, in_=amx, inc=[(sL, 16)])
    p.sync.op("dma_start", out=dmask_sb,
              in_=DMASK.rearrange("s p t -> p s t"), inc=[(sL, 16)])
    p.sync.wait(sL)
    p.barrier()
    p.vector.reduce_max(ar_sb, am_sb, axis=AX.X)
    p.barrier()
    p.scalar.activation(ar_sb, ar_sb, AF.Ln, bias=ctx.cbias(1e-12, 1), scale=1.0)
    p.scalar.activation(ar_sb, ar_sb, AF.Exp, scale=0.5)
    p.barrier()

    JG = [[0, 1, 2], [3, 4, 5], [6, 7, 8], [9, 10, 11], [12, 13, 14], [15]]
    nQK = p.sem("cQK")
    nEX = p.sem("cEX")
    nMK = p.sem("cMK")
    nAV = p.sem("cAV")
    nBE = p.sem("cBE")
    nKV = p.sem("cKV")

    for slot in range(2):
        nkc = 8 if slot == 0 else 16
        p.barrier()
        for r in range(2):
            p.barrier()
            p.scalar.activation(bsx, s_sb[:, r * T + slot * 512:
                                          r * T + slot * 512 + 512], AF.Square)
            p.barrier()
            p.tensor.matmul(ctx.pv(4)[0:1, :], ctx.ones_col_bf, bsx,
                            start=(r == 0), stop=(r == 1))
        p.barrier()
        p.scalar.activation(row1, ctx.pv(4)[0:1, :], AF.Ln,
                            bias=ctx.cbias(1e-12, 1), scale=1.0)
        p.scalar.activation(row1, row1, AF.Exp, scale=0.5)
        p.barrier()
        p.vector.tensor_single_scalar(row1, row1, ar_sb[:, 0:1], ALU.mult)
        p.barrier()
        p.vector.reduce_max(mrow, row1, axis=AX.X)
        p.barrier()
        p.tensor.matmul(ctx.pv(4)[:, 0:1], ctx.ones_row_f, mrow,
                        start=True, stop=True)
        p.barrier()
        p.vector.tensor_scalar_mul(mcol, ctx.pv(4)[:, 0:1], -SC16)
        p.barrier()

        for gi, grp in enumerate(JG):
            tiles = []
            for sub in range(4):
                tiles.append(("diag", sub, None))
            for kc in range(nkc):
                for sub in range(4):
                    tiles.append(("main", sub, kc))
            ntile = len(tiles)
            qk0 = nQK.rel
            ex0 = nEX.rel
            av0 = nAV.rel
            be0 = nBE.rel
            kv0 = nKV.rel
            mk0 = nMK.rel
            first = 0
            last = ntile - 1

            for kc in range(nkc):
                b = kc % 2
                own = kc if kc <= 7 else 15 - kc
                osl = 0 if kc <= 7 else 1
                if kc >= 2:
                    p.gpsimd.wait(nAV, av0 + len(grp) * 4 * (kc - 1) + len(grp) * 4)
                p.gpsimd.op("dma_start", out=kvr[b][:, 0:1024],
                            in_=aall[own][:, osl * 512:(osl + 1) * 512].rearrange(
                                "(r p) t -> p r t", p=128), inc=[(nKV, 16)])
                p.gpsimd.op("dma_start", out=kvr[b][:, 1024:1024 + 4 * DM],
                            in_=hall[own][osl * 512:(osl + 1) * 512, :].rearrange(
                                "(s p) j -> p s j", p=128), inc=[(nKV, 16)])

            def _cols(kc):
                ring = kc % 2
                if kc >= 2:
                    p.vector.wait(nQK, qk0 + 4 * kc)
                    p.vector.wait(nEX, ex0 + 4 * kc)
                p.vector.op("tensor_add", bcol[:, ring:ring + 1], mcol,
                            zs_sb[:, slot * 16 + kc: slot * 16 + kc + 1],
                            inc=[(nBE, 1)])
            for kc in range(min(2, nkc)):
                _cols(kc)
            for idx in range(4):
                p.vector.wait(nEX, ex0 + idx + 1)
                p.vector.op("tensor_mul", Pbuf[:, (idx % 3) * 512:(idx % 3 + 1) * 512],
                            Pbuf[:, (idx % 3) * 512:(idx % 3 + 1) * 512],
                            dmask_sb[:, idx * 512:(idx + 1) * 512], inc=[(nMK, 1)])
            for kc in range(2, nkc):
                _cols(kc)

            for idx in range(ntile):
                t_ = tiles[idx]
                if t_[0] == "diag":
                    bias_ap = mcol
                else:
                    kc_ = t_[2]
                    bias_ap = bcol[:, (kc_ % 2):(kc_ % 2) + 1]
                    p.scalar.wait(nBE, be0 + kc_ + 1)
                p.scalar.wait(nQK, qk0 + idx + 1)
                if idx >= 3:
                    p.scalar.wait(nAV, av0 + (idx - 2) * len(grp))
                p.scalar.op("activation", Pbuf[:, (idx % 3) * 512:(idx % 3 + 1) * 512],
                            ctx.pv(4 + idx % 2), AF.Exp, scale=SC16,
                            bias=bias_ap, inc=[(nEX, 1)])

            for idx, t_ in enumerate(tiles):
                kind, sub, kc = t_
                if kind == "diag":
                    qlhs = [ao_sb[:, r * T + slot * 512 + sub * 128:
                                  r * T + slot * 512 + (sub + 1) * 128]
                            for r in range(2)]
                else:
                    qlhs = [kvr[kc % 2][:, r * 512 + sub * 128:
                                        r * 512 + (sub + 1) * 128]
                            for r in range(2)]
                    p.tensor.wait(nKV, kv0 + 32 * (kc + 1))
                if idx >= 2:
                    p.tensor.wait(nEX, ex0 + idx - 1)
                p.tensor.matmul(ctx.pv(4 + idx % 2), qlhs[0],
                                s_sb[:, 0 * T + slot * 512:
                                     0 * T + slot * 512 + 512],
                                start=True, stop=False)
                p.tensor.op("matmul", ctx.pv(4 + idx % 2), qlhs[1],
                            s_sb[:, 1 * T + slot * 512:
                                 1 * T + slot * 512 + 512],
                            start=False, stop=True, inc=[(nQK, 1)])
                for cidx in ([idx - 1] if idx >= 1 else []) + ([idx] if idx == ntile - 1 else []):
                    ct = tiles[cidx]
                    if cidx < 4:
                        p.tensor.wait(nMK, mk0 + cidx + 1)
                    else:
                        p.tensor.wait(nEX, ex0 + cidx + 1)
                    pb = Pbuf[:, (cidx % 3) * 512:(cidx % 3 + 1) * 512]
                    st = (cidx == first)
                    sp = (cidx == last)
                    if gi == 0:
                        p.tensor.matmul(ctx.pv(3)[0:1, :], ctx.ones_col_bf, pb,
                                        start=st, stop=sp)
                    for ji, jc in enumerate(grp):
                        if ct[0] == "diag":
                            vlhs = ho_sb[:, (slot * 4 + ct[1]) * DM + jc * 128:
                                         (slot * 4 + ct[1]) * DM + (jc + 1) * 128]
                        else:
                            vlhs = kvr[ct[2] % 2][:, 1024 + ct[1] * DM + jc * 128:
                                                  1024 + ct[1] * DM + (jc + 1) * 128]
                        mm_inc = [(nAV, 1)]
                        p.tensor.op("matmul", ctx.pv(ji), vlhs, pb,
                                    start=st, stop=sp, inc=mm_inc)

            p.barrier()
            if gi == 0:
                p.vector.reciprocal(linv, ctx.pv(3)[0:1, :])
                p.barrier()
                p.tensor.matmul(ctx.pv(4), ctx.ones_row_f, linv, start=True,
                                stop=True)
                p.barrier()
                p.vector.tensor_copy(linv_bc, ctx.pv(4))
                p.barrier()
            for ji, jc in enumerate(grp):
                p.barrier()
                p.vector.tensor_mul(t1f, ctx.pv(ji), linv_bc)
                p.barrier()
                p.vector.tensor_copy(OC[:, jc * T + slot * 512:
                                        jc * T + slot * 512 + 512], t1f)
            p.barrier()

    bo_sb = ctx.sb("bo_sb", [128, DCH], F32)
    sB = p.sem("cB")
    p.gpsimd.op("dma_start", out=bo_sb, in_=bo, inc=[(sB, 16)])
    p.vector.wait(sB)
    p.barrier()
    emit_proj_resid(ctx, OC, WoT, xsh, xout, T, 1.0, sO, bias_col=bo_sb)
    p.sync.wait(sO)
    p.gpsimd.wait(sO)
    return ctx



def host_prep(I):
    f32 = lambda a: np.ascontiguousarray(np.asarray(a, np.float32))
    bfa = lambda a: np.ascontiguousarray(np.asarray(a).astype(BF))
    H = {}
    W1T = f32(I["L0_ffn_W1"]).T
    W3T = f32(I["L0_ffn_W3"]).T

    def pack_w(WT):
        w = np.ascontiguousarray(WT).reshape(DCH, 128, 64, 64, 2)
        w = np.transpose(w, (3, 0, 1, 4, 2))
        return np.ascontiguousarray(w.reshape(64, DCH, 128, 128).astype(np.float32))

    H["W1p"] = pack_w(W1T)
    H["W3p"] = pack_w(W3T)
    H["W2T"] = np.ascontiguousarray(f32(I["L0_ffn_W2"]).T)
    H["H128"] = np.ascontiguousarray(_hadamard(128).astype(np.float32))
    H64 = _hadamard(64) / 64.0
    H64B = np.zeros((128, 128), np.float64)
    H64B[:64, :64] = H64
    H64B[64:, 64:] = H64
    H["H64B"] = np.ascontiguousarray(H64B.astype(np.float32))

    PSW = np.zeros((128, 128), np.float32)
    for m in range(128):
        PSW[(m + 64) % 128, m] = 1.0
    H["PSW"] = bfa(PSW)
    s_le_t = np.tril(np.ones((128, 128), np.float32)).T
    H["CUM"] = bfa(s_le_t)
    H["SUF"] = bfa(1.0 - s_le_t)
    H["IDT"] = np.eye(128, dtype=BF)
    H["IDTF"] = np.eye(128, dtype=np.float32)
    msk = []
    for sub in range(4):
        m_ = np.zeros((128, 512), np.float32)
        for ii in range(128):
            m_[ii, sub * 128 + ii:] = 1.0
        msk.append(m_)
    H["DMASK"] = bfa(np.stack(msk))
    H["zsel"] = []
    H["selpf"] = []
    for i in range(NCORES):
        c0, c1 = CORE_CHUNKS[i]
        z = np.zeros((128, 32), np.float32)
        sel = np.zeros((128, 32), np.float32)
        for slot, cch in ((0, c0), (1, c1)):
            for kc in range(16):
                if kc >= cch:
                    z[:, slot * 16 + kc] = -1e9
            for j in range(15):
                if j < cch:
                    sel[:, slot * 16 + j] = 1.0
        H["zsel"].append(z)
        H["selpf"].append(np.ascontiguousarray(sel))

    for nm, theta in (("r0", 10000.0), ("r1", 1000000.0)):
        cos, sin = _rope_tables(SEQ, 128, theta)
        cdup = bfa(np.concatenate([cos.T, cos.T], axis=0))
        sdup = bfa(np.concatenate([-sin.T, sin.T], axis=0))
        H[f"cos_{nm}"] = [np.ascontiguousarray(cdup[:, core_tokens(i)])
                          for i in range(NCORES)]
        H[f"sin_{nm}"] = [np.ascontiguousarray(sdup[:, core_tokens(i)])
                          for i in range(NCORES)]

    H["Wq"] = f32(I["L0_attn_Wq"])
    H["Wk"] = f32(I["L0_attn_Wk"])
    H["Wv"] = f32(I["L0_attn_Wv"])
    H["Wo0"] = f32(I["L0_attn_Wo"])
    H["anw"] = np.ascontiguousarray(f32(I["L1_ffn_attn_norm_w"]).reshape(DCH, 128).T)
    H["qnw"] = f32(I["L1_ffn_q_norm_w"]).reshape(128, 1)
    H["knw"] = f32(I["L1_ffn_k_norm_w"]).reshape(128, 1)
    H["qwT"] = np.ascontiguousarray(f32(I["L1_ffn_q_w"]).T)
    H["kwT"] = np.ascontiguousarray(f32(I["L1_ffn_k_w"]).T)
    H["vwT"] = np.ascontiguousarray(f32(I["L1_ffn_v_w"]).T)
    H["owT"] = np.ascontiguousarray(f32(I["L1_ffn_o_w"]).T)
    H["fnw"] = np.ascontiguousarray(f32(I["L1_ffn_ffn_norm_w"]).reshape(DCH, 128).T)
    H["gTf"] = np.ascontiguousarray(f32(I["L1_ffn_gate_w"]).T)
    H["wgT"] = np.ascontiguousarray(np.transpose(f32(I["L1_ffn_wg"]), (0, 2, 1)))
    H["wuT"] = np.ascontiguousarray(np.transpose(f32(I["L1_ffn_wu"]), (0, 2, 1)))
    H["wdT"] = np.ascontiguousarray(np.transpose(f32(I["L1_ffn_wd"]), (0, 2, 1)))
    SEL = np.zeros((8, 8 * 128), np.float32)
    for e in range(8):
        SEL[e, e * 128:(e + 1) * 128] = 1.0
    H["SEL"] = SEL
    H["Ws1"] = f32(I["L1_attn_Ws"])
    H["Wa1"] = f32(I["L1_attn_Wa"])
    H["WoT1"] = np.ascontiguousarray(f32(I["L1_attn_Wo"]).T)
    H["bo1"] = np.ascontiguousarray(f32(I["L1_attn_bo"]).reshape(DCH, 128).T)
    return H




def _emit_gather(ctx, src2d, dst2d, name):
    p = ctx.p
    cc = p.sem("cc_" + name)
    p.barrier()
    p.gpsimd.op("collective_compute", "AllGather", ALU.bypass,
                replica_groups=[list(range(NCORES))],
                ins=[src2d], outs=[dst2d], inc=[(cc, 1)])
    p.gpsimd.wait(cc)
    p.barrier()


def build_fused():
    ctx = Ctx("fused", bf16_banks=2)
    nc, p = ctx.nc, ctx.p
    T = TPC

    EXT = {}

    def ein(name, shape, dtype=F32):
        EXT[name] = nc.dram_tensor(name, shape, dtype, kind="ExternalInput").ap()
        return EXT[name]

    xsh0 = ein("xsh", [DM, T])
    xout_e = nc.dram_tensor("xout", [DM, T], F32, kind="ExternalOutput").ap()
    for nm, shp, dt in [
        ("W2T", [SEQ, DM], F32), ("H128", [128, 128], F32),
        ("W1p", [64, DCH, 128, 128], F32), ("W3p", [64, DCH, 128, 128], F32),
        ("H64B", [128, 128], F32),
        ("Wq", [DM, DM], F32), ("Wk", [DM, 512], F32), ("Wv", [DM, 512], F32),
        ("Wo0", [DM, DM], F32),
        ("cos_r0", [128, T], BF16), ("sin_r0", [128, T], BF16),
        ("cos_r1", [128, T], BF16), ("sin_r1", [128, T], BF16),
        ("CUM", [128, 128], BF16), ("SUF", [128, 128], BF16),
        ("IDT", [128, 128], BF16), ("PSW", [128, 128], BF16),
        ("IDTF", [128, 128], F32),
        ("DMASK", [4, 128, 512], BF16), ("zsel", [128, 32], F32),
        ("selpf", [128, 32], F32),
        ("anw", [128, DCH], F32), ("qnw", [128, 1], F32), ("knw", [128, 1], F32),
        ("qwT", [DM, DM], F32), ("kwT", [DM, 512], F32), ("vwT", [DM, 512], F32),
        ("owT", [DM, DM], F32), ("fnw", [128, DCH], F32), ("gTf", [DM, 8], F32),
        ("wgT", [8, DM, 1408], F32), ("wuT", [8, DM, 1408], F32),
        ("wdT", [8, 1408, DM], F32), ("SEL", [8, 8 * 128], F32),
        ("Ws1", [DM, 256], F32), ("Wa1", [DM, 256], F32),
        ("WoT1", [DM, DM], F32), ("bo1", [128, DCH], F32),
    ]:
        ein(nm, shp, dt)

    def dint(name, shape, dtype):
        return nc.dram_tensor(name, shape, dtype, kind="Internal").ap()

    def dint_sh(name, shape, dtype):
        return nc.dram_tensor(name, shape, dtype, kind="Internal",
                              addr_space="Shared").ap()

    G2b = dint("G2b", [16, 64, 128, 128], BF16)
    xA = dint("xA", [DM, T], F32)
    xB = dint("xB", [DM, T], F32)
    hidb = dint("hidb", [DM, T], F32)
    h2b = dint("h2b", [DM, T], F32)
    sqL = dint("sqL", [DM, T], BF16)
    skL = dint("skL", [512, T], BF16)
    vtL = dint("vtL", [T, 512], BF16)
    kmL = dint("kmL", [1, 4], F32)
    oglaL = dint("oglaL", [8, 16, 128, 128], BF16)
    acL = dint("acL", [8, 16, 128, 128], BF16)
    boffL = dint("boffL", [128, 32], F32)
    DL = dint("DL", [2, 128, 4], F32)
    UL = dint("UL", [2, 4, 128, 128], F32)
    skg = dint_sh("skg", [NCORES * 512, T], BF16)
    vg = dint_sh("vg", [NCORES * T, 512], BF16)
    kmg = dint_sh("kmg", [NCORES, 4], F32)
    Dgf = dint_sh("Dgf", [NCORES * 2 * 128, 4], F32)
    Ugf = dint_sh("Ugf", [NCORES * 2 * 4 * 128, 128], F32)
    aL = dint("aL", [256, T], BF16)
    htL = dint("htL", [T, DM], BF16)
    amL = dint("amL", [1, 1], F32)
    ag = dint_sh("ag", [NCORES * 256, T], BF16)
    hg = dint_sh("hg", [NCORES * T, DM], BF16)
    amg = dint_sh("amg", [NCORES, 1], F32)

    ctx.alloc_consts()
    for v in (1e-5, 1e-6, 1e-12):
        ctx.cbias(v, 1)
    ctx.enable_arena(206 * 1024)

    kmx_view = bass.AP(kmg.tensor, 0, [[1, 4], [4, 8]])
    skg_v = skg.rearrange("(n r) t -> n r t", n=NCORES)
    vg_v = vg.rearrange("(n r) t -> n r t", n=NCORES)
    Dg_v = Dgf.rearrange("(n a b) c -> n a b c", n=NCORES, a=2)
    Ug_v = Ugf.rearrange("(n a g b) c -> n a g b c", n=NCORES, a=2, g=4)
    ag_v = ag.rearrange("(n r) t -> n r t", n=NCORES)
    hg_v = hg.rearrange("(n r) t -> n r t", n=NCORES)
    amx_view = amg.rearrange("c o -> (c o)")

    def phase(tag, builder, iomap, *args):
        ctx.phase_reset(tag)
        ctx.iomap = iomap
        builder(*args, ctx=ctx)
        ctx.iomap = None
        p.barrier()

    shared0 = {k: EXT[k] for k in ("CUM", "SUF", "IDT", "PSW", "DMASK",
                                   "zsel", "selpf", "IDTF", "SEL")}

    phase("g", build_gprep, {"W2T": EXT["W2T"], "H128": EXT["H128"], "G2": G2b})

    def f0(tag, xin, xo):
        phase(tag, build_f0, {"xsh": xin, "W1p": EXT["W1p"], "W3p": EXT["W3p"],
                              "G2": G2b, "H64B": EXT["H64B"], "xout": xo})

    def a0(tag, xin, xo):
        phase(tag + "a", build_a0a,
              {"xsh": xin, "Wq": EXT["Wq"], "Wk": EXT["Wk"], "Wv": EXT["Wv"],
               "cosd": EXT["cos_r0"], "sind": EXT["sin_r0"], **shared0,
               "sq_out": sqL, "sk_out": skL, "vtok_out": vtL,
               "ogla_out": oglaL, "acache_out": acL, "boff_out": boffL,
               "D_out": DL, "U_out": UL, "kmax_out": kmL})
        _emit_gather(ctx, skL, skg, tag + "sk")
        _emit_gather(ctx, vtL, vg, tag + "v")
        _emit_gather(ctx, kmL, kmg, tag + "km")
        _emit_gather(ctx, DL.rearrange("a b c -> (a b) c"), Dgf, tag + "D")
        _emit_gather(ctx, UL.rearrange("a g b c -> (a g b) c"), Ugf, tag + "U")
        phase(tag + "b", lambda ctx=None: build_flash("a0", ctx=ctx),
              {"xsh": xin, "sq_in": sqL, "skall": skg_v, "vall": vg_v,
               "skown": skL, "vown": vtL, "ogla": oglaL, "acache": acL,
               "boffe": boffL, "Dg": Dg_v, "Ug": Ug_v, "kmx": kmx_view,
               **shared0, "Wo": EXT["Wo0"], "xout": xo})

    def f1(tag, xin, xmid1, xmid2, xo):
        phase(tag + "a", build_f1a,
              {"xsh": xin, "anw": EXT["anw"], "qnw": EXT["qnw"],
               "knw": EXT["knw"], "qwT": EXT["qwT"], "kwT": EXT["kwT"],
               "vwT": EXT["vwT"], "cosd": EXT["cos_r1"], "sind": EXT["sin_r1"],
               **shared0, "xout": xmid1, "hidden_out": hidb, "sq_out": sqL,
               "sk_out": skL, "vtok_out": vtL, "kmax_out": kmL})
        _emit_gather(ctx, skL, skg, tag + "sk")
        _emit_gather(ctx, vtL, vg, tag + "v")
        _emit_gather(ctx, kmL, kmg, tag + "km")
        phase(tag + "b", lambda ctx=None: build_flash("f1", ctx=ctx),
              {"xsh": xmid1, "sq_in": sqL, "skall": skg_v, "vall": vg_v,
               "skown": skL, "vown": vtL, "hidden_in": hidb, "h2_out": h2b,
               "kmx": kmx_view, **shared0, "Wo": EXT["owT"], "xout": xmid2})
        phase(tag + "c", lambda ctx=None: build_moe(False, ctx=ctx),
              {"xsh": xmid2, "h2sh": h2b, "fnw": EXT["fnw"], "gTf": EXT["gTf"],
               "wgT": EXT["wgT"], "wuT": EXT["wuT"], "wdT": EXT["wdT"],
               **shared0, "xout": xo})

    def a1(tag, xin, xo):
        phase(tag + "a", build_a1a,
              {"xsh": xin, "Ws": EXT["Ws1"], "Wa": EXT["Wa1"], **shared0,
               "sdm_out": sqL[0:256, :] if False else aL,
               "adm_out": aL, "htok_out": htL, "amax_out": amL})
        pass

    sdmL = dint("sdmL", [256, T], BF16)

    def a1(tag, xin, xo):
        phase(tag + "a", build_a1a,
              {"xsh": xin, "Ws": EXT["Ws1"], "Wa": EXT["Wa1"], **shared0,
               "sdm_out": sdmL, "adm_out": aL, "htok_out": htL,
               "amax_out": amL})
        _emit_gather(ctx, aL, ag, tag + "a")
        _emit_gather(ctx, htL, hg, tag + "h")
        _emit_gather(ctx, amL, amg, tag + "am")
        phase(tag + "b", build_a1b,
              {"xsh": xin, "sdm": sdmL, "aall": ag_v, "hall": hg_v,
               "aown": aL, "hown": htL, "amx": amx_view, **shared0,
               "WoT": EXT["WoT1"], "bo": EXT["bo1"], "xout": xo})

    f0("p1", xsh0, xA)
    a0("p2", xA, xB)
    f0("p3", xB, xA)
    f1("p4", xA, xB, xA, xB)
    a1("p5", xB, xA)
    f1("p6", xA, xB, xA, xout_e)
    return ctx



_CTX_CACHE = {}


def _get(name, builder, *args):
    if name not in _CTX_CACHE:
        ctx = builder(*args)
        ctx.p.replay(ctx.nc)
        _CTX_CACHE[name] = ctx
    return _CTX_CACHE[name]


def _run(ctx, in_maps):
    return run_bass_kernel_spmd(ctx.nc, in_maps, core_ids=list(range(NCORES)))


def _run_built(name, builder, in_maps):
    ctx = _get(name, builder)
    return run_bass_kernel_spmd(ctx.nc, in_maps, core_ids=list(range(NCORES)))


def kernel(**I):
    return kernel_fused(**I)


def kernel_fused(**I):
    x = np.ascontiguousarray(np.asarray(I["L0_attn_hidden_states"],
                                        np.float32)[0])
    H = host_prep(I)
    shards = shard_x(x)
    ctx = _get("fused", build_fused)
    in_maps = []
    for i in range(NCORES):
        in_maps.append({
            "xsh": shards[i], "W2T": H["W2T"], "H128": H["H128"],
            "W1p": H["W1p"], "W3p": H["W3p"], "H64B": H["H64B"],
            "Wq": H["Wq"], "Wk": H["Wk"], "Wv": H["Wv"], "Wo0": H["Wo0"],
            "cos_r0": H["cos_r0"][i], "sin_r0": H["sin_r0"][i],
            "cos_r1": H["cos_r1"][i], "sin_r1": H["sin_r1"][i],
            "CUM": H["CUM"], "SUF": H["SUF"], "IDT": H["IDT"],
            "PSW": H["PSW"], "IDTF": H["IDTF"], "DMASK": H["DMASK"],
            "zsel": H["zsel"][i], "selpf": H["selpf"][i],
            "anw": H["anw"], "qnw": H["qnw"], "knw": H["knw"],
            "qwT": H["qwT"], "kwT": H["kwT"], "vwT": H["vwT"],
            "owT": H["owT"], "fnw": H["fnw"], "gTf": H["gTf"],
            "wgT": H["wgT"], "wuT": H["wuT"], "wdT": H["wdT"],
            "SEL": H["SEL"], "Ws1": H["Ws1"], "Wa1": H["Wa1"],
            "WoT1": H["WoT1"], "bo1": H["bo1"],
        })
    r = run_bass_kernel_spmd(ctx.nc, in_maps, core_ids=list(range(NCORES)))
    xo = unshard_x([np.asarray(r.results[i]["xout"]) for i in range(NCORES)])
    return np.ascontiguousarray(xo).reshape(1, SEQ, DM)


def kernel_multilaunch(**I):
    x = np.ascontiguousarray(np.asarray(I["L0_attn_hidden_states"],
                                        np.float32)[0])
    H = host_prep(I)
    asbf = np.asarray

    r = _run_built("gprep", build_gprep,
                   [{"W2T": H["W2T"], "H128": H["H128"]}] * NCORES)
    G2 = asbf(r.results[0]["G2"])

    def run_f0(shards):
        in_maps = [{"xsh": shards[i], "W1p": H["W1p"], "W3p": H["W3p"],
                    "G2": G2, "H64B": H["H64B"]} for i in range(NCORES)]
        r = _run_built("f0", build_f0, in_maps)
        return [asbf(r.results[i]["xout"]) for i in range(NCORES)]

    def assemble_kv(ra):
        skall = np.stack([asbf(ra.results[i]["sk_out"]) for i in range(NCORES)])
        vall = np.stack([asbf(ra.results[i]["vtok_out"]) for i in range(NCORES)])
        km = np.stack([asbf(ra.results[i]["kmax_out"])[0] for i in range(NCORES)])
        km_in = np.zeros((1, 32), np.float32)
        for g in range(4):
            km_in[0, g * 8:(g + 1) * 8] = km[:, g]
        return skall, vall, km_in

    def run_a0(shards):
        in_maps = [{"xsh": shards[i], "Wq": H["Wq"], "Wk": H["Wk"],
                    "Wv": H["Wv"], "cosd": H["cos_r0"][i], "sind": H["sin_r0"][i],
                    "CUM": H["CUM"], "SUF": H["SUF"], "IDT": H["IDT"],
                    "PSW": H["PSW"]} for i in range(NCORES)]
        ra = _run_built("a0a", build_a0a, in_maps)
        skall, vall, km_in = assemble_kv(ra)
        Dg = np.stack([asbf(ra.results[i]["D_out"]) for i in range(NCORES)])
        Ug = np.stack([asbf(ra.results[i]["U_out"]) for i in range(NCORES)])
        in_maps_b = []
        for i in range(NCORES):
            rr = ra.results[i]
            in_maps_b.append({
                "xsh": shards[i], "sq_in": asbf(rr["sq_out"]),
                "skall": skall, "vall": vall,
                "skown": asbf(rr["sk_out"]), "vown": asbf(rr["vtok_out"]),
                "ogla": asbf(rr["ogla_out"]), "acache": asbf(rr["acache_out"]),
                "boffe": asbf(rr["boff_out"]),
                "Dg": Dg, "Ug": Ug, "selpf": H["selpf"][i],
                "kmx": km_in, "zsel": H["zsel"][i], "DMASK": H["DMASK"],
                "Wo": H["Wo0"]})
        rb = _run_built("a0b", build_a0b, in_maps_b)
        return [asbf(rb.results[i]["xout"]) for i in range(NCORES)]

    def run_f1(shards):
        in_maps = [{"xsh": shards[i], "anw": H["anw"], "qnw": H["qnw"],
                    "knw": H["knw"], "qwT": H["qwT"], "kwT": H["kwT"],
                    "vwT": H["vwT"], "cosd": H["cos_r1"][i],
                    "sind": H["sin_r1"][i], "PSW": H["PSW"], "IDT": H["IDT"]}
                   for i in range(NCORES)]
        ra = _run_built("f1a", build_f1a, in_maps)
        skall, vall, km_in = assemble_kv(ra)
        in_maps_b = []
        for i in range(NCORES):
            rr = ra.results[i]
            in_maps_b.append({
                "xsh": asbf(rr["xout"]), "sq_in": asbf(rr["sq_out"]),
                "skall": skall, "vall": vall,
                "skown": asbf(rr["sk_out"]), "vown": asbf(rr["vtok_out"]),
                "hidden_in": asbf(rr["hidden_out"]),
                "kmx": km_in, "zsel": H["zsel"][i], "DMASK": H["DMASK"],
                "Wo": H["owT"]})
        rb = _run_built("f1flash", lambda: build_flash("f1"), in_maps_b)
        in_maps_c = [{"xsh": asbf(rb.results[i]["xout"]),
                      "h2sh": asbf(rb.results[i]["h2_out"]),
                      "fnw": H["fnw"], "gTf": H["gTf"], "wgT": H["wgT"],
                      "wuT": H["wuT"], "wdT": H["wdT"], "IDTF": H["IDTF"],
                      "SEL": H["SEL"]} for i in range(NCORES)]
        rc = _run_built("moe", build_moe, in_maps_c)
        return [asbf(rc.results[i]["xout"]) for i in range(NCORES)]

    def run_a1(shards):
        in_maps = [{"xsh": shards[i], "Ws": H["Ws1"], "Wa": H["Wa1"],
                    "IDT": H["IDT"]} for i in range(NCORES)]
        ra = _run_built("a1a", build_a1a, in_maps)
        aall = np.stack([asbf(ra.results[i]["adm_out"]) for i in range(NCORES)])
        hall = np.stack([asbf(ra.results[i]["htok_out"]) for i in range(NCORES)])
        amx = np.zeros((1, 8), np.float32)
        for i in range(NCORES):
            amx[0, i] = asbf(ra.results[i]["amax_out"])[0, 0]
        in_maps_b = []
        for i in range(NCORES):
            rr = ra.results[i]
            in_maps_b.append({
                "xsh": shards[i], "sdm": asbf(rr["sdm_out"]),
                "aall": aall, "hall": hall,
                "aown": asbf(rr["adm_out"]), "hown": asbf(rr["htok_out"]),
                "amx": amx, "zsel": H["zsel"][i], "DMASK": H["DMASK"],
                "WoT": H["WoT1"], "bo": H["bo1"]})
        rb = _run_built("a1b", build_a1b, in_maps_b)
        return [asbf(rb.results[i]["xout"]) for i in range(NCORES)]

    sh = shard_x(x)
    sh = run_f0(sh)
    sh = run_a0(sh)
    sh = run_f0(sh)
    sh = run_f1(sh)
    sh = run_a1(sh)
    sh = run_f1(sh)
    x = unshard_x(sh)
    return np.ascontiguousarray(x).reshape(1, SEQ, DM)
